# version 04
# baseline (speedup 1.0000x reference)
import contextlib
import math
import numpy as np
import ml_dtypes
import concourse.bass as bass
import concourse.mybir as mybir
from concourse.bass_utils import run_bass_kernel_spmd

F32 = mybir.dt.float32
BF16 = mybir.dt.bfloat16
I32 = mybir.dt.int32
AF = mybir.ActivationFunctionType
ALU = mybir.AluOpType
AX = mybir.AxisListType.X

B, S, D = 8, 4096, 1024
ND = D // 128
EXP = 4096
NE = EXP // 128
TC = 512
NTC = S // TC
BF = ml_dtypes.bfloat16
H = 16
HD = 64
NT = S // 128

N_CORES = 8


class Sem:
    __slots__ = ("h", "n")

    def __init__(self, stack, nc, name):
        self.h = stack.enter_context(nc.semaphore(name))
        self.n = 0

    def bump(self, k=1):
        self.n += k
        return self.n

    @property
    def now(self):
        return (self, self.n)


class Ctx:
    def __init__(self):
        self.nc = bass.Bass("TRN2", target_bir_lowering=False, debug=False)
        self.stack = contextlib.ExitStack()
        self.q = {"sync": [], "vector": [], "scalar": [], "tensor": [], "gpsimd": []}
        self.sems = {}
        self.host = {}
        self._uid = 0

    def uid(self):
        self._uid += 1
        return self._uid

    def begin_block_scope(self):
        self._blk_ctr = 0

    _BLK_PREFIX = ("l0a", "l0f", "l1f", "l2a", "l2f", "l3f", "at_")

    def sem(self, name):
        if name.startswith(self._BLK_PREFIX):
            if name not in getattr(self, "_alias", {}):
                if not hasattr(self, "_alias"):
                    self._alias = {}
                self._alias[name] = f"blk{self._blk_ctr}"
                self._blk_ctr += 1
            name = self._alias[name]
        if name not in self.sems:
            self.sems[name] = Sem(self.stack, self.nc, name)
        return self.sems[name]

    def din(self, name, arr, dt):
        self.host[name] = np.ascontiguousarray(arr)
        return self.nc.dram_tensor(name, list(arr.shape), dt,
                                   kind="ExternalInput").ap()

    def dout(self, name, shape, dt):
        return self.nc.dram_tensor(name, list(shape), dt, kind="ExternalOutput").ap()

    def dscratch(self, name, shape, dt):
        return self.nc.dram_tensor(name, list(shape), dt, kind="Internal").ap()

    def sbuf(self, name, shape, dt, stack=None):
        cm = self.nc.sbuf_tensor(name, list(shape), dt)
        return (stack or self.stack).enter_context(cm)

    def psum(self, name, shape, dt=F32, stack=None):
        cm = self.nc.psum_tensor(name, list(shape), dt)
        return (stack or self.stack).enter_context(cm)

    def step(self, eng, fn, waits=(), inc=None, inck=1):
        wlist = [(s.h, v) for (s, v) in waits if v > 0]
        semh = None
        if inc is not None:
            inc.bump(inck)
            semh = inc.h

        def go(e, wlist=wlist, fn=fn, semh=semh, inck=inck):
            for h_, v in wlist:
                e.wait_ge(h_, v)
            ins = fn(e)
            if semh is not None:
                ins.then_inc(semh, inck)
        self.q[eng].append(go)

    def dma(self, queue, dst, src, waits=(), inc=None):
        w = list(waits)
        if inc is not None and inc.n > 0:
            w.append((inc, inc.n))
        self.step(queue, lambda e: e.dma_start(out=dst, in_=src),
                  waits=w, inc=inc, inck=16)

    def finalize(self, final_waits=()):
        for s, v in final_waits:
            self.q["sync"].append(lambda e, h=s.h, v=v: e.wait_ge(h, v))
        nc = self.nc
        with nc.Block() as block:
            for name, starter in (("sync", block.sync), ("vector", block.vector),
                                  ("scalar", block.scalar), ("tensor", block.tensor),
                                  ("gpsimd", block.gpsimd)):
                fns = self.q[name]
                if fns:
                    starter(lambda e, fns=fns: [f(e) for f in fns])
        self.stack.close()
        return nc


class PsumPool:

    def __init__(self, ctx, name, n, shape, stack=None):
        self.banks = [ctx.psum(f"{name}{i}", shape, stack=stack) for i in range(n)]
        self.free = [None] * n
        self.rr = 0
        self.n = n

    def acquire(self):
        idx = self.rr % self.n
        self.rr += 1
        return idx, self.banks[idx], self.free[idx]

    def set_free(self, idx, semval):
        self.free[idx] = semval


class WStream:

    def __init__(self, ctx, bufs, first_wait=()):
        self.ctx = ctx
        self.bufs = bufs
        self.nbuf = len(bufs)
        self.s_ld = ctx.sem("ws_ld_shared")
        self.free_at = [None] * self.nbuf
        self.first_wait = list(first_wait)
        self.i = 0

    def fetch(self, src_ap, rows, cast, extra_waits=()):
        ctx = self.ctx
        idx = self.i % self.nbuf
        kc = rows // 128
        cols = src_ap.shape[1]
        buf = self.bufs[idx]
        waits = list(extra_waits)
        if self.i < self.nbuf:
            waits += self.first_wait
        if self.free_at[idx] is not None:
            waits.append(self.free_at[idx])
        self.i += 1
        src3 = src_ap.rearrange("(kc p) gcol -> p kc gcol", p=128)
        q = "gpsimd" if cast else "sync"
        ctx.dma(q, buf[:, 0:kc * cols].rearrange("p (kc c) -> p kc c", kc=kc),
                src3, waits=waits, inc=self.s_ld)
        return buf, self.s_ld.now, idx

    def release(self, idx, sem_val):
        self.free_at[idx] = sem_val


class Globals:
    def __init__(self, ctx: Ctx):
        self.ctx = ctx
        self.xbf = ctx.sbuf("XBF", (128, ND * S), BF16)
        self.wcol = [ctx.sbuf(f"WCOL{i}", (128, 4096), BF16) for i in range(3)]
        self.ones_col_bf = ctx.sbuf("ones_col_bf", (128, 1), BF16)
        self.ones_row_f = ctx.sbuf("ones_row_f", (1, 128), F32)
        self.eps_col = ctx.sbuf("eps_col", (1, 1), F32)
        self.ident_bf = ctx.sbuf("ident_bf", (128, 128), BF16)
        self.ident_f32 = ctx.sbuf("ident_f32", (128, 128), F32)
        idb = ctx.din("g_ident_bf", np.eye(128, dtype=BF), BF16)
        idf = ctx.din("g_ident_f32", np.eye(128, dtype=np.float32), F32)
        si = ctx.sem("g_ident")
        ctx.dma("sync", self.ident_bf[:], idb[:], inc=si)
        ctx.dma("sync", self.ident_f32[:], idf[:], inc=si)
        self.ident_done = si.now
        s = ctx.sem("g_init")
        ctx.step("vector", lambda e: e.memset(self.ones_col_bf[:], 1.0))
        ctx.step("vector", lambda e: e.memset(self.ones_row_f[:], 1.0))
        ctx.step("vector", lambda e: e.memset(self.eps_col[:], 1e-5), inc=s)
        self.init_done = s.now


def emit_load_xbf(ctx, g, x_src_fp32):
    u = ctx.uid()
    s_ld = ctx.sem(f"xbf_ld_{u}")
    ctx.dma("gpsimd",
            g.xbf[:].rearrange("p (c t) -> p c t", c=ND),
            x_src_fp32.rearrange("(c p) t -> p c t", p=128),
            inc=s_ld)
    return s_ld


class LNEpilogue:
    def __init__(self, ctx: Ctx, g):
        self.ctx, self.g = ctx, g
        self.xr = ctx.sbuf("ep_xr", (128, ND * TC), F32)
        self.res = ctx.sbuf("ep_res", (128, 4 * TC), F32)
        self.sh = ctx.sbuf("ep_sh", (128, 2 * TC), BF16)
        self.sq = ctx.sbuf("ep_sq", (128, 2 * TC), BF16)
        self.rows = ctx.sbuf("ep_rows", (1, 3 * TC), F32)
        self.bct = ctx.sbuf("ep_bct", (128, 2 * TC), F32)
        self.tmp = ctx.sbuf("ep_tmp", (128, TC), F32)
        self.ps_stat = ctx.psum("ep_ps_stat", (1, 2 * TC))
        self.rows_d = ctx.dscratch("ep_rows_d", (1, 2 * TC), F32)
        self.s_res = ctx.sem("ep_res")
        self.s_ev = ctx.sem("ep_ev")
        self.s_sq = ctx.sem("ep_sq")
        self.s_stp = ctx.sem("ep_stp")
        self.s_row = ctx.sem("ep_row")
        self.s_bc = ctx.sem("ep_bc")
        self.s_ap = ctx.sem("ep_ap")
        self.s_tmp = ctx.sem("ep_tmp")
        self.s_out = ctx.sem("ep_out")
        self.s_xbf = ctx.sem("ep_xbf")
        self.gi = 0
        self.x_in = None
        self.x_out = None
        self._pending = None
        self.debug_dram = None
        self.debug_dram2 = None

    def set_io(self, x_in_dram, x_out_dram):
        self.x_in, self.x_out = x_in_dram, x_out_dram

    def load_res(self, tc, dc):
        ctx = self.ctx
        k = self.gi * ND + dc
        slot = k % 4
        dst = self.res[:, slot * TC:(slot + 1) * TC]
        waits = [(self.s_ev, k - 3)] if k >= 4 else []
        ctx.dma("scalar", dst,
                self.x_in[dc * 128:(dc + 1) * 128, tc * TC:(tc + 1) * TC],
                waits=waits, inc=self.s_res)
        return dst, self.s_res.now

    def consume(self, tc, dc, psum_ap, bias_col, waits, res_info, scale=None):
        ctx, g = self.ctx, self.g
        gi = self.gi
        res, res_rdy = res_info
        xr = self.xr[:, dc * TC:(dc + 1) * TC]
        w = list(waits) + [res_rdy]
        if dc == 0 and gi >= 1:
            w.append((self.s_out, 16 * ND * gi))
            w.append((self.s_xbf, ND * gi))
        if scale is not None:
            ctx.step("vector",
                     lambda e: e.scalar_tensor_tensor(
                         out=xr, in0=psum_ap, scalar=scale, in1=res,
                         op0=ALU.mult, op1=ALU.add), waits=w,
                     inc=(None if bias_col is not None else self.s_ev))
            if bias_col is not None:
                ctx.step("vector",
                         lambda e: e.tensor_scalar(
                             out=xr, in0=xr, scalar1=bias_col, scalar2=None,
                             op0=ALU.add), inc=self.s_ev)
        else:
            bc = bias_col if bias_col is not None else 0.0
            ctx.step("vector",
                     lambda e: e.scalar_tensor_tensor(
                         out=xr, in0=psum_ap, scalar=bc, in1=res,
                         op0=ALU.add, op1=ALU.add), waits=w, inc=self.s_ev)
        ev = self.s_ev.now
        if self.debug_dram is not None:
            ctx.dma("gpsimd",
                    self.debug_dram[dc * 128:(dc + 1) * 128,
                                    tc * TC:(tc + 1) * TC],
                    xr, waits=[ev], inc=self.s_res)
        k = gi * ND + dc
        slot = k % 2
        sh = self.sh[:, slot * TC:(slot + 1) * TC]
        sq = self.sq[:, slot * TC:(slot + 1) * TC]
        wsh = [ev] + ([(self.s_stp, k - 1)] if k >= 2 else [])
        ctx.step("gpsimd", lambda e: e.tensor_copy(out=sh, in_=xr),
                 waits=wsh, inc=self.s_sq)
        ctx.step("scalar", lambda e: e.activation(out=sq, in_=xr, func=AF.Square),
                 waits=wsh, inc=self.s_sq)
        sqv = self.s_sq.now
        w2 = [sqv]
        if dc == 0 and gi >= 1:
            w2.append((self.s_row, 3 * gi))
        self.flush_stats()
        self._pending = (sh, sq, dc, w2)
        return ev

    def flush_stats(self):
        if self._pending is None:
            return
        sh, sq, dc, w2 = self._pending
        self._pending = None
        ctx, g = self.ctx, self.g
        ctx.step("tensor",
                 lambda e, sh=sh, sq=sq, dc=dc: (
                     e.matmul(out=self.ps_stat[:, 0:TC], lhsT=g.ones_col_bf[:],
                              rhs=sh, start=(dc == 0), stop=(dc == ND - 1)),
                     e.matmul(out=self.ps_stat[:, TC:2 * TC],
                              lhsT=g.ones_col_bf[:], rhs=sq,
                              start=(dc == 0), stop=(dc == ND - 1)))[-1],
                 waits=w2, inc=self.s_stp)

    def finish(self, tc):
        self.flush_stats()
        ctx, g = self.ctx, self.g
        gi = self.gi
        st = (self.s_stp, ND * (gi + 1))
        r_rs = self.rows[:, 0:TC]
        r_m = self.rows[:, TC:2 * TC]
        r_v = self.rows[:, 2 * TC:3 * TC]
        ctx.step("vector", lambda e: e.tensor_scalar(
            out=r_m, in0=self.ps_stat[:, 0:TC], scalar1=1.0 / D, scalar2=None,
            op0=ALU.mult), waits=[st, (self.s_bc, max(0, 32 * gi - 16))])
        ctx.step("vector", lambda e: e.scalar_tensor_tensor(
            out=r_v, in0=r_m, scalar=-1.0, in1=r_m, op0=ALU.mult, op1=ALU.mult))
        ctx.step("vector", lambda e: e.scalar_tensor_tensor(
            out=r_v, in0=self.ps_stat[:, TC:2 * TC], scalar=1.0 / D, in1=r_v,
            op0=ALU.mult, op1=ALU.add), inc=self.s_row)
        rv = self.s_row.now
        ctx.step("scalar", lambda e: e.activation(
            out=r_rs, in_=r_v, func=AF.Ln, bias=g.eps_col[:]),
            waits=[rv, g.init_done])
        ctx.step("scalar", lambda e: e.activation(
            out=r_rs, in_=r_rs, func=AF.Exp, scale=-0.5), inc=self.s_row)
        rs = self.s_row.now
        ctx.step("vector", lambda e: e.tensor_scalar(
            out=r_m, in0=r_m, scalar1=-1.0, scalar2=None, op0=ALU.mult),
            waits=[rs], inc=self.s_row)
        rneg = self.s_row.now
        ctx.dma("sync", self.rows_d[:], self.rows[:, 0:2 * TC],
                waits=[rneg, (self.s_bc, 32 * gi)], inc=self.s_bc)
        ctx.dma("sync", self.bct[:], self.rows_d.partition_broadcast(128),
                waits=[(self.s_bc, 32 * gi + 16), (self.s_ap, ND * gi)],
                inc=self.s_bc)
        bcv = (self.s_bc, 32 * (gi + 1))
        if self.debug_dram2 is not None:
            ctx.dma("gpsimd", self.debug_dram2[:, gi * 2 * TC:(gi + 1) * 2 * TC],
                    self.bct[:], waits=[bcv], inc=self.s_res)
        rstd_b = self.bct[:, 0:TC]
        negm_b = self.bct[:, TC:2 * TC]
        for dc in range(ND):
            xr = self.xr[:, dc * TC:(dc + 1) * TC]
            ctx.step("gpsimd",
                     lambda e, xr=xr: e.tensor_add(self.tmp[:], xr, negm_b),
                     waits=[bcv, (self.s_ap, gi * ND + dc)], inc=self.s_tmp)
            ctx.step("vector",
                     lambda e, xr=xr: e.tensor_mul(xr, self.tmp[:], rstd_b),
                     waits=[self.s_tmp.now], inc=self.s_ap)
            apv = self.s_ap.now
            xbf_dst = g.xbf[:, dc * S + tc * TC: dc * S + (tc + 1) * TC]
            ctx.step("scalar",
                     lambda e, xr=xr, xbf_dst=xbf_dst: e.copy(out=xbf_dst, in_=xr),
                     waits=[apv], inc=self.s_xbf)
            ctx.dma("sync",
                    self.x_out[dc * 128:(dc + 1) * 128, tc * TC:(tc + 1) * TC],
                    xr, waits=[apv], inc=self.s_out)
        self.gi += 1

    def block_done(self):
        return [(self.s_out, 16 * ND * self.gi), (self.s_xbf, ND * self.gi)]


def emit_precast(ctx, src_fp32, dst_bf16, rows, cols, bufs, s_done,
                 first_wait=()):
    n = rows // 128
    s_in = ctx.sem("pc_in")
    s_out = ctx.sem("pc_out")
    b_out = s_out.n
    for i in range(n):
        waits = [(s_out, b_out + 16 * (i - 1))] if i >= 2 else list(first_wait)
        ctx.dma("gpsimd", bufs[i % 2][:, 0:cols],
                src_fp32[i * 128:(i + 1) * 128, :], waits=waits, inc=s_in)
        ctx.dma("scalar", dst_bf16[i * 128:(i + 1) * 128, :],
                bufs[i % 2][:, 0:cols], waits=[s_in.now], inc=s_out)
    ctx.step("sync", lambda e: e.nop(), waits=[(s_out, b_out + 16 * n)],
             inc=s_done)
    return s_done.now


def emit_l0_attn(ctx, g, ep, io, W):
    u = "l0a"
    xbf_ready = io["xbf_ready"]
    xin_d = ctx.dscratch(u + "_xin", (EXP, S), BF16)
    xsk_d = ctx.dscratch(u + "_xsk", (EXP, S), BF16)
    a_d = ctx.dscratch(u + "_a", (EXP, S), F32)
    b_d = ctx.dscratch(u + "_b", (EXP, S), BF16)
    s_d = ctx.dscratch(u + "_s", (EXP, S), BF16)
    wzbf_d = ctx.dscratch(u + "_wzbf", (EXP, EXP), BF16)
    whbf_d = ctx.dscratch(u + "_whbf", (EXP, EXP), BF16)
    wobf_d = ctx.dscratch(u + "_wobf", (EXP, D), BF16)

    bias = ctx.sbuf(u + "_bias", (128, 4 * NE + ND), F32)
    s_bias = ctx.sem(u + "_bias")
    for i, bname in enumerate(("b1", "b2", "bz", "bh")):
        ctx.dma("sync", bias[:, i * NE:(i + 1) * NE], W[bname][:],
                waits=(list(io["prev_done"]) if i == 0 else []), inc=s_bias)
    ctx.dma("sync", bias[:, 4 * NE:4 * NE + ND], W["bo"][:], inc=s_bias)
    bias_ld = s_bias.now

    stA = contextlib.ExitStack()
    pcb = [ctx.sbuf(u + f"_pcb{i}", (128, 4096), BF16, stack=stA) for i in range(2)]
    s_pc = ctx.sem(u + "_pc")
    d1_ = emit_precast(ctx, W["Wz"], wzbf_d, EXP, EXP, pcb, s_pc)
    d2_ = emit_precast(ctx, W["Wh"], whbf_d, EXP, EXP, pcb, s_pc,
                       first_wait=[d1_])
    emit_precast(ctx, W["Wo"], wobf_d, EXP, D, pcb, s_pc, first_wait=[d2_])
    pc_done = s_pc.now

    ws = WStream(ctx, g.wcol[0:2], first_wait=[xbf_ready])
    s_mmA = ctx.sem(u + "_mmA")
    s_evA = ctx.sem(u + "_evA")
    s_oA = ctx.sem(u + "_oA")
    ppA = PsumPool(ctx, u + "_psA", 4, (128, 512), stack=stA)
    stg = [ctx.sbuf(u + f"_stgA{i}", (128, S), BF16, stack=stA) for i in range(2)]
    for eo in range(NE):
        stg_free = (s_oA, s_oA.n)
        for wi, wname in enumerate(("W1", "W2")):
            buf, rdy, idx = ws.fetch(W[wname][:, eo * 128:(eo + 1) * 128], D, True)
            last_mm = None
            for nu in range(8):
                bi, bank, bfree = ppA.acquire()
                def mm(e, buf=buf, nu=nu, bank=bank):
                    for c in range(ND):
                        ins = e.matmul(
                            out=bank[:], lhsT=buf[:, c * 128:(c + 1) * 128],
                            rhs=g.xbf[:, c * S + nu * 512: c * S + (nu + 1) * 512],
                            start=(c == 0), stop=(c == ND - 1))
                    return ins
                waits = [rdy] + ([bfree] if bfree else [])
                ctx.step("tensor", mm, waits=waits, inc=s_mmA)
                mmv = s_mmA.now
                last_mm = mmv
                bcol = bias[:, wi * NE + eo: wi * NE + eo + 1]
                dst = stg[wi][:, nu * 512:(nu + 1) * 512]
                ctx.step("scalar",
                         lambda e, bank=bank, dst=dst, bcol=bcol:
                         e.activation(out=dst, in_=bank[:], func=AF.Silu,
                                      bias=bcol),
                         waits=[mmv, bias_ld, stg_free], inc=s_evA)
                ppA.set_free(bi, s_evA.now)
            ws.release(idx, last_mm)
        evA = s_evA.now
        ctx.dma("sync", xin_d[eo * 128:(eo + 1) * 128, :], stg[0][:],
                waits=[evA], inc=s_oA)
        ctx.dma("sync", xsk_d[eo * 128:(eo + 1) * 128, :], stg[1][:],
                waits=[evA], inc=s_oA)
    phaseA_done = (s_oA, s_oA.n)
    stA.close()

    stB = contextlib.ExitStack()
    wsB = WStream(ctx, g.wcol[0:3], first_wait=[phaseA_done, pc_done])
    s_xinB = ctx.sem(u + "_xinB")
    s_mmB = ctx.sem(u + "_mmB")
    s_evB = ctx.sem(u + "_evB")
    s_abB = ctx.sem(u + "_abB")
    s_oB = ctx.sem(u + "_oB")
    ppZ = PsumPool(ctx, u + "_psZ", 2, (128, 512), stack=stB)
    ppH = PsumPool(ctx, u + "_psH", 2, (128, 512), stack=stB)
    zt = ctx.sbuf(u + "_zt", (128, 2 * 512), F32, stack=stB)
    abt = ctx.sbuf(u + "_abt", (128, 4 * 512), F32, stack=stB)
    abtb = ctx.sbuf(u + "_abtb", (128, 2 * 512), BF16, stack=stB)
    TQ = 1024
    for tq in range(S // TQ):
        wx = [phaseA_done] if tq == 0 else [(s_mmB, s_mmB.n)]
        ctx.dma("sync",
                g.xbf[:].rearrange("p (kc t) -> p kc t", kc=NE)[:, :, 0:TQ],
                xin_d.rearrange("(kc p) t -> p kc t", p=128)
                [:, :, tq * TQ:(tq + 1) * TQ],
                waits=wx, inc=s_xinB)
        xin_rdy = s_xinB.now
        for fo in range(NE):
            fetched = []
            for wi, wd in enumerate((wzbf_d, whbf_d)):
                buf, rdy, idx = wsB.fetch(wd[:, fo * 128:(fo + 1) * 128], EXP,
                                          False)
                fetched.append((buf, rdy, idx))
            for half in range(2):
                banks = []
                for wi in range(2):
                    buf, rdy, idx = fetched[wi]
                    pp = ppZ if wi == 0 else ppH
                    bi, bank, bfree = pp.acquire()
                    def mmB(e, buf=buf, half=half, bank=bank):
                        for c in range(NE):
                            ins = e.matmul(
                                out=bank[:],
                                lhsT=buf[:, c * 128:(c + 1) * 128],
                                rhs=g.xbf[:, c * TQ + half * 512:
                                          c * TQ + (half + 1) * 512],
                                start=(c == 0), stop=(c == NE - 1))
                        return ins
                    waits = [rdy, xin_rdy] + ([bfree] if bfree else [])
                    ctx.step("tensor", mmB, waits=waits, inc=s_mmB)
                    banks.append((pp, bi, bank))
                mmv = s_mmB.now
                zcol = bias[:, 2 * NE + fo: 2 * NE + fo + 1]
                hcol = bias[:, 3 * NE + fo: 3 * NE + fo + 1]
                sl = half
                zdst = zt[:, sl * 512:(sl + 1) * 512]
                wab = (s_abB, max(0, s_abB.n - 2))
                ctx.step("scalar",
                         lambda e, b_=banks[0][2], zdst=zdst, zcol=zcol:
                         e.activation(out=zdst, in_=b_[:], func=AF.Sigmoid,
                                      bias=zcol),
                         waits=[mmv, wab], inc=s_evB)
                banks[0][0].set_free(banks[0][1], s_evB.now)
                evv = s_evB.now
                adst = abt[:, (2 * sl) * 512:(2 * sl + 1) * 512]
                bdst = abtb[:, sl * 512:(sl + 1) * 512]
                wo_ = (s_oB, max(0, s_oB.n - 2 * 16))
                ctx.step("vector",
                         lambda e, zdst=zdst, adst=adst:
                         e.tensor_scalar(out=adst, in0=zdst, scalar1=-1.0,
                                         scalar2=1.0, op0=ALU.mult, op1=ALU.add),
                         waits=[evv, wo_], inc=s_abB)
                ctx.step("vector",
                         lambda e, b_=banks[1][2], zdst=zdst, bdst=bdst,
                         hcol=hcol:
                         e.scalar_tensor_tensor(out=bdst, in0=b_[:], scalar=hcol,
                                                in1=zdst, op0=ALU.add,
                                                op1=ALU.mult), inc=s_abB)
                banks[1][0].set_free(banks[1][1], s_abB.now)
                abv = s_abB.now
                t0 = tq * TQ + half * 512
                ctx.dma("scalar", a_d[fo * 128:(fo + 1) * 128, t0:t0 + 512],
                        adst, waits=[abv], inc=s_oB)
                ctx.dma("scalar", b_d[fo * 128:(fo + 1) * 128, t0:t0 + 512],
                        bdst, waits=[abv], inc=s_oB)
            for buf, rdy, idx in fetched:
                wsB.release(idx, s_mmB.now)
    phaseB_done = (s_oB, s_oB.n)
    stB.close()

    stC = contextlib.ExitStack()
    s_ldC = ctx.sem(u + "_ldC")
    s_scC = ctx.sem(u + "_scC")
    s_oC = ctx.sem(u + "_oC")
    at_ = ctx.sbuf(u + "_at", (128, S), F32, stack=stC)
    bt_ = ctx.sbuf(u + "_bt", (128, S), BF16, stack=stC)
    ht_ = ctx.sbuf(u + "_hC", (128, S), F32, stack=stC)
    skt = ctx.sbuf(u + "_skt", (128, S), BF16, stack=stC)
    for fo in range(NE):
        wl = [phaseB_done] if fo == 0 else [(s_scC, s_scC.n)]
        ctx.dma("sync", at_[:], a_d[fo * 128:(fo + 1) * 128, :], waits=wl,
                inc=s_ldC)
        ctx.dma("sync", bt_[:], b_d[fo * 128:(fo + 1) * 128, :], waits=wl,
                inc=s_ldC)
        wl2 = wl + [(s_oC, s_oC.n)]
        ctx.dma("sync", skt[:], xsk_d[fo * 128:(fo + 1) * 128, :], waits=wl2,
                inc=s_ldC)
        ldv = s_ldC.now
        wsc = [ldv] + ([(s_oC, s_oC.n - 16)] if fo >= 1 else [])
        ctx.step("vector",
                 lambda e: e.tensor_tensor_scan(
                     out=ht_[:], data0=at_[:], data1=bt_[:], initial=0.0,
                     op0=ALU.mult, op1=ALU.add),
                 waits=wsc, inc=s_scC)
        ctx.step("vector",
                 lambda e: e.tensor_tensor(out=skt[:], in0=ht_[:], in1=skt[:],
                                           op=ALU.add), inc=s_scC)
        ctx.step("vector", lambda e: e.engine_nop(), inc=s_scC)
        scv = s_scC.now
        ctx.dma("sync", s_d[fo * 128:(fo + 1) * 128, :], skt[:],
                waits=[scv], inc=s_oC)
    phaseC_done = (s_oC, s_oC.n)
    stC.close()

    stD = contextlib.ExitStack()
    wsD = WStream(ctx, g.wcol[0:3], first_wait=[phaseB_done, pc_done])
    s_sD = ctx.sem(u + "_sD")
    s_mmD = ctx.sem(u + "_mmD")
    sst = ctx.sbuf(u + "_sst", (128, 2 * NE * TC), BF16, stack=stD)
    ppD = PsumPool(ctx, u + "_psD", 2, (128, TC), stack=stD)
    ep.set_io(io["x_in"], io["x_out"])
    for tc in range(NTC):
        rot = tc % 2
        sbuf_tc = sst[:, rot * NE * TC:(rot + 1) * NE * TC]
        wl = [phaseC_done] if tc == 0 else [(s_mmD, max(0, s_mmD.n - ND))]
        ctx.dma("sync", sbuf_tc.rearrange("p (kc t) -> p kc t", kc=NE),
                s_d.rearrange("(kc p) t -> p kc t", p=128)
                [:, :, tc * TC:(tc + 1) * TC], waits=wl, inc=s_sD)
        srdy = s_sD.now
        for dc in range(ND):
            res_info = ep.load_res(tc, dc)
            buf, rdy, idx = wsD.fetch(wobf_d[:, dc * 128:(dc + 1) * 128], EXP,
                                      False)
            bi, bank, bfree = ppD.acquire()
            def mmD(e, buf=buf, bank=bank, sbuf_tc=sbuf_tc):
                for c in range(NE):
                    ins = e.matmul(
                        out=bank[:], lhsT=buf[:, c * 128:(c + 1) * 128],
                        rhs=sbuf_tc[:, c * TC:(c + 1) * TC],
                        start=(c == 0), stop=(c == NE - 1))
                return ins
            waits = [rdy, srdy] + ([bfree] if bfree else [])
            ctx.step("tensor", mmD, waits=waits, inc=s_mmD)
            wsD.release(idx, s_mmD.now)
            ev = ep.consume(tc, dc, bank[:],
                            bias[:, 4 * NE + dc:4 * NE + dc + 1],
                            [s_mmD.now], res_info)
            ppD.set_free(bi, ev)
        ep.finish(tc)
    stD.close()
    return ep.block_done()


def emit_dequant(ctx, q_dram, scales_dram, out_dram, rows, cols, addc, mulc,
                 prev_done, stack, tile_cols=1024):
    u = ctx.uid()
    n = rows // 128
    m = cols // tile_cols
    ib = [ctx.sbuf(f"dq_i{u}_{i}", (128, tile_cols), I32, stack=stack)
          for i in range(2)]
    sb = (ctx.sbuf(f"dq_s{u}", (128, tile_cols), F32, stack=stack)
          if scales_dram is not None else None)
    wb = [ctx.sbuf(f"dq_w{u}_{i}", (128, tile_cols), BF16, stack=stack)
          for i in range(2)]
    tmp_ = ctx.sbuf(f"dq_t{u}", (128, tile_cols), F32, stack=stack)
    s_in = ctx.sem("dq_in")
    s_cp = ctx.sem("dq_cp")
    s_out = ctx.sem("dq_out")
    b_in, b_cp, b_out = s_in.n, s_cp.n, s_out.n
    k = 0
    for i in range(n):
        for j in range(m):
            w = list(prev_done) if k == 0 else []
            if k >= 2:
                w.append((s_cp, b_cp + k - 1))
            ctx.dma("sync", ib[k % 2][:],
                    q_dram[i * 128:(i + 1) * 128,
                           j * tile_cols:(j + 1) * tile_cols],
                    waits=w, inc=s_in)
            if scales_dram is not None:
                ctx.dma("sync", sb[:],
                        scales_dram[i * 128:(i + 1) * 128,
                                    j * tile_cols:(j + 1) * tile_cols],
                        waits=[(s_cp, b_cp + k)] if k >= 1 else list(prev_done),
                        inc=s_in)
                inr = s_in.now
                if mulc != 1.0:
                    ctx.step("vector", lambda e: e.tensor_scalar(
                        out=sb[:], in0=sb[:], scalar1=mulc, scalar2=None,
                        op0=ALU.mult), waits=[inr])
                ctx.step("vector",
                         lambda e, k=k: e.scalar_tensor_tensor(
                             out=tmp_[:], in0=ib[k % 2][:],
                             scalar=float(addc), in1=sb[:], op0=ALU.add,
                             op1=ALU.mult),
                         waits=[inr])
                ctx.step("vector",
                         lambda e, k=k: e.tensor_copy(out=wb[k % 2][:],
                                                      in_=tmp_[:]),
                         waits=[(s_out, b_out + 16 * (k - 1))], inc=s_cp)
            else:
                ctx.step("vector",
                         lambda e, k=k: e.tensor_scalar(
                             out=wb[k % 2][:], in0=ib[k % 2][:],
                             scalar1=float(mulc), scalar2=None, op0=ALU.mult),
                         waits=[s_in.now, (s_out, b_out + 16 * (k - 1))],
                         inc=s_cp)
            ctx.dma("scalar",
                    out_dram[i * 128:(i + 1) * 128,
                             j * tile_cols:(j + 1) * tile_cols],
                    wb[k % 2][:], waits=[s_cp.now], inc=s_out)
            k += 1
    return (s_out, b_out + 16 * n * m)


def emit_l1_ffn(ctx, g, ep, io, W):
    u = "l1f"
    prev = io["prev_done"]
    w1_d = ctx.dscratch(u + "_w1", (D, EXP), BF16)
    w2_d = ctx.dscratch(u + "_w2", (EXP, D), BF16)

    vec = ctx.sbuf(u + "_vec", (128, 2 * ND + NE + ND), F32)
    halfb2 = ctx.sbuf(u + "_hb2", (128, ND), F32)
    s_vec = ctx.sem(u + "_vec")
    ctx.dma("sync", vec[:, 0:ND], W["ln_gamma"][:], waits=list(prev),
            inc=s_vec)
    ctx.dma("sync", vec[:, ND:2 * ND], W["ln_beta"][:], inc=s_vec)
    ctx.dma("sync", vec[:, 2 * ND:2 * ND + NE], W["fc1_bias"][:], inc=s_vec)
    ctx.dma("sync", vec[:, 2 * ND + NE:2 * ND + NE + ND], W["fc2_bias"][:],
            inc=s_vec)
    vr = s_vec.now
    ctx.step("vector", lambda e: e.tensor_scalar(
        out=halfb2[:], in0=vec[:, 2 * ND + NE:2 * ND + NE + ND], scalar1=0.5,
        scalar2=None, op0=ALU.mult), waits=[vr], inc=s_vec)
    vr2 = s_vec.now

    stL = contextlib.ExitStack()
    dq1 = emit_dequant(ctx, W["fc1_q"], W["fc1_scales"], w1_d, D, EXP,
                       -128.0, 1.0, prev, stL)
    dq2 = emit_dequant(ctx, W["fc2_q"], W["fc2_scales"], w2_d, EXP, D,
                       -128.0, 0.5, prev, stL)

    sqb = ctx.sbuf(u + "_sqb", (128, 2 * TC), BF16, stack=stL)
    rowsL = ctx.sbuf(u + "_rows", (1, 3 * TC), F32, stack=stL)
    bctL = ctx.sbuf(u + "_bct", (128, 2 * TC), F32, stack=stL)
    xfL = ctx.sbuf(u + "_xf", (128, 2 * TC), F32, stack=stL)
    tmpL = ctx.sbuf(u + "_tmp", (128, TC), F32, stack=stL)
    psL = ctx.psum(u + "_psL", (1, 2 * TC), stack=stL)
    psLb = ctx.psum(u + "_psLb", (128, 2 * TC), stack=stL)
    s_lsq = ctx.sem(u + "_lsq")
    s_lst = ctx.sem(u + "_lst")
    s_lrow = ctx.sem(u + "_lrow")
    s_lbc = ctx.sem(u + "_lbc")
    s_lxf = ctx.sem(u + "_lxf")
    s_lap = ctx.sem(u + "_lap")
    xbf_ready = io["xbf_ready"]
    for tc in range(NTC):
        for dc in range(ND):
            k = tc * ND + dc
            xb = g.xbf[:, dc * S + tc * TC: dc * S + (tc + 1) * TC]
            wq = [xbf_ready] + ([(s_lst, s_lst.n - 1)] if k >= 2 else [])
            sq = sqb[:, (k % 2) * TC:(k % 2 + 1) * TC]
            ctx.step("scalar", lambda e, xb=xb, sq=sq: e.activation(
                out=sq, in_=xb, func=AF.Square), waits=wq, inc=s_lsq)
            w2_ = [s_lsq.now]
            if dc == 0 and tc >= 1:
                w2_.append((s_lrow, s_lrow.n))
            ctx.step("tensor",
                     lambda e, xb=xb, sq=sq, dc=dc: (
                         e.matmul(out=psL[:, 0:TC], lhsT=g.ones_col_bf[:],
                                  rhs=xb, start=(dc == 0), stop=(dc == ND - 1)),
                         e.matmul(out=psL[:, TC:2 * TC], lhsT=g.ones_col_bf[:],
                                  rhs=sq, start=(dc == 0), stop=(dc == ND - 1))
                     )[-1], waits=w2_, inc=s_lst)
        r_m = rowsL[:, 0:TC]
        r_v = rowsL[:, TC:2 * TC]
        r_rs = rowsL[:, 2 * TC:3 * TC]
        stv = (s_lst, s_lst.n)
        ctx.step("vector", lambda e: e.tensor_scalar(
            out=r_m, in0=psL[:, 0:TC], scalar1=1.0 / D, scalar2=None,
            op0=ALU.mult), waits=[stv])
        ctx.step("vector", lambda e: e.scalar_tensor_tensor(
            out=r_v, in0=r_m, scalar=-1.0, in1=r_m, op0=ALU.mult, op1=ALU.mult))
        ctx.step("vector", lambda e: e.scalar_tensor_tensor(
            out=r_v, in0=psL[:, TC:2 * TC], scalar=1.0 / D, in1=r_v,
            op0=ALU.mult, op1=ALU.add), inc=s_lrow)
        rv = s_lrow.now
        ctx.step("scalar", lambda e: e.activation(
            out=r_rs, in_=r_v, func=AF.Ln, bias=g.eps_col[:]),
            waits=[rv, g.init_done])
        ctx.step("scalar", lambda e: e.activation(
            out=r_rs, in_=r_rs, func=AF.Exp, scale=-0.5), inc=s_lrow)
        rs = s_lrow.now
        ctx.step("vector", lambda e: e.tensor_scalar(
            out=r_m, in0=r_m, scalar1=-1.0, scalar2=None, op0=ALU.mult),
            waits=[rs], inc=s_lrow)
        rneg = s_lrow.now
        ctx.step("tensor", lambda e: (
            e.matmul(out=psLb[:, 0:TC], lhsT=g.ones_row_f[:], rhs=r_rs,
                     start=True, stop=True),
            e.matmul(out=psLb[:, TC:2 * TC], lhsT=g.ones_row_f[:], rhs=r_m,
                     start=True, stop=True))[-1],
            waits=[rneg, (s_lbc, s_lbc.n)], inc=s_lbc)
        ctx.step("scalar", lambda e: e.copy(out=bctL[:], in_=psLb[:]),
                 waits=[s_lbc.now, (s_lap, s_lap.n)], inc=s_lbc)
        bcv = s_lbc.now
        for dc in range(ND):
            k_ = tc * ND + dc
            xf = xfL[:, (k_ % 2) * TC:(k_ % 2 + 1) * TC]
            wld = [(s_lap, s_lap.n - 3)] if k_ >= 2 else []
            ctx.dma("scalar", xf,
                    io["x_in"][dc * 128:(dc + 1) * 128, tc * TC:(tc + 1) * TC],
                    waits=wld, inc=s_lxf)
            ctx.step("gpsimd",
                     lambda e, xf=xf: e.tensor_add(tmpL[:], xf,
                                                   bctL[:, TC:2 * TC]),
                     waits=[s_lxf.now, bcv, (s_lap, s_lap.n)], inc=s_lap)
            ctx.step("vector",
                     lambda e, xf=xf: e.tensor_mul(tmpL[:], tmpL[:],
                                                   bctL[:, 0:TC]),
                     waits=[s_lap.now])
            xbf_dst = g.xbf[:, dc * S + tc * TC: dc * S + (tc + 1) * TC]
            ctx.step("vector",
                     lambda e, xbf_dst=xbf_dst, dc=dc: e.tensor_scalar(
                         out=xbf_dst, in0=tmpL[:], scalar1=vec[:, dc:dc + 1],
                         scalar2=vec[:, ND + dc:ND + dc + 1], op0=ALU.mult,
                         op1=ALU.add), waits=[vr2], inc=s_lap)
    ln_done = (s_lap, s_lap.n)
    stL.close()

    stM = contextlib.ExitStack()
    ybuf = ctx.sbuf(u + "_ybuf", (128, NE * 1024), BF16, stack=stM)
    ppF = PsumPool(ctx, u + "_psF", 4, (128, 512), stack=stM)
    ws1 = WStream(ctx, g.wcol[0:2], first_wait=[ln_done, dq1])
    ws2 = WStream(ctx, g.wcol[2:3], first_wait=[dq2])
    s_mm1 = ctx.sem(u + "_mm1")
    s_ev1 = ctx.sem(u + "_ev1")
    s_mm2 = ctx.sem(u + "_mm2")
    ep.set_io(io["x_in"], io["x_out"])
    TQ = 1024
    for tq in range(S // TQ):
        mm2_at_start = (s_mm2, s_mm2.n)
        for fo in range(NE):
            buf, rdy, idx = ws1.fetch(w1_d[:, fo * 128:(fo + 1) * 128], D, False)
            last = None
            for nu in range(2):
                bi, bank, bfree = ppF.acquire()
                def mm1(e, buf=buf, tq=tq, nu=nu, bank=bank):
                    for c in range(ND):
                        ins = e.matmul(
                            out=bank[:], lhsT=buf[:, c * 128:(c + 1) * 128],
                            rhs=g.xbf[:, c * S + tq * TQ + nu * 512:
                                      c * S + tq * TQ + (nu + 1) * 512],
                            start=(c == 0), stop=(c == ND - 1))
                    return ins
                ctx.step("tensor", mm1,
                         waits=[rdy] + ([bfree] if bfree else []), inc=s_mm1)
                last = s_mm1.now
                ydst = ybuf[:, fo * 1024 + nu * 512: fo * 1024 + (nu + 1) * 512]
                bcol = vec[:, 2 * ND + fo:2 * ND + fo + 1]
                ctx.step("scalar",
                         lambda e, bank=bank, ydst=ydst, bcol=bcol:
                         e.activation(out=ydst, in_=bank[:], func=AF.Silu,
                                      bias=bcol),
                         waits=[last, vr, mm2_at_start], inc=s_ev1)
                ppF.set_free(bi, s_ev1.now)
            ws1.release(idx, last)
        ev1_done = (s_ev1, s_ev1.n)
        for half in range(2):
            tc = tq * 2 + half
            for dc in range(ND):
                res_info = ep.load_res(tc, dc)
                buf, rdy, idx = ws2.fetch(w2_d[:, dc * 128:(dc + 1) * 128],
                                          EXP, False)
                bi, bank, bfree = ppF.acquire()
                def mm2(e, buf=buf, half=half, bank=bank):
                    for c in range(NE):
                        ins = e.matmul(
                            out=bank[:], lhsT=buf[:, c * 128:(c + 1) * 128],
                            rhs=ybuf[:, c * 1024 + half * 512:
                                     c * 1024 + (half + 1) * 512],
                            start=(c == 0), stop=(c == NE - 1))
                    return ins
                ctx.step("tensor", mm2,
                         waits=[rdy, ev1_done] + ([bfree] if bfree else []),
                         inc=s_mm2)
                ws2.release(idx, s_mm2.now)
                ev = ep.consume(tc, dc, bank[:], halfb2[:, dc:dc + 1],
                                [s_mm2.now, vr2], res_info)
                ppF.set_free(bi, ev)
            ep.finish(tc)
    stM.close()
    return ep.block_done()


def emit_l3_ffn(ctx, g, ep, io, W):
    u = "l3f"
    HID = 3072
    NH = HID // 128
    prev = io["prev_done"]
    wg_d = ctx.dscratch(u + "_wg", (D, HID), BF16)
    wu_d = ctx.dscratch(u + "_wu", (D, HID), BF16)
    wd_d = ctx.dscratch(u + "_wd", (HID, D), BF16)
    st0 = contextlib.ExitStack()
    dqg = emit_dequant(ctx, W["wgT"], None, wg_d, D, HID, 0.0, 1.0 / 127.0,
                       prev, st0)
    dqu = emit_dequant(ctx, W["wuT"], None, wu_d, D, HID, 0.0, 1.0 / 127.0,
                       prev, st0)
    dqd = emit_dequant(ctx, W["wdT"], None, wd_d, HID, D, 0.0, 1.0 / 127.0,
                       prev, st0)
    st0.close()

    stM = contextlib.ExitStack()
    ybuf = ctx.sbuf(u + "_ybuf", (128, NH * 1024), BF16, stack=stM)
    tg = ctx.sbuf(u + "_tg", (128, 2 * 512), F32, stack=stM)
    ppF = PsumPool(ctx, u + "_psF", 4, (128, 512), stack=stM)
    wsg = WStream(ctx, g.wcol[0:1], first_wait=[dqg, io["xbf_ready"]])
    wsu = WStream(ctx, g.wcol[1:2], first_wait=[dqu, io["xbf_ready"]])
    wsd = WStream(ctx, g.wcol[2:3], first_wait=[dqd])
    s_mm1 = ctx.sem(u + "_mm1")
    s_ev1 = ctx.sem(u + "_ev1")
    s_sil = ctx.sem(u + "_sil")
    s_mm2 = ctx.sem(u + "_mm2")
    ep.set_io(io["x_in"], io["x_out"])
    TQ = 1024
    for tq in range(S // TQ):
        mm2_at_start = (s_mm2, s_mm2.n)
        for fo in range(NH):
            bufg, rdyg, idxg = wsg.fetch(wg_d[:, fo * 128:(fo + 1) * 128], D,
                                         False)
            bufu, rdyu, idxu = wsu.fetch(wu_d[:, fo * 128:(fo + 1) * 128], D,
                                         False)
            lastg = lastu = None
            for nu in range(2):
                big, bankg, bfreeg = ppF.acquire()
                biu, banku, bfreeu = ppF.acquire()
                for which, (buf, bank, bfree) in enumerate(
                        ((bufg, bankg, bfreeg), (bufu, banku, bfreeu))):
                    def mm1(e, buf=buf, tq=tq, nu=nu, bank=bank):
                        for c in range(ND):
                            ins = e.matmul(
                                out=bank[:], lhsT=buf[:, c * 128:(c + 1) * 128],
                                rhs=g.xbf[:, c * S + tq * TQ + nu * 512:
                                          c * S + tq * TQ + (nu + 1) * 512],
                                start=(c == 0), stop=(c == ND - 1))
                        return ins
                    ctx.step("tensor", mm1,
                             waits=[rdyg if which == 0 else rdyu] +
                                   ([bfree] if bfree else []), inc=s_mm1)
                lastg = lastu = s_mm1.now
                sl = nu
                tgd = tg[:, sl * 512:(sl + 1) * 512]
                wsl = (s_ev1, max(0, s_ev1.n - 1))
                ctx.step("scalar",
                         lambda e, bankg=bankg, tgd=tgd:
                         e.activation(out=tgd, in_=bankg[:], func=AF.Silu),
                         waits=[lastg, wsl, mm2_at_start], inc=s_sil)
                ppF.set_free(big, s_sil.now)
                ydst = ybuf[:, fo * 1024 + nu * 512: fo * 1024 + (nu + 1) * 512]
                ctx.step("vector",
                         lambda e, banku=banku, tgd=tgd, ydst=ydst:
                         e.tensor_tensor(out=ydst, in0=tgd, in1=banku[:],
                                         op=ALU.mult),
                         waits=[s_sil.now, mm2_at_start], inc=s_ev1)
                ppF.set_free(biu, s_ev1.now)
            wsg.release(idxg, lastg)
            wsu.release(idxu, lastu)
        ev1_done = (s_ev1, s_ev1.n)
        for half in range(2):
            tc = tq * 2 + half
            for dc in range(ND):
                res_info = ep.load_res(tc, dc)
                buf, rdy, idx = wsd.fetch(wd_d[:, dc * 128:(dc + 1) * 128],
                                          HID, False)
                bi, bank, bfree = ppF.acquire()
                def mm2(e, buf=buf, half=half, bank=bank):
                    for c in range(NH):
                        ins = e.matmul(
                            out=bank[:], lhsT=buf[:, c * 128:(c + 1) * 128],
                            rhs=ybuf[:, c * 1024 + half * 512:
                                     c * 1024 + (half + 1) * 512],
                            start=(c == 0), stop=(c == NH - 1))
                    return ins
                ctx.step("tensor", mm2,
                         waits=[rdy, ev1_done] + ([bfree] if bfree else []),
                         inc=s_mm2)
                wsd.release(idx, s_mm2.now)
                ev = ep.consume(tc, dc, bank[:], None, [s_mm2.now], res_info)
                ppF.set_free(bi, ev)
            ep.finish(tc)
    stM.close()
    return ep.block_done()


def emit_l2_attn(ctx, g, ep, io, W):
    io = dict(io)
    io.setdefault("dbg_sc", None); io.setdefault("dbg_yt", None); io.setdefault("dbg_ut", None)
    """W: W_inT (1024, 64) f32; W_outT (64, 1024) f32; log_A, B_param, C_param
    (64, 1) f32; log_dt (1, 1) f32."""
    u = "l2a"
    N = 64
    prev = io["prev_done"]
    stM = contextlib.ExitStack()
    win = ctx.sbuf(u + "_win", (128, ND * N), BF16, stack=stM)
    wout = ctx.sbuf(u + "_wout", (N, D), BF16, stack=stM)
    ut = ctx.sbuf(u + "_ut", (N, S), F32, stack=stM)
    sc = ctx.sbuf(u + "_sc", (N, 16), F32, stack=stM)
    scm = ctx.sbuf(u + "_scm", (N, 12), I32, stack=stM)
    yt = ctx.sbuf(u + "_yt", (N, S), BF16, stack=stM)
    ppU = PsumPool(ctx, u + "_psU", 2, (N, 512), stack=stM)
    ppD = PsumPool(ctx, u + "_psD", 2, (128, TC), stack=stM)
    s_ld = ctx.sem(u + "_ld")
    s_mmU = ctx.sem(u + "_mmU")
    s_evU = ctx.sem(u + "_evU")
    s_zoh = ctx.sem(u + "_zoh")
    s_scan = ctx.sem(u + "_scan")
    s_mmD = ctx.sem(u + "_mmD")
    ctx.dma("gpsimd", win[:].rearrange("p (c n) -> p c n", c=ND),
            W["W_inT"].rearrange("(c p) n -> p c n", p=128), waits=prev,
            inc=s_ld)
    ctx.dma("gpsimd", wout[:], W["W_outT"][:], waits=prev, inc=s_ld)
    ctx.dma("sync", sc[:, 0:1], W["log_A"][:], waits=prev, inc=s_ld)
    ctx.dma("sync", sc[:, 1:2], W["B_param"][:], waits=prev, inc=s_ld)
    ctx.dma("sync", sc[:, 2:3], W["C_param"][:], waits=prev, inc=s_ld)
    ctx.dma("sync", sc[:, 3:4], W["log_dt"].partition_broadcast(N), waits=prev,
            inc=s_ld)
    ldv = s_ld.now
    C_ = lambda i: sc[:, i:i + 1]
    def zstep(eng, fn):
        ctx.step(eng, fn, waits=[s_zoh.now], inc=s_zoh)
    ctx.step("scalar", lambda e: e.activation(out=C_(5), in_=C_(3), func=AF.Exp),
             waits=[ldv], inc=s_zoh)
    zstep("scalar", lambda e: e.activation(out=C_(4), in_=C_(0), func=AF.Exp))
    zstep("vector", lambda e: e.tensor_single_scalar(
        out=scm[:, 0:1], in_=C_(4), scalar=1e-6, op=ALU.is_lt))
    zstep("vector", lambda e: e.tensor_scalar(
        out=C_(4), in0=C_(4), scalar1=-1.0, scalar2=None, op0=ALU.mult))
    zstep("vector", lambda e: e.tensor_mul(C_(6), C_(4), C_(5)))
    zstep("scalar", lambda e: e.activation(out=C_(6), in_=C_(6), func=AF.Exp))
    zstep("vector", lambda e: e.memset(C_(11), 1.0))
    zstep("vector", lambda e: e.select(
        out=C_(10), mask=scm[:, 0:1], on_true=C_(11), on_false=C_(4)))
    zstep("vector", lambda e: e.reciprocal(out=C_(12), in_=C_(10)))
    zstep("vector", lambda e: e.tensor_scalar(
        out=C_(13), in0=C_(6), scalar1=-1.0, scalar2=None, op0=ALU.add))
    zstep("vector", lambda e: e.tensor_mul(C_(7), C_(13), C_(12)))
    zstep("vector", lambda e: e.select(
        out=C_(14), mask=scm[:, 0:1], on_true=C_(5), on_false=C_(7)))
    zstep("vector", lambda e: e.tensor_mul(C_(8), C_(14), C_(1)))
    zoh_done = s_zoh.now
    if io.get("dbg_sc") is not None:
        ctx.dma("gpsimd", io["dbg_sc"], sc[:], waits=[zoh_done], inc=s_ld)

    for nu in range(NTC):
        bi, bank, bfree = ppU.acquire()
        def mmU(e, nu=nu, bank=bank):
            for c in range(ND):
                ins = e.matmul(
                    out=bank[:], lhsT=win[:, c * N:(c + 1) * N],
                    rhs=g.xbf[:, c * S + nu * TC: c * S + (nu + 1) * TC],
                    start=(c == 0), stop=(c == ND - 1))
            return ins
        ctx.step("tensor", mmU, waits=[ldv, io["xbf_ready"]] +
                 ([bfree] if bfree else []), inc=s_mmU)
        ctx.step("vector",
                 lambda e, nu=nu, bank=bank: e.tensor_copy(
                     out=ut[:, nu * TC:(nu + 1) * TC], in_=bank[:]),
                 waits=[s_mmU.now], inc=s_evU)
        ppU.set_free(bi, s_evU.now)
    uev = s_evU.now
    bu = ctx.sbuf(u + "_bu", (N, S), F32, stack=stM)
    hsc = ctx.sbuf(u + "_hsc", (N, S), F32, stack=stM)
    ctx.step("vector", lambda e: e.tensor_scalar(
        out=bu[:], in0=ut[:], scalar1=C_(8), scalar2=None, op0=ALU.mult),
        waits=[uev, zoh_done])
    abar_b = sc[:, 6:7].broadcast_to([N, S])
    ctx.step("vector", lambda e: e.tensor_tensor_scan(
        out=hsc[:], data0=abar_b, data1=bu[:], initial=0.0,
        op0=ALU.mult, op1=ALU.add))
    ctx.step("vector", lambda e: e.tensor_scalar(
        out=yt[:], in0=hsc[:], scalar1=C_(2), scalar2=None, op0=ALU.mult),
        inc=s_scan)
    scan_done = s_scan.now
    if io.get("dbg_yt") is not None:
        ctx.dma("gpsimd", io["dbg_yt"], yt[:], waits=[scan_done], inc=s_ld)
    if io.get("dbg_ut") is not None:
        ctx.dma("gpsimd", io["dbg_ut"], ut[:], waits=[scan_done], inc=s_ld)

    ep.set_io(io["x_in"], io["x_out"])
    for tc in range(NTC):
        for dc in range(ND):
            res_info = ep.load_res(tc, dc)
            bi, bank, bfree = ppD.acquire()
            ctx.step("tensor",
                     lambda e, dc=dc, tc=tc, bank=bank: e.matmul(
                         out=bank[:], lhsT=wout[:, dc * 128:(dc + 1) * 128],
                         rhs=yt[:, tc * TC:(tc + 1) * TC],
                         start=True, stop=True),
                     waits=[scan_done] + ([bfree] if bfree else []), inc=s_mmD)
            ev = ep.consume(tc, dc, bank[:], None, [s_mmD.now], res_info)
            ppD.set_free(bi, ev)
        ep.finish(tc)
    stM.close()
    return ep.block_done()


def emit_quantize(ctx, q, x_ap, out_ap, K, waits, inc):
    NB = K // 32
    s_v, s_a, s_g, s_n = q["s_v"], q["s_a"], q["s_g"], q["s_n"]
    x3 = x_ap.rearrange("p (b e) -> p b e", e=32)
    ctx.step("vector", lambda e: e.tensor_reduce(
        out=q["bm"][:, 0:NB], in_=x3, axis=AX, op=ALU.max,
        apply_absolute_value=True), waits=list(waits) + [s_g.now], inc=s_n)
    ctx.step("vector", lambda e: e.tensor_scalar(
        out=q["bm"][:, 0:NB], in0=q["bm"][:, 0:NB], scalar1=1.0 / 7.5,
        scalar2=1e-30, op0=ALU.mult, op1=ALU.max), waits=[s_n.now], inc=s_n)
    ctx.step("vector", lambda e: e.reciprocal(out=q["rs"][:, 0:NB],
                                              in_=q["bm"][:, 0:NB]),
             waits=[s_n.now], inc=s_n)
    rsb = q["rs"][:, 0:NB].unsqueeze(2).broadcast_to([128, NB, 32])
    yq3 = q["yq"][:, 0:K].rearrange("p (b e) -> p b e", e=32)
    ctx.step("vector", lambda e: e.tensor_tensor(
        out=yq3, in0=x3, in1=rsb, op=ALU.mult), waits=[s_n.now], inc=s_v)
    ctx.step("scalar", lambda e: e.activation(
        out=q["ya"][:, 0:K], in_=q["yq"][:, 0:K], func=AF.Abs),
        waits=[s_v.now, s_g.now], inc=s_a)
    av = s_a.now
    ctx.step("gpsimd", lambda e: e.tensor_single_scalar(
        out=q["c2"][:, 0:K], in_=q["ya"][:, 0:K], scalar=2.0, op=ALU.is_ge),
        waits=[av, q["s_u"].now])
    ctx.step("gpsimd", lambda e: e.tensor_single_scalar(
        out=q["c4"][:, 0:K], in_=q["ya"][:, 0:K], scalar=4.0, op=ALU.is_ge),
        inc=s_g)
    gv = s_g.now
    ctx.step("vector", lambda e: e.tensor_scalar(
        out=q["t1"][:, 0:K], in0=q["c2"][:, 0:K], scalar1=-4.0, scalar2=8.0,
        op0=ALU.mult, op1=ALU.add), waits=[gv])
    ctx.step("vector", lambda e: e.scalar_tensor_tensor(
        out=q["t1"][:, 0:K], in0=q["c4"][:, 0:K], scalar=-2.0,
        in1=q["t1"][:, 0:K], op0=ALU.mult, op1=ALU.add))
    ctx.step("vector", lambda e: e.tensor_tensor(
        out=q["u"][:, 0:K], in0=q["yq"][:, 0:K], in1=q["t1"][:, 0:K],
        op=ALU.mult))
    ctx.step("vector", lambda e: e.tensor_copy(out=q["i32"][:, 0:K],
                                               in_=q["u"][:, 0:K]))
    ctx.step("vector", lambda e: e.tensor_copy(out=q["u"][:, 0:K],
                                               in_=q["i32"][:, 0:K]))
    ctx.step("vector", lambda e: e.tensor_scalar(
        out=q["t2"][:, 0:K], in0=q["c2"][:, 0:K], scalar1=0.125, scalar2=0.125,
        op0=ALU.mult, op1=ALU.add))
    ctx.step("vector", lambda e: e.scalar_tensor_tensor(
        out=q["t2"][:, 0:K], in0=q["c4"][:, 0:K], scalar=0.25,
        in1=q["t2"][:, 0:K], op0=ALU.mult, op1=ALU.add))
    ctx.step("vector", lambda e: e.tensor_tensor(
        out=q["u"][:, 0:K], in0=q["u"][:, 0:K], in1=q["t2"][:, 0:K],
        op=ALU.mult), inc=q["s_u"])
    bmb = q["bm"][:, 0:NB].unsqueeze(2).broadcast_to([128, NB, 32])
    u3 = q["u"][:, 0:K].rearrange("p (b e) -> p b e", e=32)
    out3 = out_ap.rearrange("p (b e) -> p b e", e=32)
    ctx.step("vector", lambda e: e.tensor_tensor(
        out=out3, in0=u3, in1=bmb, op=ALU.mult), inc=inc)


def quant_scratch(ctx, name, K, stack):
    q = {n: ctx.sbuf(f"{name}_{n}", (128, K), F32, stack=stack)
         for n in ("yq", "ya", "u")}
    for n in ("c2", "c4", "t1", "t2"):
        q[n] = ctx.sbuf(f"{name}_{n}", (128, K), BF16, stack=stack)
    q["i32"] = ctx.sbuf(f"{name}_i32", (128, K), I32, stack=stack)
    q["bm"] = ctx.sbuf(f"{name}_bm", (128, K // 32), F32, stack=stack)
    q["rs"] = ctx.sbuf(f"{name}_rs", (128, K // 32), F32, stack=stack)
    q["s_v"] = ctx.sem(f"{name}_sv")
    q["s_u"] = ctx.sem(f"{name}_su")
    q["s_n"] = ctx.sem(f"{name}_sn")
    q["s_a"] = ctx.sem(f"{name}_sa")
    q["s_g"] = ctx.sem(f"{name}_sg")
    return q


def emit_l0_ffn(ctx, g, ep, io, W):
    u = "l0f"
    prev = io["prev_done"]
    wdT_d = ctx.dscratch(u + "_wdT", (D, D), BF16)

    bias = ctx.sbuf(u + "_bias", (128, ND), F32)
    s_b = ctx.sem(u + "_b")
    ctx.dma("sync", bias[:], W["bias"][:], waits=list(prev), inc=s_b)
    bias_ld = s_b.now

    st0 = contextlib.ExitStack()
    q0 = quant_scratch(ctx, u + "_q0", D, st0)
    wf = ctx.sbuf(u + "_wf", (128, D), F32, stack=st0)
    wq = ctx.sbuf(u + "_wq", (128, D), BF16, stack=st0)
    wtv = ctx.sbuf(u + "_wtv", (128, D), BF16, stack=st0)
    ps_t = ctx.psum(u + "_psT", (128, D), BF16, stack=st0)
    s_ld0 = ctx.sem(u + "_ld0")
    s_q0 = ctx.sem(u + "_q0")
    s_tr0 = ctx.sem(u + "_tr0")
    s_o0 = ctx.sem(u + "_o0")
    for i in range(ND):
        wl = list(prev) if i == 0 else [(s_q0, s_q0.n)]
        ctx.dma("sync", wf[:], W["weight"][i * 128:(i + 1) * 128, :],
                waits=wl, inc=s_ld0)
        emit_quantize(ctx, q0, wf[:], wq[:], D,
                      [s_ld0.now] + ([(s_tr0, s_tr0.n - 1)] if i >= 1 else []),
                      s_q0)
        def tr(e, i=i):
            for j in range(ND):
                ins = e.transpose(out=ps_t[:, j * 128:(j + 1) * 128],
                                  in_=wq[:, j * 128:(j + 1) * 128],
                                  identity=g.ident_bf[:])
            return ins
        ctx.step("tensor", tr, waits=[s_q0.now, g.ident_done,
                                      (s_o0, s_o0.n - 16)], inc=s_tr0)
        ctx.step("vector", lambda e: e.tensor_copy(out=wtv[:], in_=ps_t[:]),
                 waits=[s_tr0.now], inc=s_tr0)
        ctx.dma("sync",
                wdT_d[:, i * 128:(i + 1) * 128]
                .rearrange("(j p) c -> p j c", p=128),
                wtv[:].rearrange("p (j c) -> p j c", c=128),
                waits=[s_tr0.now], inc=s_o0)
    p0_done = (s_o0, s_o0.n)
    st0.close()

    stM = contextlib.ExitStack()
    q1 = quant_scratch(ctx, u + "_q1", 2 * TC, stM)
    wdT_sb = ctx.sbuf(u + "_wdsb", (128, ND * D), BF16, stack=stM)
    xtok = ctx.sbuf(u + "_xtok", (128, D), F32, stM)
    xqtok = ctx.sbuf(u + "_xqtok", (128, D), BF16, stM)
    xqT = ctx.sbuf(u + "_xqT", (128, ND * TC), BF16, stM)
    xfin = ctx.sbuf(u + "_xfin", (128, ND * TC), F32, stM)
    ps_ti = ctx.psum(u + "_psTi", (128, TC), F32, stack=stM)
    ps_to = ctx.psum(u + "_psTo", (128, D), BF16, stack=stM)
    ppM = PsumPool(ctx, u + "_psM", 2, (128, TC), stack=stM)
    s_wsb = ctx.sem(u + "_wsb")
    s_xin = ctx.sem(u + "_xin")
    s_ti = ctx.sem(u + "_ti")
    s_tie = ctx.sem(u + "_tie")
    s_q1 = ctx.sem(u + "_q1")
    s_to = ctx.sem(u + "_to")
    s_toe = ctx.sem(u + "_toe")
    s_mm = ctx.sem(u + "_mm")
    ctx.dma("sync", wdT_sb[:].rearrange("p (kc c) -> p kc c", kc=ND),
            wdT_d.rearrange("(kc p) c -> p kc c", p=128), waits=[p0_done],
            inc=s_wsb)
    wsb_rdy = s_wsb.now
    ep.set_io(io["x_in"], io["x_out"])
    for tc in range(NTC):
        wl = [(s_q1, s_q1.n)] if tc >= 1 else []
        ctx.dma("sync", xfin[:].rearrange("p (c t) -> p c t", c=ND),
                io["x_in"].rearrange("(c p) t -> p c t", p=128)
                [:, :, tc * TC:(tc + 1) * TC], waits=wl, inc=s_xin)
        xrdy = s_xin.now
        for tt in range(4):
            kq = 4 * tc + tt
            for hw in range(2):
                def tri(e, tt=tt, hw=hw):
                    for j in range(4):
                        dc = hw * 4 + j
                        ins = e.transpose(
                            out=ps_ti[:, j * 128:(j + 1) * 128],
                            in_=xfin[:, dc * TC + tt * 128:
                                     dc * TC + (tt + 1) * 128],
                            identity=g.ident_f32[:])
                    return ins
                ctx.step("tensor", tri,
                         waits=[xrdy, g.ident_done, (s_tie, s_tie.n)],
                         inc=s_ti)
                ctx.step("scalar",
                         lambda e, hw=hw: e.copy(
                             out=xtok[:, hw * TC:(hw + 1) * TC], in_=ps_ti[:]),
                         waits=[s_ti.now, (s_q1, s_q1.n)],
                         inc=s_tie)
            emit_quantize(ctx, q1, xtok[:], xqtok[:], D,
                          [s_tie.now] + ([(s_to, s_to.n)] if kq >= 1 else []),
                          s_q1)
            ctx.step("tensor",
                     lambda e, tt=tt: [
                         e.transpose(out=ps_to[:, dc * 128:(dc + 1) * 128],
                                     in_=xqtok[:, dc * 128:(dc + 1) * 128],
                                     identity=g.ident_bf[:])
                         for dc in range(ND)][-1],
                     waits=[s_q1.now, (s_toe, s_toe.n)], inc=s_to)
            ctx.step("vector",
                     lambda e, tt=tt: [
                         e.tensor_copy(
                             out=xqT[:, dc * TC + tt * 128:
                                     dc * TC + (tt + 1) * 128],
                             in_=ps_to[:, dc * 128:(dc + 1) * 128])
                         for dc in range(ND)][-1],
                     waits=[s_to.now], inc=s_toe)
        xq_rdy = (s_toe, s_toe.n)
        for dc in range(ND):
            res_info = ep.load_res(tc, dc)
            bi, bank, bfree = ppM.acquire()
            def mmq(e, dc=dc, bank=bank):
                for c in range(ND):
                    ins = e.matmul(
                        out=bank[:],
                        lhsT=wdT_sb[:, c * D + dc * 128: c * D + dc * 128 + 128],
                        rhs=xqT[:, c * TC:(c + 1) * TC],
                        start=(c == 0), stop=(c == ND - 1))
                return ins
            ctx.step("tensor", mmq,
                     waits=[xq_rdy, wsb_rdy] + ([bfree] if bfree else []),
                     inc=s_mm)
            ev = ep.consume(tc, dc, bank[:], bias[:, dc:dc + 1], [s_mm.now],
                            res_info)
            ppM.set_free(bi, ev)
        ep.finish(tc)
    stM.close()
    return ep.block_done()


def l2ffn_coeffs(w):
    i = np.arange(4096)
    packs = {}
    for nm in ("lowD", "lowO", "lowOT", "highD", "highO", "highOT"):
        packs[nm] = []
    for s in range(12):
        pair = (i >> (s + 1) << s) | (i & ((1 << s) - 1))
        bit = (i >> s) & 1
        Dv = w[s, pair, bit, bit]
        Ov = w[s, pair, bit, 1 - bit]
        OTv = w[s, pair, 1 - bit, bit]
        if s < 7:
            packs["lowD"].append(Dv)
            packs["lowO"].append(Ov)
            packs["lowOT"].append(OTv)
        else:
            packs["highD"].append(Dv)
            packs["highO"].append(Ov)
            packs["highOT"].append(OTv)
    out = {}
    for nm, lst in packs.items():
        a = np.stack(lst)
        ns = a.shape[0]
        a = a.reshape(ns * 32, 128).T
        out[nm] = np.ascontiguousarray(a.astype(np.float32))
    return out


def l2ffn_masks():
    m = [np.eye(128, dtype=BF)]
    for s in range(7):
        q = np.arange(128) ^ (1 << s)
        mm = np.zeros((128, 128), dtype=BF)
        mm[np.arange(128), q] = 1.0
        m.append(mm)
    return np.ascontiguousarray(np.concatenate(m, axis=1))


def emit_l2_ffn(ctx, g, ep, io, W):
    u = "l2f"
    prev = io["prev_done"]
    a_d = ctx.dscratch(u + "_a", (D, EXP), BF16)
    b_d = ctx.dscratch(u + "_b", (EXP, D), BF16)
    apre_d = ctx.dscratch(u + "_apre", (EXP, 1025), BF16)
    bpre_d = ctx.dscratch(u + "_bpre", (EXP, D), BF16)

    biasA = ctx.sbuf(u + "_biasA", (128, NE), F32)
    bias2 = ctx.sbuf(u + "_bias2", (128, ND), F32)

    stP = contextlib.ExitStack()
    coe = ctx.sbuf(u + "_coe", (128, 2048), F32, stack=stP)
    msk = ctx.sbuf(u + "_msk", (128, 8 * 128), BF16, stack=stP)
    vec2 = ctx.sbuf(u + "_vec2", (128, NE + 2 * ND), F32, stack=stP)
    s_c = ctx.sem(u + "_c")
    offs = dict(up_lowD=0, up_lowO=224, up_lowOT=448, dn_lowD=672, dn_lowO=896,
                dn_lowOT=1120, up_highD=1344, up_highO=1504, dn_highD=1664,
                dn_highOT=1824)
    first = True
    for nm in offs:
        n = 224 if "low" in nm else 160
        ctx.dma("sync", coe[:, offs[nm]:offs[nm] + n], W[nm][:],
                waits=(list(prev) if first else []), inc=s_c)
        first = False
    ctx.dma("sync", msk[:], W["masks"][:], inc=s_c)
    ctx.dma("sync", vec2[:, 0:NE], W["up_bias"][:], inc=s_c)
    ctx.dma("sync", vec2[:, NE:NE + ND], W["b_con"][:], inc=s_c)
    ctx.dma("sync", vec2[:, NE + ND:NE + 2 * ND], W["down_bias"][:], inc=s_c)
    c_rdy = s_c.now
    s_b2 = ctx.sem(u + "_b2")
    ctx.step("vector", lambda e: e.tensor_tensor(
        out=bias2[:], in0=vec2[:, NE:NE + ND], in1=vec2[:, NE + ND:NE + 2 * ND],
        op=ALU.add), waits=[c_rdy], inc=s_b2)
    bias2_rdy = s_b2.now

    mdiag = msk[:, 0:128]
    moff = lambda s: msk[:, (s + 1) * 128:(s + 2) * 128]
    ccol = lambda nm, s, J: coe[:, offs[nm] + (s if "low" in nm else s - 7) * 32
                               + J: offs[nm] + (s if "low" in nm else s - 7) * 32
                               + J + 1]

    s_bld = ctx.sem(u + "_bld")
    s_cmp = ctx.sem(u + "_cmp")

    def build_S(Sdst, s, J, Dnm, Onm, extra=()):
        ctx.step("vector", lambda e: e.tensor_scalar(
            out=Sdst, in0=mdiag, scalar1=ccol(Dnm, s, J), scalar2=None,
            op0=ALU.mult), waits=[c_rdy, s_bld.now] + list(extra), inc=s_bld)
        ctx.step("vector", lambda e: e.scalar_tensor_tensor(
            out=Sdst, in0=moff(s), scalar=ccol(Onm, s, J), in1=Sdst,
            op0=ALU.mult, op1=ALU.add), waits=[s_bld.now], inc=s_bld)

    def compose_low(J, specs, SB, R0, R1, ps_c, extra_first=()):
        build_S(R0[:], specs[0][0], J, specs[0][1], specs[0][2],
                extra=list(extra_first))
        cur, nxt = R0, R1
        for (s, Dn, On) in specs[1:]:
            build_S(SB[:], s, J, Dn, On, extra=[s_cmp.now])
            ctx.step("tensor",
                     lambda e, cur=cur: e.matmul(
                         out=ps_c[:], lhsT=SB[:], rhs=cur[:],
                         start=True, stop=True),
                     waits=[s_bld.now, s_cmp.now], inc=s_cmp)
            ctx.step("vector",
                     lambda e, nxt=nxt: e.tensor_copy(out=nxt[:], in_=ps_c[:]),
                     waits=[s_cmp.now], inc=s_cmp)
            cur, nxt = nxt, cur
        return cur

    def high_dram(src_d, NCOL, stages, Dnm, Onm, ready, tag):
        bl = [ctx.sbuf(f"{u}_hl{tag}{i}", (128, NCOL), BF16, stack=stP)
              for i in range(2)]
        bh = [ctx.sbuf(f"{u}_hh{tag}{i}", (128, NCOL), BF16, stack=stP)
              for i in range(2)]
        bt = [ctx.sbuf(f"{u}_ht{tag}{i}", (128, NCOL), BF16, stack=stP)
              for i in range(2)]
        s_l = ctx.sem(f"{u}_hld{tag}")
        s_k = ctx.sem(f"{u}_hk{tag}")
        s_s = ctx.sem(f"{u}_hs{tag}")
        kk = 0
        for si, s in enumerate(stages):
            stride = 1 << (s - 7)
            stage_start_store = s_s.n
            for t in range(NE):
                if t & stride:
                    continue
                t2 = t | stride
                b = kk % 2
                w0 = [ready] if kk == 0 else []
                if si > 0 and (kk % 16) == 0:
                    w0.append((s_s, stage_start_store))
                if kk >= 2:
                    w0.append((s_s, s_s.n - 32))
                ctx.dma("sync", bl[b][:], src_d[t * 128:(t + 1) * 128, :],
                        waits=w0, inc=s_l)
                ctx.dma("sync", bh[b][:], src_d[t2 * 128:(t2 + 1) * 128, :],
                        inc=s_l)
                lrd = s_l.now
                ctx.step("vector", lambda e, b=b, s=s, t=t: e.tensor_scalar(
                    out=bt[b][:], in0=bl[b][:], scalar1=ccol(Dnm, s, t),
                    scalar2=None, op0=ALU.mult), waits=[lrd, c_rdy])
                ctx.step("vector", lambda e, b=b, s=s, t=t:
                         e.scalar_tensor_tensor(
                             out=bt[b][:], in0=bh[b][:], scalar=ccol(Onm, s, t),
                             in1=bt[b][:], op0=ALU.mult, op1=ALU.add))
                ctx.step("vector", lambda e, b=b, s=s, t2=t2: e.tensor_scalar(
                    out=bh[b][:], in0=bh[b][:], scalar1=ccol(Dnm, s, t2),
                    scalar2=None, op0=ALU.mult))
                ctx.step("vector", lambda e, b=b, s=s, t2=t2:
                         e.scalar_tensor_tensor(
                             out=bh[b][:], in0=bl[b][:], scalar=ccol(Onm, s, t2),
                             in1=bh[b][:], op0=ALU.mult, op1=ALU.add),
                         inc=s_k)
                kv = s_k.now
                ctx.dma("scalar", src_d[t * 128:(t + 1) * 128, :], bt[b][:],
                        waits=[kv], inc=s_s)
                ctx.dma("scalar", src_d[t2 * 128:(t2 + 1) * 128, :], bh[b][:],
                        waits=[kv], inc=s_s)
                kk += 1
            stage_done = (s_s, s_s.n)
            ready = stage_done
        return (s_s, s_s.n)

    SB = ctx.sbuf(u + "_SB", (128, 128), BF16, stack=stP)
    R0 = ctx.sbuf(u + "_R0", (128, 128), BF16, stack=stP)
    R1 = ctx.sbuf(u + "_R1", (128, 128), BF16, stack=stP)
    wldb = [ctx.sbuf(u + f"_wld{i}", (128, 1025), BF16, stack=stP)
            for i in range(2)]
    ps_c = ctx.psum(u + "_psc", (128, 128), F32, stack=stP)
    ps_a = ctx.psum(u + "_psa", (128, 3 * 512), F32, stack=stP)
    wov = ctx.sbuf(u + "_wov", (128, 1088), BF16, stack=stP)

    s_wld = ctx.sem(u + "_wld")
    s_app = ctx.sem(u + "_app")
    s_wo = ctx.sem(u + "_wo")
    NCOL = 1025
    for J in range(NE):
        ctx.dma("gpsimd", wldb[J % 2][:, 0:NCOL],
                W["WexpT_aug"][J * 128:(J + 1) * 128, :],
                waits=([(s_app, max(0, s_app.n - 3))] if J >= 2
                       else list(prev)),
                inc=s_wld)
        wrdy = s_wld.now
        cur = compose_low(J, [(6, "up_lowD", "up_lowOT")] +
                          [(s, "up_lowD", "up_lowO") for s in range(5, -1, -1)],
                          SB, R0, R1, ps_c,
                          extra_first=[(s_app, max(0, s_app.n - 1))])
        def app(e, cur=cur, J=J):
            o = None
            for a0 in (0, 512, 1024):
                n = min(512, NCOL - a0)
                o = e.matmul(out=ps_a[:, a0:a0 + n], lhsT=cur[:],
                             rhs=wldb[J % 2][:, a0:a0 + n],
                             start=True, stop=True)
            return o
        ctx.step("tensor", app, waits=[s_cmp.now, wrdy, (s_app, s_app.n)],
                 inc=s_app)
        ctx.step("vector",
                 lambda e: e.tensor_copy(out=wov[:, 0:NCOL],
                                         in_=ps_a[:, 0:NCOL]),
                 waits=[s_app.now, (s_wo, 16 * J)], inc=s_app)
        ctx.dma("sync", apre_d[J * 128:(J + 1) * 128, :], wov[:, 0:NCOL],
                waits=[s_app.now], inc=s_wo)
    up_comp = (s_wo, s_wo.n)
    up_hi = high_dram(apre_d, 1025, range(7, 12), "up_highD", "up_highO",
                      up_comp, "U")
    s_lw = ctx.sem(u + "_lw")
    s_tc = ctx.sem(u + "_tc")
    s_te = ctx.sem(u + "_te")
    s_wo3 = ctx.sem(u + "_wo3")
    s_bx = ctx.sem(u + "_bx")
    for J in range(NE):
        w0 = [up_hi] if J == 0 else [(s_wo3, s_wo3.n)]
        ctx.dma("sync", wov[:, 0:NCOL], apre_d[J * 128:(J + 1) * 128, :],
                waits=w0, inc=s_lw)
        ctx.step("tensor",
                 lambda e: [
                     e.transpose(out=ps_a[:].bitcast(BF16)[:, c * 128:(c + 1) * 128],
                                 in_=wov[:, c * 128:(c + 1) * 128],
                                 identity=g.ident_bf[:])
                     for c in range(ND)][-1],
                 waits=[s_lw.now, g.ident_done, (s_te, s_te.n)], inc=s_tc)
        ctx.step("gpsimd",
                 lambda e, J=J: e.tensor_copy(
                     out=biasA[:, J:J + 1], in_=wov[:, 1024:1025]),
                 waits=[s_lw.now], inc=s_bx)
        ctx.step("vector",
                 lambda e: e.tensor_copy(out=wov[:, 0:D],
                                         in_=ps_a[:].bitcast(BF16)[:, 0:D]),
                 waits=[s_tc.now, s_bx.now], inc=s_te)
        ctx.dma("sync",
                a_d[:, J * 128:(J + 1) * 128]
                .rearrange("(c p) f -> p c f", p=128),
                wov[:, 0:D].rearrange("p (c f) -> p c f", f=128),
                waits=[s_te.now], inc=s_wo3)
    up_done = (s_wo3, s_wo3.n)
    s_ba = ctx.sem(u + "_ba")
    ctx.step("vector", lambda e: e.tensor_tensor(
        out=biasA[:], in0=biasA[:], in1=vec2[:, 0:NE], op=ALU.add),
        waits=[(s_bx, s_bx.n), c_rdy], inc=s_ba)
    biasA_rdy = s_ba.now

    s_pcd = ctx.sem(u + "_pcd")
    for J in range(NE):
        w0 = [(s_pcd, s_pcd.n - 32)] if J >= 2 else [(s_app, s_app.n)]
        ctx.dma("gpsimd", wldb[J % 2][:, 0:D],
                W["W_con"][J * 128:(J + 1) * 128, :], waits=w0, inc=s_pcd)
        ctx.dma("scalar", bpre_d[J * 128:(J + 1) * 128, :],
                wldb[J % 2][:, 0:D], waits=[s_pcd.now], inc=s_pcd)
    pcd_done = (s_pcd, s_pcd.n)
    dn_hi = high_dram(bpre_d, D, range(11, 6, -1), "dn_highD", "dn_highOT",
                      pcd_done, "D")
    s_app2 = ctx.sem(u + "_app2")
    s_wo2 = ctx.sem(u + "_wo2")
    s_wld2 = ctx.sem(u + "_wld2")
    for J in range(NE):
        w0 = [dn_hi] if J < 2 else [(s_app2, s_app2.n - 3)]
        ctx.dma("sync", wldb[J % 2][:, 0:D],
                bpre_d[J * 128:(J + 1) * 128, :], waits=w0, inc=s_wld2)
        wrdy = s_wld2.now
        cur = compose_low(J, [(0, "dn_lowD", "dn_lowO")] +
                          [(s, "dn_lowD", "dn_lowOT") for s in range(1, 7)],
                          SB, R0, R1, ps_c,
                          extra_first=[(s_app2, max(0, s_app2.n - 1))]
                          if J >= 1 else [up_done])
        def app2(e, cur=cur, J=J):
            o = None
            for a0 in (0, 512):
                o = e.matmul(out=ps_a[:, a0:a0 + 512], lhsT=cur[:],
                             rhs=wldb[J % 2][:, a0:a0 + 512],
                             start=True, stop=True)
            return o
        ctx.step("tensor", app2, waits=[s_cmp.now, wrdy, (s_app2, s_app2.n)],
                 inc=s_app2)
        ctx.step("vector",
                 lambda e: e.tensor_copy(out=wov[:, 0:D], in_=ps_a[:, 0:D]),
                 waits=[(s_app2, s_app2.n), (s_wo2, s_wo2.n)], inc=s_app2)
        ctx.dma("sync", b_d[J * 128:(J + 1) * 128, :], wov[:, 0:D],
                waits=[(s_app2, s_app2.n)], inc=s_wo2)
    dn_done = (s_wo2, s_wo2.n)
    stP.close()

    stM = contextlib.ExitStack()
    ybuf = ctx.sbuf(u + "_ybuf", (128, NE * 1024), BF16, stack=stM)
    ppF = PsumPool(ctx, u + "_psF", 4, (128, 512), stack=stM)
    ws1 = WStream(ctx, g.wcol[0:2], first_wait=[up_done, io["xbf_ready"]])
    ws2 = WStream(ctx, g.wcol[2:3], first_wait=[dn_done])
    s_mm1 = ctx.sem(u + "_mm1")
    s_ev1 = ctx.sem(u + "_ev1")
    s_mm2 = ctx.sem(u + "_mm2")
    ep.set_io(io["x_in"], io["x_out"])
    TQ = 1024
    for tq in range(S // TQ):
        mm2_at_start = (s_mm2, s_mm2.n)
        for fo in range(NE):
            buf, rdy, idx = ws1.fetch(a_d[:, fo * 128:(fo + 1) * 128], D, False)
            last = None
            for nu in range(2):
                bi, bank, bfree = ppF.acquire()
                def mm1(e, buf=buf, tq=tq, nu=nu, bank=bank):
                    for c in range(ND):
                        ins = e.matmul(
                            out=bank[:], lhsT=buf[:, c * 128:(c + 1) * 128],
                            rhs=g.xbf[:, c * S + tq * TQ + nu * 512:
                                      c * S + tq * TQ + (nu + 1) * 512],
                            start=(c == 0), stop=(c == ND - 1))
                    return ins
                ctx.step("tensor", mm1,
                         waits=[rdy] + ([bfree] if bfree else []), inc=s_mm1)
                last = s_mm1.now
                ydst = ybuf[:, fo * 1024 + nu * 512: fo * 1024 + (nu + 1) * 512]
                ctx.step("scalar",
                         lambda e, bank=bank, ydst=ydst, fo=fo:
                         e.activation(out=ydst, in_=bank[:], func=AF.Gelu,
                                      bias=biasA[:, fo:fo + 1]),
                         waits=[last, biasA_rdy, mm2_at_start], inc=s_ev1)
                ppF.set_free(bi, s_ev1.now)
            ws1.release(idx, last)
        ev1_done = (s_ev1, s_ev1.n)
        for half in range(2):
            tc = tq * 2 + half
            for dc in range(ND):
                res_info = ep.load_res(tc, dc)
                buf, rdy, idx = ws2.fetch(b_d[:, dc * 128:(dc + 1) * 128],
                                          EXP, False)
                bi, bank, bfree = ppF.acquire()
                def mm2(e, buf=buf, half=half, bank=bank):
                    for c in range(NE):
                        ins = e.matmul(
                            out=bank[:], lhsT=buf[:, c * 128:(c + 1) * 128],
                            rhs=ybuf[:, c * 1024 + half * 512:
                                     c * 1024 + (half + 1) * 512],
                            start=(c == 0), stop=(c == NE - 1))
                    return ins
                ctx.step("tensor", mm2,
                         waits=[rdy, ev1_done] + ([bfree] if bfree else []),
                         inc=s_mm2)
                ws2.release(idx, s_mm2.now)
                ev = ep.consume(tc, dc, bank[:], bias2[:, dc:dc + 1],
                                [s_mm2.now, bias2_rdy], res_info)
                ppF.set_free(bi, ev)
            ep.finish(tc)
    stM.close()
    return ep.block_done()


def attn_host_consts(has_alibi):
    ind = np.zeros((128, 8 * 16), dtype=BF)
    for oc in range(8):
        for p in range(128):
            ind[p, oc * 16 + 2 * oc + (p >= 64)] = 1.0
    out = {"ind": ind}
    if has_alibi:
        def slopes_pow2(n):
            start = 2.0 ** (-2.0 ** (-(math.log2(n) - 3)))
            return [start * start ** i for i in range(n)]
        sl = np.array(slopes_pow2(16), dtype=np.float64)
        pos = np.arange(S, dtype=np.float64)
        ak = (sl[:, None] * pos[None, :])
        ak_hi = ak.astype(BF)
        ak_lo = (ak - ak_hi.astype(np.float64)).astype(BF)
        out["alibi_k"] = np.ascontiguousarray(ak_hi)
        out["alibi_k_lo"] = np.ascontiguousarray(ak_lo)
        out["alibi_q"] = ak.astype(np.float32)
    return out


def emit_attn(ctx, g, ep, io, W, cfg):
    u = cfg["name"]
    su = "at"
    alibi = cfg["alibi"]
    KDIM = 67 if alibi else 65
    prev = io["prev_done"]
    qT_d = ctx.dscratch(u + "_qd", (D, S), BF16)
    kT_d = ctx.dscratch(u + "_kd", (D, S), BF16)
    v_d = ctx.dscratch(u + "_vd", (H, S, HD), BF16)
    b_d = ctx.dscratch(u + "_bd", (16, S), F32)
    lrow_d = ctx.dscratch(u + "_lrd", (2, 512), F32)

    ind_sb = ctx.sbuf(u + "_ind", (128, 128), BF16)
    bias_sb = ctx.sbuf(u + "_bias", (128, 4 * ND), F32)
    s_c = ctx.sem(su + "_c")
    ctx.dma("sync", ind_sb[:], W["ind"][:], waits=list(prev), inc=s_c)
    if cfg["qkv_bias"]:
        for i, nm in enumerate(("bq", "bk", "bv", "bo")):
            ctx.dma("sync", bias_sb[:, i * ND:(i + 1) * ND], W[nm][:], inc=s_c)
    c_rdy = s_c.now

    stA = contextlib.ExitStack()
    ws = WStream(ctx, g.wcol[0:2], first_wait=[io["xbf_ready"]] + list(prev))
    ppA = PsumPool(ctx, u + "_psA", 2, (128, 512), stack=stA)
    ps_ind = ctx.psum(u + "_psI", (16, 512), stack=stA)
    ps_tv = ctx.psum(u + "_psTv", (128, 512), BF16, stack=stA)
    stg = ctx.sbuf(u + "_stg", (128, 4 * 512), BF16, stack=stA)
    sqg = ctx.sbuf(u + "_sqg", (128, 512), BF16, stack=stA)
    nrm = ctx.sbuf(u + "_nrm", (16, 2 * S), F32, stack=stA)
    vtv = ctx.sbuf(u + "_vtv", (128, 512), BF16, stack=stA)
    aq_sb = ctx.sbuf(u + "_aq", (16, S), F32, stack=stA) if alibi else None
    s_mmA = ctx.sem(su + "_mmA")
    s_evA = ctx.sem(su + "_evA")
    s_sqA = ctx.sem(su + "_sqA")
    s_indA = ctx.sem(su + "_indA")
    s_nrm = ctx.sem(su + "_nrm")
    s_oA = ctx.sem(su + "_oA")
    s_tvA = ctx.sem(su + "_tvA")
    ctx.step("vector", lambda e: e.memset(nrm[:], 0.0), waits=list(prev),
             inc=s_nrm)
    if alibi:
        ctx.dma("sync", aq_sb[:], W["alibi_q"][:], inc=s_c)
    slot_free = [None] * 4
    for wi, wname in enumerate(("WqT", "WkT", "WvT")):
        dst_d = (qT_d, kT_d, None)[wi]
        evsc = (1.0, cfg["kscale"], cfg["vscale"])[wi]
        bcol0 = (0, 1, 2)[wi] * ND
        for oc in range(ND):
            buf, rdy, idx = ws.fetch(W[wname][:, oc * 128:(oc + 1) * 128], D,
                                     True)
            last = None
            for nu in range(NTC):
                bi, bank, bfree = ppA.acquire()
                def mmA(e, buf=buf, nu=nu, bank=bank):
                    for c in range(ND):
                        ins = e.matmul(
                            out=bank[:], lhsT=buf[:, c * 128:(c + 1) * 128],
                            rhs=g.xbf[:, c * S + nu * 512:c * S + (nu + 1) * 512],
                            start=(c == 0), stop=(c == ND - 1))
                    return ins
                ctx.step("tensor", mmA,
                         waits=[rdy] + ([bfree] if bfree else []), inc=s_mmA)
                last = s_mmA.now
                sl = (wi * NTC * ND + oc * NTC + nu) % 4
                sslot = stg[:, sl * 512:(sl + 1) * 512]
                wst = [last, c_rdy]
                if slot_free[sl] is not None:
                    wst += slot_free[sl]
                bcol = (bias_sb[:, bcol0 + oc:bcol0 + oc + 1]
                        if cfg["qkv_bias"] else None)
                ctx.step("scalar",
                         lambda e, bank=bank, sslot=sslot, bcol=bcol,
                         evsc=evsc: e.activation(
                             out=sslot, in_=bank[:], func=AF.Identity,
                             **({"bias": bcol} if bcol is not None else {}),
                             scale=evsc),
                         waits=wst, inc=s_evA)
                ppA.set_free(bi, s_evA.now)
                evv = s_evA.now
                if wi < 2:
                    ctx.dma("sync",
                            dst_d[oc * 128:(oc + 1) * 128,
                                  nu * 512:(nu + 1) * 512], sslot,
                            waits=[evv], inc=s_oA)
                    ctx.step("scalar",
                             lambda e, sslot=sslot: e.activation(
                                 out=sqg[:], in_=sslot, func=AF.Square),
                             waits=[evv, (s_indA, s_indA.n)], inc=s_sqA)
                    ctx.step("tensor",
                             lambda e, oc=oc: e.matmul(
                                 out=ps_ind[:],
                                 lhsT=ind_sb[:, oc * 16:(oc + 1) * 16],
                                 rhs=sqg[:], start=True, stop=True),
                             waits=[s_sqA.now, (s_nrm, s_nrm.n)], inc=s_indA)
                    nslot = nrm[:, wi * S + nu * 512: wi * S + (nu + 1) * 512]
                    ctx.step("vector",
                             lambda e, nslot=nslot: e.tensor_tensor(
                                 out=nslot, in0=nslot, in1=ps_ind[:],
                                 op=ALU.add),
                             waits=[s_indA.now], inc=s_nrm)
                    slot_free[sl] = [(s_oA, s_oA.n), (s_sqA, s_sqA.n)]
                else:
                    ctx.step("tensor",
                             lambda e, sslot=sslot: [
                                 e.transpose(
                                     out=ps_tv[:, tt * 128:(tt + 1) * 128],
                                     in_=sslot[:, tt * 128:(tt + 1) * 128],
                                     identity=g.ident_bf[:])
                                 for tt in range(4)][-1],
                             waits=[evv, g.ident_done, (s_tvA, s_tvA.n)],
                             inc=s_tvA)
                    ctx.step("vector",
                             lambda e: e.tensor_copy(out=vtv[:], in_=ps_tv[:]),
                             waits=[s_tvA.now, (s_oA, s_oA.n)], inc=s_tvA)
                    tvv = (s_tvA, s_tvA.n)
                    for hh in range(2):
                        hidx = 2 * oc + hh
                        ctx.dma("sync",
                                v_d[hidx, nu * 512:(nu + 1) * 512, :]
                                .rearrange("(tt p) c -> p tt c", p=128),
                                vtv[:].rearrange("p (tt c) -> p tt c", c=128)
                                [:, :, hh * 64:(hh + 1) * 64],
                                waits=[tvv], inc=s_oA)
                    slot_free[sl] = [tvv]
            ws.release(idx, last)
    projA_done = [(s_oA, s_oA.n), (s_nrm, s_nrm.n)]
    brow = ctx.sbuf(u + "_brw", (16, S), F32, stack=stA)
    nkm = ctx.sbuf(u + "_nkm", (16, 2), F32, stack=stA)
    s_bnd = ctx.sem(su + "_bnd")
    ctx.step("scalar", lambda e: e.activation(
        out=nrm[:, 0:S], in_=nrm[:, 0:S], func=AF.Sqrt), waits=projA_done,
        inc=s_bnd)
    ctx.step("scalar", lambda e: e.activation(
        out=nrm[:, S:2 * S], in_=nrm[:, S:2 * S], func=AF.Sqrt), inc=s_bnd)
    ctx.step("vector", lambda e: e.tensor_reduce(
        out=nkm[:, 0:1], in_=nrm[:, S:2 * S], axis=AX, op=ALU.max),
        waits=[s_bnd.now], inc=s_bnd)
    ctx.step("vector", lambda e: e.tensor_scalar(
        out=brow[:], in0=nrm[:, 0:S], scalar1=nkm[:, 0:1], scalar2=1.0,
        op0=ALU.mult, op1=ALU.add), waits=[s_bnd.now], inc=s_bnd)
    if alibi:
        ctx.step("vector", lambda e: e.tensor_tensor(
            out=brow[:], in0=brow[:], in1=aq_sb[:], op=ALU.add),
            waits=[s_bnd.now, s_c.now], inc=s_bnd)
    ctx.dma("sync", b_d[:], brow[:], waits=[s_bnd.now], inc=s_bnd)
    bnd_done = (s_bnd, s_bnd.n)
    stA.close()

    stB = contextlib.ExitStack()
    kh = [ctx.sbuf(u + f"_kh{i}", (KDIM, S), BF16, stack=stB) for i in range(2)]
    rh = [ctx.sbuf(u + f"_rh{i}", (KDIM, S), BF16, stack=stB) for i in range(2)]
    vh = [ctx.sbuf(u + f"_vh{i}", (128, NT * HD), BF16, stack=stB)
          for i in range(2)]
    pt = ctx.sbuf(u + "_pt", (128, 2048), BF16, stack=stB)
    lr = ctx.sbuf(u + "_lr", (1, 2 * 1024), F32, stack=stB)
    rlb = ctx.sbuf(u + "_rlb", (128, 512), F32, stack=stB)
    ps_sc = ctx.psum(u + "_psS", (128, 2048), F32, stack=stB)
    ppV = PsumPool(ctx, u + "_psV", 2, (128, 512), stack=stB)
    s_kv = ctx.sem(su + "_kv")
    s_mmS = ctx.sem(su + "_mmS")
    s_exp = ctx.sem(su + "_exp")
    s_msk = ctx.sem(su + "_msk")
    s_av = ctx.sem(su + "_av")
    s_lmm = ctx.sem(su + "_lmm")
    s_le = ctx.sem(su + "_le")
    s_rl = ctx.sem(su + "_rl")
    s_oe = ctx.sem(su + "_oe")
    for pair in range(H // 2):
        hA, hB = 2 * pair, 2 * pair + 1
        ldw = [bnd_done] if pair == 0 else [(s_mmS, s_mmS.n), (s_av, s_av.n),
                                            (s_lmm, s_lmm.n)]
        for hh, hidx in ((0, hA), (1, hB)):
            ctx.dma("sync", kh[hh][0:64, :],
                    kT_d[64 * hidx:64 * hidx + 64, :], waits=ldw, inc=s_kv)
            ctx.step("vector",
                     lambda e, hh=hh: e.memset(kh[hh][64:65, :], -1.0),
                     waits=ldw, inc=s_kv)
            if alibi:
                ctx.dma("sync", kh[hh][65:66, :],
                        W["alibi_k"][hidx:hidx + 1, :], waits=ldw, inc=s_kv)
                ctx.dma("sync", kh[hh][66:67, :],
                        W["alibi_k_lo"][hidx:hidx + 1, :], waits=ldw, inc=s_kv)
            ctx.dma("sync", rh[hh][0:64, :],
                    qT_d[64 * hidx:64 * hidx + 64, :], waits=ldw, inc=s_kv)
            ctx.dma("gpsimd", rh[hh][64:65, :], b_d[hidx:hidx + 1, :],
                    waits=ldw, inc=s_kv)
            if alibi:
                ctx.dma("sync", rh[hh][65:66, :], W["ones_row"][:],
                        waits=ldw, inc=s_kv)
                ctx.dma("sync", rh[hh][66:67, :], W["ones_row"][:],
                        waits=ldw, inc=s_kv)
            ctx.dma("sync",
                    vh[hh][:].rearrange("p (kt c) -> p kt c", c=HD),
                    v_d[hidx].rearrange("(kt p) c -> p kt c", p=128),
                    waits=ldw, inc=s_kv)
        kv_rdy = s_kv.now
        for qc in range(NTC):
            nkt = 4 * (qc + 1)
            li_A, lbank_A = None, ep.ps_stat[:, 0:512]
            lbank_B = ep.ps_stat[:, 512:1024]
            bi, avbank, avfree = ppV.acquire()
            for kt in range(nkt):
                half = kt % 2
                scA = ps_sc[:, half * 1024:half * 1024 + 512]
                scB = ps_sc[:, half * 1024 + 512:half * 1024 + 1024]
                wsc = [kv_rdy]
                if kt >= 2:
                    wsc.append((s_msk, s_msk.n - 1))
                ctx.step("tensor",
                         lambda e, scA=scA, scB=scB, kt=kt, qc=qc: (
                             e.matmul(out=scA,
                                      lhsT=kh[0][:, kt * 128:(kt + 1) * 128],
                                      rhs=rh[0][:, qc * 512:(qc + 1) * 512],
                                      start=True, stop=True),
                             e.matmul(out=scB,
                                      lhsT=kh[1][:, kt * 128:(kt + 1) * 128],
                                      rhs=rh[1][:, qc * 512:(qc + 1) * 512],
                                      start=True, stop=True))[-1],
                         waits=wsc, inc=s_mmS)
                mmv = s_mmS.now
                pth = pt[:, half * 1024:(half + 1) * 1024]
                ctx.step("scalar",
                         lambda e, pth=pth, half=half: e.activation(
                             out=pth, in_=ps_sc[:, half * 1024:
                                             (half + 1) * 1024],
                             func=AF.Exp),
                         waits=[mmv, (s_av, max(0, s_av.n - 1)),
                                (s_lmm, max(0, s_lmm.n - 1))], inc=s_exp)
                expv = s_exp.now
                if kt >= 4 * qc:
                    ctx.step("gpsimd",
                             lambda e, pth=pth, qc=qc, kt=kt: (
                                 e.affine_select(
                                     out=pth[:, 0:512], in_=pth[:, 0:512],
                                     pattern=[[1, 512]],
                                     base=qc * 512 - kt * 128,
                                     channel_multiplier=-1,
                                     compare_op=ALU.is_ge, fill=0.0),
                                 e.affine_select(
                                     out=pth[:, 512:1024], in_=pth[:, 512:1024],
                                     pattern=[[1, 512]],
                                     base=qc * 512 - kt * 128,
                                     channel_multiplier=-1,
                                     compare_op=ALU.is_ge, fill=0.0))[-1],
                             waits=[expv], inc=s_msk)
                else:
                    ctx.step("gpsimd", lambda e: e.engine_nop(), waits=[expv],
                             inc=s_msk)
                mskv = s_msk.now
                ctx.step("tensor",
                         lambda e, ptA=pth[:, 0:512], ptB=pth[:, 512:1024],
                         avbank=avbank, kt=kt, nkt=nkt, lbank_A=lbank_A,
                         lbank_B=lbank_B: (
                             e.matmul(out=avbank[0:64, :],
                                      lhsT=vh[0][:, kt * HD:(kt + 1) * HD],
                                      rhs=ptA, start=(kt == 0),
                                      stop=(kt == nkt - 1),
                                      tile_position=(0, 0)),
                             e.matmul(out=avbank[64:128, :],
                                      lhsT=vh[1][:, kt * HD:(kt + 1) * HD],
                                      rhs=ptB, start=(kt == 0),
                                      stop=(kt == nkt - 1),
                                      tile_position=(0, 64)))[-1],
                         waits=[mskv] + ([avfree] if avfree and kt == 0
                                         else []), inc=s_av)
                ctx.step("tensor",
                         lambda e, ptA=pth[:, 0:512], ptB=pth[:, 512:1024],
                         kt=kt, nkt=nkt, lbank_A=lbank_A, lbank_B=lbank_B: (
                             e.matmul(out=lbank_A, lhsT=g.ones_col_bf[:],
                                      rhs=ptA, start=(kt == 0),
                                      stop=(kt == nkt - 1)),
                             e.matmul(out=lbank_B, lhsT=g.ones_col_bf[:],
                                      rhs=ptB, start=(kt == 0),
                                      stop=(kt == nkt - 1)))[-1],
                         waits=([(s_le, s_le.n)] if kt == 0 else []),
                         inc=s_lmm)
            avv = s_av.now
            lmmv = s_lmm.now
            ctx.step("scalar", lambda e: (
                e.copy(out=lr[0:1, 0:512], in_=lbank_A),
                e.copy(out=lr[0:1, 512:1024], in_=lbank_B))[-1],
                waits=[lmmv, (s_rl, s_rl.n)], inc=s_le)
            ctx.step("vector", lambda e: e.reciprocal(
                out=lr[0:1, 1024:2048], in_=lr[0:1, 0:1024]),
                waits=[s_le.now], inc=s_le)
            ctx.dma("sync", lrow_d.rearrange("a b -> (a b)").unsqueeze(0),
                    lr[0:1, 1024:2048], waits=[s_le.now], inc=s_le)
            ctx.dma("sync", rlb[0:64, :],
                    lrow_d[0:1, :].partition_broadcast(64),
                    waits=[(s_le, s_le.n), (s_oe, s_oe.n)], inc=s_rl)
            ctx.dma("sync", rlb[64:128, :],
                    lrow_d[1:2, :].partition_broadcast(64),
                    waits=[(s_le, s_le.n)], inc=s_rl)
            rlv = s_rl.now
            xdst = g.xbf[:, pair * S + qc * 512: pair * S + (qc + 1) * 512]
            ctx.step("vector",
                     lambda e, avbank=avbank, xdst=xdst: e.tensor_tensor(
                         out=xdst, in0=avbank[:], in1=rlb[:], op=ALU.mult),
                     waits=[rlv, avv], inc=s_oe)
            ppV.set_free(bi, s_oe.now)
    phaseB_done = (s_oe, s_oe.n)
    stB.close()

    stC = contextlib.ExitStack()
    wsC = WStream(ctx, g.wcol[0:2], first_wait=[phaseB_done])
    ppC = PsumPool(ctx, u + "_psC", 2, (128, TC), stack=stC)
    s_mmC = ctx.sem(su + "_mmC")
    ep.set_io(io["x_in"], io["x_out"])
    for tc in range(NTC):
        for dc in range(ND):
            res_info = ep.load_res(tc, dc)
            buf, rdy, idx = wsC.fetch(W["WoT"][:, dc * 128:(dc + 1) * 128], D,
                                      True)
            bi, bank, bfree = ppC.acquire()
            def mmC(e, buf=buf, tc=tc, bank=bank):
                for c in range(ND):
                    ins = e.matmul(
                        out=bank[:], lhsT=buf[:, c * 128:(c + 1) * 128],
                        rhs=g.xbf[:, c * S + tc * TC:c * S + (tc + 1) * TC],
                        start=(c == 0), stop=(c == ND - 1))
                return ins
            ctx.step("tensor", mmC, waits=[rdy] + ([bfree] if bfree else []),
                     inc=s_mmC)
            wsC.release(idx, s_mmC.now)
            bcol = bias_sb[:, 3 * ND + dc:3 * ND + dc + 1] if cfg["qkv_bias"] \
                else None
            ev = ep.consume(tc, dc, bank[:], bcol, [s_mmC.now, c_rdy],
                            res_info, scale=cfg["oscale"])
            ppC.set_free(bi, ev)
        ep.finish(tc)
    stC.close()
    return ep.block_done()


def _col(v):
    return np.ascontiguousarray(v.reshape(-1, 128).T)


def _cT(a):
    return np.ascontiguousarray(a.T)


def _declare_block(ctx, bi, inputs):
    P = f"b{bi}_"
    if bi == 0:
        W = {}
        for nm in ("W1", "W2", "Wz", "Wh", "Wo"):
            W[nm] = ctx.din(P + nm, inputs[f"L0_attn_{nm}"], F32)
        for nm in ("b1", "b2", "bz", "bh", "bo"):
            W[nm] = ctx.din(P + nm, _col(inputs[f"L0_attn_{nm}"]), F32)
        return lambda c, g, e, io: emit_l0_attn(c, g, e, io, W)
    if bi == 1:
        W = dict(weight=ctx.din(P + "w", inputs["L0_ffn_weight"], F32),
                 bias=ctx.din(P + "b", _col(inputs["L0_ffn_bias"]), F32))
        return lambda c, g, e, io: emit_l0_ffn(c, g, e, io, W)
    if bi == 2:
        ac = attn_host_consts(False)
        W = dict(
            WqT=ctx.din(P + "wq", _cT(inputs["L1_attn_W_query"]), F32),
            WkT=ctx.din(P + "wk", _cT(inputs["L1_attn_W_key"]), F32),
            WvT=ctx.din(P + "wv", _cT(inputs["L1_attn_W_value"]), F32),
            WoT=ctx.din(P + "wo", _cT(inputs["L1_attn_W_proj"]), F32),
            ind=ctx.din(P + "ind", ac["ind"], BF16),
            ones_row=ctx.din(P + "or", np.ones((1, S), dtype=BF), BF16),
        )
        cfg = dict(name=P + "a", kscale=2.0 ** -13, vscale=1.0 / 32.0,
                   oscale=1.0 / 32.0, qkv_bias=False, alibi=False)
        return lambda c, g, e, io: emit_attn(c, g, e, io, W, cfg)
    if bi == 3:
        W = dict(
            ln_gamma=ctx.din(P + "g", _col(inputs["L1_ffn_ln_gamma"]), F32),
            ln_beta=ctx.din(P + "be", _col(inputs["L1_ffn_ln_beta"]), F32),
            fc1_q=ctx.din(P + "f1q", inputs["L1_ffn_fc1_q"], I32),
            fc1_scales=ctx.din(P + "f1s", np.ascontiguousarray(
                np.repeat(inputs["L1_ffn_fc1_scales"], 128, axis=0)), F32),
            fc1_bias=ctx.din(P + "f1b", _col(inputs["L1_ffn_fc1_bias"]), F32),
            fc2_q=ctx.din(P + "f2q", inputs["L1_ffn_fc2_q"], I32),
            fc2_scales=ctx.din(P + "f2s", np.ascontiguousarray(
                np.repeat(inputs["L1_ffn_fc2_scales"], 128, axis=0)), F32),
            fc2_bias=ctx.din(P + "f2b", _col(inputs["L1_ffn_fc2_bias"]), F32),
        )
        return lambda c, g, e, io: emit_l1_ffn(c, g, e, io, W)
    if bi == 4:
        W = dict(
            W_inT=ctx.din(P + "wi", _cT(inputs["L2_attn_W_in"]), F32),
            W_outT=ctx.din(P + "wo", _cT(inputs["L2_attn_W_out"]), F32),
            log_A=ctx.din(P + "la", inputs["L2_attn_log_A"].reshape(64, 1), F32),
            B_param=ctx.din(P + "bp", inputs["L2_attn_B_param"].reshape(64, 1),
                            F32),
            C_param=ctx.din(P + "cp", inputs["L2_attn_C_param"].reshape(64, 1),
                            F32),
            log_dt=ctx.din(P + "ld", inputs["L2_attn_log_dt"].reshape(1, 1),
                           F32),
        )
        return lambda c, g, e, io: emit_l2_attn(c, g, e, io, W)
    if bi == 5:
        co = l2ffn_coeffs(inputs["L2_ffn_up_weights"])
        cd = l2ffn_coeffs(inputs["L2_ffn_down_weights"])
        aug = np.ascontiguousarray(np.concatenate(
            [_cT(inputs["L2_ffn_W_exp"]),
             np.asarray(inputs["L2_ffn_b_exp"]).reshape(-1, 1)], axis=1))
        W = dict(
            WexpT_aug=ctx.din(P + "wea", aug, F32),
            W_con=ctx.din(P + "wc", inputs["L2_ffn_W_con"], F32),
            masks=ctx.din(P + "msk", l2ffn_masks(), BF16),
            up_bias=ctx.din(P + "ub", _col(inputs["L2_ffn_up_bias"]), F32),
            b_con=ctx.din(P + "bc", _col(inputs["L2_ffn_b_con"]), F32),
            down_bias=ctx.din(P + "db", _col(inputs["L2_ffn_down_bias"]), F32),
        )
        for pre, cc in (("up", co), ("dn", cd)):
            for k in ("lowD", "lowO", "lowOT", "highD", "highO", "highOT"):
                if pre == "up" and k == "highOT":
                    continue
                if pre == "dn" and k == "highO":
                    continue
                W[f"{pre}_{k}"] = ctx.din(P + pre + k, cc[k], F32)
        return lambda c, g, e, io: emit_l2_ffn(c, g, e, io, W)
    if bi == 6:
        ac = attn_host_consts(True)
        W = dict(
            WqT=ctx.din(P + "wq", _cT(inputs["L3_attn_Wq"]), F32),
            WkT=ctx.din(P + "wk", _cT(inputs["L3_attn_Wk"]), F32),
            WvT=ctx.din(P + "wv", _cT(inputs["L3_attn_Wv"]), F32),
            WoT=ctx.din(P + "wo", _cT(inputs["L3_attn_Wo"]), F32),
            bq=ctx.din(P + "bq", _col(inputs["L3_attn_bq"]), F32),
            bk=ctx.din(P + "bk", _col(inputs["L3_attn_bk"]), F32),
            bv=ctx.din(P + "bv", _col(inputs["L3_attn_bv"]), F32),
            bo=ctx.din(P + "bo", _col(inputs["L3_attn_bo"]), F32),
            ind=ctx.din(P + "ind", ac["ind"], BF16),
            ones_row=ctx.din(P + "or", np.ones((1, S), dtype=BF), BF16),
            alibi_k=ctx.din(P + "ak", ac["alibi_k"], BF16),
            alibi_k_lo=ctx.din(P + "akl", ac["alibi_k_lo"], BF16),
            alibi_q=ctx.din(P + "aq", ac["alibi_q"], F32),
        )
        cfg = dict(name=P + "a", kscale=1.0 / 8.0, vscale=1.0, oscale=None,
                   qkv_bias=True, alibi=True)
        return lambda c, g, e, io: emit_attn(c, g, e, io, W, cfg)
    if bi == 7:
        W = dict(
            wgT=ctx.din(P + "wg", _cT(inputs["L3_ffn_w_gate"]), I32),
            wuT=ctx.din(P + "wu", _cT(inputs["L3_ffn_w_up"]), I32),
            wdT=ctx.din(P + "wd", _cT(inputs["L3_ffn_w_down"]), I32),
        )
        return lambda c, g, e, io: emit_l3_ffn(c, g, e, io, W)
    raise ValueError(bi)


FUSED = True


def _build_block_program(bi, inputs):
    ctx = Ctx()
    g = Globals(ctx)
    ep = LNEpilogue(ctx, g)
    x_in = ctx.din("x_in", np.zeros((D, S), np.float32), F32)
    x_out = ctx.dout("x_out", (D, S), F32)
    emit = _declare_block(ctx, bi, inputs)
    ctx.begin_block_scope()
    s_x = emit_load_xbf(ctx, g, x_in)
    done = emit(ctx, g, ep, dict(x_in=x_in, x_out=x_out, xbf_ready=s_x.now,
                                 prev_done=[s_x.now]))
    nc = ctx.finalize(final_waits=done)
    return nc, ctx.host


def _build_fused_program(inputs):
    ctx = Ctx()
    g = Globals(ctx)
    ep = LNEpilogue(ctx, g)
    x_in = ctx.din("x_in", np.zeros((D, S), np.float32), F32)
    x_out = ctx.dout("x_out", (D, S), F32)
    xm = [ctx.dscratch(f"xmid{i}", (D, S), F32) for i in range(7)]
    chain = [x_in] + xm + [x_out]
    emits = [_declare_block(ctx, bi, inputs) for bi in range(8)]
    s_x = emit_load_xbf(ctx, g, x_in)
    done = [s_x.now]
    xbf_ready = s_x.now
    for bi in range(8):
        ctx.begin_block_scope()
        done = emits[bi](ctx, g, ep, dict(
            x_in=chain[bi], x_out=chain[bi + 1], xbf_ready=xbf_ready,
            prev_done=done))
        xbf_ready = done[1]
    nc = ctx.finalize(final_waits=done)
    return nc, ctx.host


def kernel(**inputs):
    inputs = {k: np.asarray(v) for k, v in inputs.items()}
    xs = [np.ascontiguousarray(inputs["L0_attn_x"][b].T) for b in range(B)]
    if FUSED:
        nc, host = _build_fused_program(inputs)
        base = dict(host)
        in_maps = [dict(base, x_in=xs[c]) for c in range(N_CORES)]
        res = run_bass_kernel_spmd(nc, in_maps, core_ids=list(range(N_CORES)))
        outs = [np.asarray(res.results[c]["x_out"]) for c in range(N_CORES)]
    else:
        cur = xs
        for bi in range(8):
            nc, host = _build_block_program(bi, inputs)
            base = dict(host)
            in_maps = [dict(base, x_in=np.ascontiguousarray(cur[c]))
                       for c in range(N_CORES)]
            res = run_bass_kernel_spmd(nc, in_maps,
                                       core_ids=list(range(N_CORES)))
            cur = [np.asarray(res.results[c]["x_out"])
                   for c in range(N_CORES)]
        outs = cur
    return np.ascontiguousarray(
        np.stack([o.T for o in outs], axis=0)).astype(np.float32)


# version 06
# speedup vs baseline: 1.0097x; 1.0097x over previous
import contextlib
import math
import numpy as np
import ml_dtypes
import concourse.bass as bass
import concourse.mybir as mybir
from concourse.bass_utils import run_bass_kernel_spmd

F32 = mybir.dt.float32
BF16 = mybir.dt.bfloat16
I32 = mybir.dt.int32
AF = mybir.ActivationFunctionType
ALU = mybir.AluOpType
AX = mybir.AxisListType.X

B, S, D = 8, 4096, 1024
ND = D // 128
EXP = 4096
NE = EXP // 128
TC = 512
NTC = S // TC
BF = ml_dtypes.bfloat16
H = 16
HD = 64
NT = S // 128

N_CORES = 8


class Sem:
    __slots__ = ("h", "n")

    def __init__(self, stack, nc, name):
        self.h = stack.enter_context(nc.semaphore(name))
        self.n = 0

    def bump(self, k=1):
        self.n += k
        return self.n

    @property
    def now(self):
        return (self, self.n)


class Ctx:
    def __init__(self):
        self.nc = bass.Bass("TRN2", target_bir_lowering=False, debug=False)
        self.stack = contextlib.ExitStack()
        self.q = {"sync": [], "vector": [], "scalar": [], "tensor": [], "gpsimd": []}
        self.sems = {}
        self.host = {}
        self._uid = 0

    def uid(self):
        self._uid += 1
        return self._uid

    def begin_block_scope(self):
        self._blk_ctr = 0

    _BLK_PREFIX = ("l0a", "l0f", "l1f", "l2a", "l2f", "l3f", "at_")

    def sem(self, name):
        if name.startswith(self._BLK_PREFIX):
            if name not in getattr(self, "_alias", {}):
                if not hasattr(self, "_alias"):
                    self._alias = {}
                self._alias[name] = f"blk{self._blk_ctr}"
                self._blk_ctr += 1
            name = self._alias[name]
        if name not in self.sems:
            self.sems[name] = Sem(self.stack, self.nc, name)
        return self.sems[name]

    def din(self, name, arr, dt):
        self.host[name] = np.ascontiguousarray(arr)
        return self.nc.dram_tensor(name, list(arr.shape), dt,
                                   kind="ExternalInput").ap()

    def dout(self, name, shape, dt):
        return self.nc.dram_tensor(name, list(shape), dt, kind="ExternalOutput").ap()

    def dscratch(self, name, shape, dt):
        return self.nc.dram_tensor(name, list(shape), dt, kind="Internal").ap()

    def sbuf(self, name, shape, dt, stack=None):
        cm = self.nc.sbuf_tensor(name, list(shape), dt)
        return (stack or self.stack).enter_context(cm)

    def psum(self, name, shape, dt=F32, stack=None):
        cm = self.nc.psum_tensor(name, list(shape), dt)
        return (stack or self.stack).enter_context(cm)

    def step(self, eng, fn, waits=(), inc=None, inck=1):
        wlist = [(s.h, v) for (s, v) in waits if v > 0]
        semh = None
        if inc is not None:
            inc.bump(inck)
            semh = inc.h

        def go(e, wlist=wlist, fn=fn, semh=semh, inck=inck):
            for h_, v in wlist:
                e.wait_ge(h_, v)
            ins = fn(e)
            if semh is not None:
                ins.then_inc(semh, inck)
        self.q[eng].append(go)

    def dma(self, queue, dst, src, waits=(), inc=None):
        w = list(waits)
        if inc is not None and inc.n > 0:
            w.append((inc, inc.n))
        self.step(queue, lambda e: e.dma_start(out=dst, in_=src),
                  waits=w, inc=inc, inck=16)

    def finalize(self, final_waits=()):
        for s, v in final_waits:
            self.q["sync"].append(lambda e, h=s.h, v=v: e.wait_ge(h, v))
        nc = self.nc
        with nc.Block() as block:
            for name, starter in (("sync", block.sync), ("vector", block.vector),
                                  ("scalar", block.scalar), ("tensor", block.tensor),
                                  ("gpsimd", block.gpsimd)):
                fns = self.q[name]
                if fns:
                    starter(lambda e, fns=fns: [f(e) for f in fns])
        self.stack.close()
        return nc


class PsumPool:

    def __init__(self, ctx, name, n, shape, stack=None):
        self.banks = [ctx.psum(f"{name}{i}", shape, stack=stack) for i in range(n)]
        self.free = [None] * n
        self.rr = 0
        self.n = n

    def acquire(self):
        idx = self.rr % self.n
        self.rr += 1
        return idx, self.banks[idx], self.free[idx]

    def set_free(self, idx, semval):
        self.free[idx] = semval


class WStream:

    def __init__(self, ctx, bufs, first_wait=(), name="ws_ld_shared"):
        self.ctx = ctx
        self.bufs = bufs
        self.nbuf = len(bufs)
        self.s_ld = ctx.sem(name)
        self.free_at = [None] * self.nbuf
        self.first_wait = list(first_wait)
        self.i = 0

    def fetch(self, src_ap, rows, cast, extra_waits=()):
        ctx = self.ctx
        idx = self.i % self.nbuf
        kc = rows // 128
        cols = src_ap.shape[1]
        buf = self.bufs[idx]
        waits = list(extra_waits)
        if self.i < self.nbuf:
            waits += self.first_wait
        if self.free_at[idx] is not None:
            waits.append(self.free_at[idx])
        self.i += 1
        src3 = src_ap.rearrange("(kc p) gcol -> p kc gcol", p=128)
        q = "gpsimd" if cast else "sync"
        ctx.dma(q, buf[:, 0:kc * cols].rearrange("p (kc c) -> p kc c", kc=kc),
                src3, waits=waits, inc=self.s_ld)
        return buf, self.s_ld.now, idx

    def release(self, idx, sem_val):
        self.free_at[idx] = sem_val


class Globals:
    def __init__(self, ctx: Ctx):
        self.ctx = ctx
        self.xbf = ctx.sbuf("XBF", (128, ND * S), BF16)
        self.wcol = [ctx.sbuf(f"WCOL{i}", (128, 4096), BF16) for i in range(3)]
        self.ones_col_bf = ctx.sbuf("ones_col_bf", (128, 1), BF16)
        self.ones_row_f = ctx.sbuf("ones_row_f", (1, 128), F32)
        self.eps_col = ctx.sbuf("eps_col", (1, 1), F32)
        self.ident_bf = ctx.sbuf("ident_bf", (128, 128), BF16)
        self.ident_f32 = ctx.sbuf("ident_f32", (128, 128), F32)
        idb = ctx.din("g_ident_bf", np.eye(128, dtype=BF), BF16)
        idf = ctx.din("g_ident_f32", np.eye(128, dtype=np.float32), F32)
        si = ctx.sem("g_ident")
        ctx.dma("sync", self.ident_bf[:], idb[:], inc=si)
        ctx.dma("sync", self.ident_f32[:], idf[:], inc=si)
        self.ident_done = si.now
        s = ctx.sem("g_init")
        ctx.step("vector", lambda e: e.memset(self.ones_col_bf[:], 1.0))
        ctx.step("vector", lambda e: e.memset(self.ones_row_f[:], 1.0))
        ctx.step("vector", lambda e: e.memset(self.eps_col[:], 1e-5), inc=s)
        self.init_done = s.now


def emit_load_xbf(ctx, g, x_src_fp32):
    u = ctx.uid()
    s_ld = ctx.sem(f"xbf_ld_{u}")
    ctx.dma("gpsimd",
            g.xbf[:].rearrange("p (c t) -> p c t", c=ND),
            x_src_fp32.rearrange("(c p) t -> p c t", p=128),
            inc=s_ld)
    return s_ld


class LNEpilogue:
    def __init__(self, ctx: Ctx, g):
        self.ctx, self.g = ctx, g
        self.xr = ctx.sbuf("ep_xr", (128, ND * TC), F32)
        self.res = ctx.sbuf("ep_res", (128, 4 * TC), F32)
        self.sh = ctx.sbuf("ep_sh", (128, 2 * TC), BF16)
        self.sq = ctx.sbuf("ep_sq", (128, 2 * TC), BF16)
        self.rows = ctx.sbuf("ep_rows", (1, 3 * TC), F32)
        self.bct = ctx.sbuf("ep_bct", (128, 2 * TC), F32)
        self.tmp = ctx.sbuf("ep_tmp", (128, TC), F32)
        self.ps_stat = ctx.psum("ep_ps_stat", (1, 2 * TC))
        self.rows_d = ctx.dscratch("ep_rows_d", (1, 2 * TC), F32)
        self.s_res = ctx.sem("ep_res")
        self.s_ev = ctx.sem("ep_ev")
        self.s_sq = ctx.sem("ep_sq")
        self.s_stp = ctx.sem("ep_stp")
        self.s_row = ctx.sem("ep_row")
        self.s_bc = ctx.sem("ep_bc")
        self.s_ap = ctx.sem("ep_ap")
        self.s_tmp = ctx.sem("ep_tmp")
        self.s_out = ctx.sem("ep_out")
        self.s_xbf = ctx.sem("ep_xbf")
        self.gi = 0
        self.x_in = None
        self.x_out = None
        self._pending = None
        self.debug_dram = None
        self.debug_dram2 = None

    def set_io(self, x_in_dram, x_out_dram):
        self.x_in, self.x_out = x_in_dram, x_out_dram

    def load_res(self, tc, dc):
        ctx = self.ctx
        k = self.gi * ND + dc
        slot = k % 4
        dst = self.res[:, slot * TC:(slot + 1) * TC]
        waits = [(self.s_ev, k - 3)] if k >= 4 else []
        ctx.dma("scalar", dst,
                self.x_in[dc * 128:(dc + 1) * 128, tc * TC:(tc + 1) * TC],
                waits=waits, inc=self.s_res)
        return dst, self.s_res.now

    def consume(self, tc, dc, psum_ap, bias_col, waits, res_info, scale=None):
        ctx, g = self.ctx, self.g
        gi = self.gi
        res, res_rdy = res_info
        xr = self.xr[:, dc * TC:(dc + 1) * TC]
        w = list(waits) + [res_rdy]
        if dc == 0 and gi >= 1:
            w.append((self.s_out, 16 * ND * gi))
            w.append((self.s_xbf, ND * gi))
        if scale is not None:
            ctx.step("vector",
                     lambda e: e.scalar_tensor_tensor(
                         out=xr, in0=psum_ap, scalar=scale, in1=res,
                         op0=ALU.mult, op1=ALU.add), waits=w,
                     inc=(None if bias_col is not None else self.s_ev))
            if bias_col is not None:
                ctx.step("vector",
                         lambda e: e.tensor_scalar(
                             out=xr, in0=xr, scalar1=bias_col, scalar2=None,
                             op0=ALU.add), inc=self.s_ev)
        else:
            bc = bias_col if bias_col is not None else 0.0
            ctx.step("vector",
                     lambda e: e.scalar_tensor_tensor(
                         out=xr, in0=psum_ap, scalar=bc, in1=res,
                         op0=ALU.add, op1=ALU.add), waits=w, inc=self.s_ev)
        ev = self.s_ev.now
        if self.debug_dram is not None:
            ctx.dma("gpsimd",
                    self.debug_dram[dc * 128:(dc + 1) * 128,
                                    tc * TC:(tc + 1) * TC],
                    xr, waits=[ev], inc=self.s_res)
        k = gi * ND + dc
        slot = k % 2
        sh = self.sh[:, slot * TC:(slot + 1) * TC]
        sq = self.sq[:, slot * TC:(slot + 1) * TC]
        wsh = [ev] + ([(self.s_stp, k - 1)] if k >= 2 else [])
        ctx.step("gpsimd", lambda e: e.tensor_copy(out=sh, in_=xr),
                 waits=wsh, inc=self.s_sq)
        ctx.step("scalar", lambda e: e.activation(out=sq, in_=xr, func=AF.Square),
                 waits=wsh, inc=self.s_sq)
        sqv = self.s_sq.now
        w2 = [sqv]
        if dc == 0 and gi >= 1:
            w2.append((self.s_row, 3 * gi))
        self.flush_stats()
        self._pending = (sh, sq, dc, w2)
        return ev

    def flush_stats(self):
        if self._pending is None:
            return
        sh, sq, dc, w2 = self._pending
        self._pending = None
        ctx, g = self.ctx, self.g
        ctx.step("tensor",
                 lambda e, sh=sh, sq=sq, dc=dc: (
                     e.matmul(out=self.ps_stat[:, 0:TC], lhsT=g.ones_col_bf[:],
                              rhs=sh, start=(dc == 0), stop=(dc == ND - 1)),
                     e.matmul(out=self.ps_stat[:, TC:2 * TC],
                              lhsT=g.ones_col_bf[:], rhs=sq,
                              start=(dc == 0), stop=(dc == ND - 1)))[-1],
                 waits=w2, inc=self.s_stp)

    def finish(self, tc):
        self.flush_stats()
        ctx, g = self.ctx, self.g
        gi = self.gi
        st = (self.s_stp, ND * (gi + 1))
        r_rs = self.rows[:, 0:TC]
        r_m = self.rows[:, TC:2 * TC]
        r_v = self.rows[:, 2 * TC:3 * TC]
        ctx.step("vector", lambda e: e.tensor_scalar(
            out=r_m, in0=self.ps_stat[:, 0:TC], scalar1=1.0 / D, scalar2=None,
            op0=ALU.mult), waits=[st, (self.s_bc, max(0, 32 * gi - 16))])
        ctx.step("vector", lambda e: e.scalar_tensor_tensor(
            out=r_v, in0=r_m, scalar=-1.0, in1=r_m, op0=ALU.mult, op1=ALU.mult))
        ctx.step("vector", lambda e: e.scalar_tensor_tensor(
            out=r_v, in0=self.ps_stat[:, TC:2 * TC], scalar=1.0 / D, in1=r_v,
            op0=ALU.mult, op1=ALU.add), inc=self.s_row)
        rv = self.s_row.now
        ctx.step("scalar", lambda e: e.activation(
            out=r_rs, in_=r_v, func=AF.Ln, bias=g.eps_col[:]),
            waits=[rv, g.init_done])
        ctx.step("scalar", lambda e: e.activation(
            out=r_rs, in_=r_rs, func=AF.Exp, scale=-0.5), inc=self.s_row)
        rs = self.s_row.now
        ctx.step("vector", lambda e: e.tensor_scalar(
            out=r_m, in0=r_m, scalar1=-1.0, scalar2=None, op0=ALU.mult),
            waits=[rs], inc=self.s_row)
        rneg = self.s_row.now
        ctx.dma("sync", self.rows_d[:], self.rows[:, 0:2 * TC],
                waits=[rneg, (self.s_bc, 32 * gi)], inc=self.s_bc)
        ctx.dma("sync", self.bct[:], self.rows_d.partition_broadcast(128),
                waits=[(self.s_bc, 32 * gi + 16), (self.s_ap, ND * gi)],
                inc=self.s_bc)
        bcv = (self.s_bc, 32 * (gi + 1))
        if self.debug_dram2 is not None:
            ctx.dma("gpsimd", self.debug_dram2[:, gi * 2 * TC:(gi + 1) * 2 * TC],
                    self.bct[:], waits=[bcv], inc=self.s_res)
        rstd_b = self.bct[:, 0:TC]
        negm_b = self.bct[:, TC:2 * TC]
        for dc in range(ND):
            xr = self.xr[:, dc * TC:(dc + 1) * TC]
            ctx.step("gpsimd",
                     lambda e, xr=xr: e.tensor_add(self.tmp[:], xr, negm_b),
                     waits=[bcv, (self.s_ap, gi * ND + dc)], inc=self.s_tmp)
            ctx.step("vector",
                     lambda e, xr=xr: e.tensor_mul(xr, self.tmp[:], rstd_b),
                     waits=[self.s_tmp.now], inc=self.s_ap)
            apv = self.s_ap.now
            xbf_dst = g.xbf[:, dc * S + tc * TC: dc * S + (tc + 1) * TC]
            ctx.step("scalar",
                     lambda e, xr=xr, xbf_dst=xbf_dst: e.copy(out=xbf_dst, in_=xr),
                     waits=[apv], inc=self.s_xbf)
            ctx.dma("sync",
                    self.x_out[dc * 128:(dc + 1) * 128, tc * TC:(tc + 1) * TC],
                    xr, waits=[apv], inc=self.s_out)
        self.gi += 1

    def block_done(self):
        return [(self.s_out, 16 * ND * self.gi), (self.s_xbf, ND * self.gi)]


def emit_precast(ctx, src_fp32, dst_bf16, rows, cols, bufs, s_done,
                 first_wait=()):
    n = rows // 128
    s_in = ctx.sem("pc_in")
    s_out = ctx.sem("pc_out")
    b_out = s_out.n
    for i in range(n):
        waits = [(s_out, b_out + 16 * (i - 1))] if i >= 2 else list(first_wait)
        ctx.dma("gpsimd", bufs[i % 2][:, 0:cols],
                src_fp32[i * 128:(i + 1) * 128, :], waits=waits, inc=s_in)
        ctx.dma("scalar", dst_bf16[i * 128:(i + 1) * 128, :],
                bufs[i % 2][:, 0:cols], waits=[s_in.now], inc=s_out)
    ctx.step("sync", lambda e: e.nop(), waits=[(s_out, b_out + 16 * n)],
             inc=s_done)
    return s_done.now


def emit_l0_attn(ctx, g, ep, io, W):
    u = "l0a"
    xbf_ready = io["xbf_ready"]
    xin_d = ctx.dscratch(u + "_xin", (EXP, S), BF16)
    xsk_d = ctx.dscratch(u + "_xsk", (EXP, S), BF16)
    a_d = ctx.dscratch(u + "_a", (EXP, S), F32)
    b_d = ctx.dscratch(u + "_b", (EXP, S), BF16)
    s_d = ctx.dscratch(u + "_s", (EXP, S), BF16)
    wzbf_d = ctx.dscratch(u + "_wzbf", (EXP, EXP), BF16)
    whbf_d = ctx.dscratch(u + "_whbf", (EXP, EXP), BF16)
    wobf_d = ctx.dscratch(u + "_wobf", (EXP, D), BF16)

    bias = ctx.sbuf(u + "_bias", (128, 4 * NE + ND), F32)
    s_bias = ctx.sem(u + "_bias")
    for i, bname in enumerate(("b1", "b2", "bz", "bh")):
        ctx.dma("sync", bias[:, i * NE:(i + 1) * NE], W[bname][:],
                waits=(list(io["prev_done"]) if i == 0 else []), inc=s_bias)
    ctx.dma("sync", bias[:, 4 * NE:4 * NE + ND], W["bo"][:], inc=s_bias)
    bias_ld = s_bias.now

    stA = contextlib.ExitStack()
    pcb = [ctx.sbuf(u + f"_pcb{i}", (128, 4096), BF16, stack=stA) for i in range(2)]
    s_pc = ctx.sem(u + "_pc")
    d1_ = emit_precast(ctx, W["Wz"], wzbf_d, EXP, EXP, pcb, s_pc)
    d2_ = emit_precast(ctx, W["Wh"], whbf_d, EXP, EXP, pcb, s_pc,
                       first_wait=[d1_])
    emit_precast(ctx, W["Wo"], wobf_d, EXP, D, pcb, s_pc, first_wait=[d2_])
    pc_done = s_pc.now

    ws = WStream(ctx, g.wcol[0:2], first_wait=[xbf_ready], name=u + "_wsA")
    s_mmA = ctx.sem(u + "_mmA")
    s_evA = ctx.sem(u + "_evA")
    s_oA = ctx.sem(u + "_oA")
    ppA = PsumPool(ctx, u + "_psA", 4, (128, 512), stack=stA)
    stg = [ctx.sbuf(u + f"_stgA{i}", (128, S), BF16, stack=stA) for i in range(2)]
    for eo in range(NE):
        stg_free = (s_oA, s_oA.n)
        for wi, wname in enumerate(("W1", "W2")):
            buf, rdy, idx = ws.fetch(W[wname][:, eo * 128:(eo + 1) * 128], D, True)
            last_mm = None
            for nu in range(8):
                bi, bank, bfree = ppA.acquire()
                def mm(e, buf=buf, nu=nu, bank=bank):
                    for c in range(ND):
                        ins = e.matmul(
                            out=bank[:], lhsT=buf[:, c * 128:(c + 1) * 128],
                            rhs=g.xbf[:, c * S + nu * 512: c * S + (nu + 1) * 512],
                            start=(c == 0), stop=(c == ND - 1))
                    return ins
                waits = [rdy] + ([bfree] if bfree else [])
                ctx.step("tensor", mm, waits=waits, inc=s_mmA)
                mmv = s_mmA.now
                last_mm = mmv
                bcol = bias[:, wi * NE + eo: wi * NE + eo + 1]
                dst = stg[wi][:, nu * 512:(nu + 1) * 512]
                ctx.step("scalar",
                         lambda e, bank=bank, dst=dst, bcol=bcol:
                         e.activation(out=dst, in_=bank[:], func=AF.Silu,
                                      bias=bcol),
                         waits=[mmv, bias_ld, stg_free], inc=s_evA)
                ppA.set_free(bi, s_evA.now)
            ws.release(idx, last_mm)
        evA = s_evA.now
        ctx.dma("sync", xin_d[eo * 128:(eo + 1) * 128, :], stg[0][:],
                waits=[evA], inc=s_oA)
        ctx.dma("sync", xsk_d[eo * 128:(eo + 1) * 128, :], stg[1][:],
                waits=[evA], inc=s_oA)
    phaseA_done = (s_oA, s_oA.n)
    stA.close()

    stB = contextlib.ExitStack()
    wsB = WStream(ctx, g.wcol[0:3], first_wait=[phaseA_done, pc_done], name=u + "_wsB")
    s_xinB = ctx.sem(u + "_xinB")
    s_mmB = ctx.sem(u + "_mmB")
    s_evB = ctx.sem(u + "_evB")
    s_abB = ctx.sem(u + "_abB")
    s_oB = ctx.sem(u + "_oB")
    ppZ = PsumPool(ctx, u + "_psZ", 2, (128, 512), stack=stB)
    ppH = PsumPool(ctx, u + "_psH", 2, (128, 512), stack=stB)
    zt = ctx.sbuf(u + "_zt", (128, 2 * 512), F32, stack=stB)
    abt = ctx.sbuf(u + "_abt", (128, 4 * 512), F32, stack=stB)
    abtb = ctx.sbuf(u + "_abtb", (128, 2 * 512), BF16, stack=stB)
    TQ = 1024
    for tq in range(S // TQ):
        wx = [phaseA_done] if tq == 0 else [(s_mmB, s_mmB.n)]
        ctx.dma("sync",
                g.xbf[:].rearrange("p (kc t) -> p kc t", kc=NE)[:, :, 0:TQ],
                xin_d.rearrange("(kc p) t -> p kc t", p=128)
                [:, :, tq * TQ:(tq + 1) * TQ],
                waits=wx, inc=s_xinB)
        xin_rdy = s_xinB.now
        for fo in range(NE):
            fetched = []
            for wi, wd in enumerate((wzbf_d, whbf_d)):
                buf, rdy, idx = wsB.fetch(wd[:, fo * 128:(fo + 1) * 128], EXP,
                                          False)
                fetched.append((buf, rdy, idx))
            for half in range(2):
                banks = []
                for wi in range(2):
                    buf, rdy, idx = fetched[wi]
                    pp = ppZ if wi == 0 else ppH
                    bi, bank, bfree = pp.acquire()
                    def mmB(e, buf=buf, half=half, bank=bank):
                        for c in range(NE):
                            ins = e.matmul(
                                out=bank[:],
                                lhsT=buf[:, c * 128:(c + 1) * 128],
                                rhs=g.xbf[:, c * TQ + half * 512:
                                          c * TQ + (half + 1) * 512],
                                start=(c == 0), stop=(c == NE - 1))
                        return ins
                    waits = [rdy, xin_rdy] + ([bfree] if bfree else [])
                    ctx.step("tensor", mmB, waits=waits, inc=s_mmB)
                    banks.append((pp, bi, bank))
                mmv = s_mmB.now
                zcol = bias[:, 2 * NE + fo: 2 * NE + fo + 1]
                hcol = bias[:, 3 * NE + fo: 3 * NE + fo + 1]
                sl = half
                zdst = zt[:, sl * 512:(sl + 1) * 512]
                wab = (s_abB, max(0, s_abB.n - 2))
                ctx.step("scalar",
                         lambda e, b_=banks[0][2], zdst=zdst, zcol=zcol:
                         e.activation(out=zdst, in_=b_[:], func=AF.Sigmoid,
                                      bias=zcol),
                         waits=[mmv, wab], inc=s_evB)
                banks[0][0].set_free(banks[0][1], s_evB.now)
                evv = s_evB.now
                adst = abt[:, (2 * sl) * 512:(2 * sl + 1) * 512]
                bdst = abtb[:, sl * 512:(sl + 1) * 512]
                wo_ = (s_oB, max(0, s_oB.n - 2 * 16))
                ctx.step("vector",
                         lambda e, zdst=zdst, adst=adst:
                         e.tensor_scalar(out=adst, in0=zdst, scalar1=-1.0,
                                         scalar2=1.0, op0=ALU.mult, op1=ALU.add),
                         waits=[evv, wo_], inc=s_abB)
                ctx.step("vector",
                         lambda e, b_=banks[1][2], zdst=zdst, bdst=bdst,
                         hcol=hcol:
                         e.scalar_tensor_tensor(out=bdst, in0=b_[:], scalar=hcol,
                                                in1=zdst, op0=ALU.add,
                                                op1=ALU.mult), inc=s_abB)
                banks[1][0].set_free(banks[1][1], s_abB.now)
                abv = s_abB.now
                t0 = tq * TQ + half * 512
                ctx.dma("scalar", a_d[fo * 128:(fo + 1) * 128, t0:t0 + 512],
                        adst, waits=[abv], inc=s_oB)
                ctx.dma("scalar", b_d[fo * 128:(fo + 1) * 128, t0:t0 + 512],
                        bdst, waits=[abv], inc=s_oB)
            for buf, rdy, idx in fetched:
                wsB.release(idx, s_mmB.now)
    phaseB_done = (s_oB, s_oB.n)
    stB.close()

    stC = contextlib.ExitStack()
    s_ldC = ctx.sem(u + "_ldC")
    s_scC = ctx.sem(u + "_scC")
    s_oC = ctx.sem(u + "_oC")
    at_ = ctx.sbuf(u + "_at", (128, S), F32, stack=stC)
    bt_ = ctx.sbuf(u + "_bt", (128, S), BF16, stack=stC)
    ht_ = ctx.sbuf(u + "_hC", (128, S), F32, stack=stC)
    skt = ctx.sbuf(u + "_skt", (128, S), BF16, stack=stC)
    for fo in range(NE):
        wl = [phaseB_done] if fo == 0 else [(s_scC, s_scC.n)]
        ctx.dma("sync", at_[:], a_d[fo * 128:(fo + 1) * 128, :], waits=wl,
                inc=s_ldC)
        ctx.dma("sync", bt_[:], b_d[fo * 128:(fo + 1) * 128, :], waits=wl,
                inc=s_ldC)
        wl2 = wl + [(s_oC, s_oC.n)]
        ctx.dma("sync", skt[:], xsk_d[fo * 128:(fo + 1) * 128, :], waits=wl2,
                inc=s_ldC)
        ldv = s_ldC.now
        wsc = [ldv] + ([(s_oC, s_oC.n - 16)] if fo >= 1 else [])
        ctx.step("vector",
                 lambda e: e.tensor_tensor_scan(
                     out=ht_[:], data0=at_[:], data1=bt_[:], initial=0.0,
                     op0=ALU.mult, op1=ALU.add),
                 waits=wsc, inc=s_scC)
        ctx.step("vector",
                 lambda e: e.tensor_tensor(out=skt[:], in0=ht_[:], in1=skt[:],
                                           op=ALU.add), inc=s_scC)
        ctx.step("vector", lambda e: e.engine_nop(), inc=s_scC)
        scv = s_scC.now
        ctx.dma("sync", s_d[fo * 128:(fo + 1) * 128, :], skt[:],
                waits=[scv], inc=s_oC)
    phaseC_done = (s_oC, s_oC.n)
    stC.close()

    stD = contextlib.ExitStack()
    wsD = WStream(ctx, g.wcol[0:3], first_wait=[phaseB_done, pc_done], name=u + "_wsD")
    s_sD = ctx.sem(u + "_sD")
    s_mmD = ctx.sem(u + "_mmD")
    sst = ctx.sbuf(u + "_sst", (128, 2 * NE * TC), BF16, stack=stD)
    ppD = PsumPool(ctx, u + "_psD", 2, (128, TC), stack=stD)
    ep.set_io(io["x_in"], io["x_out"])
    for tc in range(NTC):
        rot = tc % 2
        sbuf_tc = sst[:, rot * NE * TC:(rot + 1) * NE * TC]
        wl = [phaseC_done] if tc == 0 else [(s_mmD, max(0, s_mmD.n - ND))]
        ctx.dma("sync", sbuf_tc.rearrange("p (kc t) -> p kc t", kc=NE),
                s_d.rearrange("(kc p) t -> p kc t", p=128)
                [:, :, tc * TC:(tc + 1) * TC], waits=wl, inc=s_sD)
        srdy = s_sD.now
        for dc in range(ND):
            res_info = ep.load_res(tc, dc)
            buf, rdy, idx = wsD.fetch(wobf_d[:, dc * 128:(dc + 1) * 128], EXP,
                                      False)
            bi, bank, bfree = ppD.acquire()
            def mmD(e, buf=buf, bank=bank, sbuf_tc=sbuf_tc):
                for c in range(NE):
                    ins = e.matmul(
                        out=bank[:], lhsT=buf[:, c * 128:(c + 1) * 128],
                        rhs=sbuf_tc[:, c * TC:(c + 1) * TC],
                        start=(c == 0), stop=(c == NE - 1))
                return ins
            waits = [rdy, srdy] + ([bfree] if bfree else [])
            ctx.step("tensor", mmD, waits=waits, inc=s_mmD)
            wsD.release(idx, s_mmD.now)
            ev = ep.consume(tc, dc, bank[:],
                            bias[:, 4 * NE + dc:4 * NE + dc + 1],
                            [s_mmD.now], res_info)
            ppD.set_free(bi, ev)
        ep.finish(tc)
    stD.close()
    return ep.block_done()


def emit_dequant(ctx, q_dram, scales_dram, out_dram, rows, cols, addc, mulc,
                 prev_done, stack, tile_cols=1024):
    u = ctx.uid()
    n = rows // 128
    m = cols // tile_cols
    ib = [ctx.sbuf(f"dq_i{u}_{i}", (128, tile_cols), I32, stack=stack)
          for i in range(2)]
    sb = (ctx.sbuf(f"dq_s{u}", (128, tile_cols), F32, stack=stack)
          if scales_dram is not None else None)
    wb = [ctx.sbuf(f"dq_w{u}_{i}", (128, tile_cols), BF16, stack=stack)
          for i in range(2)]
    tmp_ = ctx.sbuf(f"dq_t{u}", (128, tile_cols), F32, stack=stack)
    s_in = ctx.sem("dq_in")
    s_cp = ctx.sem("dq_cp")
    s_out = ctx.sem("dq_out")
    b_in, b_cp, b_out = s_in.n, s_cp.n, s_out.n
    k = 0
    for i in range(n):
        for j in range(m):
            w = list(prev_done) if k == 0 else []
            if k >= 2:
                w.append((s_cp, b_cp + k - 1))
            ctx.dma("sync", ib[k % 2][:],
                    q_dram[i * 128:(i + 1) * 128,
                           j * tile_cols:(j + 1) * tile_cols],
                    waits=w, inc=s_in)
            if scales_dram is not None:
                ctx.dma("sync", sb[:],
                        scales_dram[i * 128:(i + 1) * 128,
                                    j * tile_cols:(j + 1) * tile_cols],
                        waits=[(s_cp, b_cp + k)] if k >= 1 else list(prev_done),
                        inc=s_in)
                inr = s_in.now
                if mulc != 1.0:
                    ctx.step("vector", lambda e: e.tensor_scalar(
                        out=sb[:], in0=sb[:], scalar1=mulc, scalar2=None,
                        op0=ALU.mult), waits=[inr])
                ctx.step("vector",
                         lambda e, k=k: e.scalar_tensor_tensor(
                             out=tmp_[:], in0=ib[k % 2][:],
                             scalar=float(addc), in1=sb[:], op0=ALU.add,
                             op1=ALU.mult),
                         waits=[inr])
                ctx.step("vector",
                         lambda e, k=k: e.tensor_copy(out=wb[k % 2][:],
                                                      in_=tmp_[:]),
                         waits=[(s_out, b_out + 16 * (k - 1))], inc=s_cp)
            else:
                ctx.step("vector",
                         lambda e, k=k: e.tensor_scalar(
                             out=wb[k % 2][:], in0=ib[k % 2][:],
                             scalar1=float(mulc), scalar2=None, op0=ALU.mult),
                         waits=[s_in.now, (s_out, b_out + 16 * (k - 1))],
                         inc=s_cp)
            ctx.dma("scalar",
                    out_dram[i * 128:(i + 1) * 128,
                             j * tile_cols:(j + 1) * tile_cols],
                    wb[k % 2][:], waits=[s_cp.now], inc=s_out)
            k += 1
    return (s_out, b_out + 16 * n * m)


def emit_l1_ffn(ctx, g, ep, io, W):
    u = "l1f"
    prev = io["prev_done"]
    w1_d = ctx.dscratch(u + "_w1", (D, EXP), BF16)
    w2_d = ctx.dscratch(u + "_w2", (EXP, D), BF16)

    vec = ctx.sbuf(u + "_vec", (128, 2 * ND + NE + ND), F32)
    halfb2 = ctx.sbuf(u + "_hb2", (128, ND), F32)
    s_vec = ctx.sem(u + "_vec")
    ctx.dma("sync", vec[:, 0:ND], W["ln_gamma"][:], waits=list(prev),
            inc=s_vec)
    ctx.dma("sync", vec[:, ND:2 * ND], W["ln_beta"][:], inc=s_vec)
    ctx.dma("sync", vec[:, 2 * ND:2 * ND + NE], W["fc1_bias"][:], inc=s_vec)
    ctx.dma("sync", vec[:, 2 * ND + NE:2 * ND + NE + ND], W["fc2_bias"][:],
            inc=s_vec)
    vr = s_vec.now
    ctx.step("vector", lambda e: e.tensor_scalar(
        out=halfb2[:], in0=vec[:, 2 * ND + NE:2 * ND + NE + ND], scalar1=0.5,
        scalar2=None, op0=ALU.mult), waits=[vr], inc=s_vec)
    vr2 = s_vec.now

    stL = contextlib.ExitStack()
    dq1 = emit_dequant(ctx, W["fc1_q"], W["fc1_scales"], w1_d, D, EXP,
                       -128.0, 1.0, prev, stL)
    dq2 = emit_dequant(ctx, W["fc2_q"], W["fc2_scales"], w2_d, EXP, D,
                       -128.0, 0.5, prev, stL)

    sqb = ctx.sbuf(u + "_sqb", (128, 2 * TC), BF16, stack=stL)
    rowsL = ctx.sbuf(u + "_rows", (1, 3 * TC), F32, stack=stL)
    bctL = ctx.sbuf(u + "_bct", (128, 2 * TC), F32, stack=stL)
    xfL = ctx.sbuf(u + "_xf", (128, 2 * TC), F32, stack=stL)
    tmpL = ctx.sbuf(u + "_tmp", (128, TC), F32, stack=stL)
    psL = ctx.psum(u + "_psL", (1, 2 * TC), stack=stL)
    psLb = ctx.psum(u + "_psLb", (128, 2 * TC), stack=stL)
    s_lsq = ctx.sem(u + "_lsq")
    s_lst = ctx.sem(u + "_lst")
    s_lrow = ctx.sem(u + "_lrow")
    s_lbc = ctx.sem(u + "_lbc")
    s_lxf = ctx.sem(u + "_lxf")
    s_lap = ctx.sem(u + "_lap")
    xbf_ready = io["xbf_ready"]
    for tc in range(NTC):
        for dc in range(ND):
            k = tc * ND + dc
            xb = g.xbf[:, dc * S + tc * TC: dc * S + (tc + 1) * TC]
            wq = [xbf_ready] + ([(s_lst, s_lst.n - 1)] if k >= 2 else [])
            sq = sqb[:, (k % 2) * TC:(k % 2 + 1) * TC]
            ctx.step("scalar", lambda e, xb=xb, sq=sq: e.activation(
                out=sq, in_=xb, func=AF.Square), waits=wq, inc=s_lsq)
            w2_ = [s_lsq.now]
            if dc == 0 and tc >= 1:
                w2_.append((s_lrow, s_lrow.n))
            ctx.step("tensor",
                     lambda e, xb=xb, sq=sq, dc=dc: (
                         e.matmul(out=psL[:, 0:TC], lhsT=g.ones_col_bf[:],
                                  rhs=xb, start=(dc == 0), stop=(dc == ND - 1)),
                         e.matmul(out=psL[:, TC:2 * TC], lhsT=g.ones_col_bf[:],
                                  rhs=sq, start=(dc == 0), stop=(dc == ND - 1))
                     )[-1], waits=w2_, inc=s_lst)
        r_m = rowsL[:, 0:TC]
        r_v = rowsL[:, TC:2 * TC]
        r_rs = rowsL[:, 2 * TC:3 * TC]
        stv = (s_lst, s_lst.n)
        ctx.step("vector", lambda e: e.tensor_scalar(
            out=r_m, in0=psL[:, 0:TC], scalar1=1.0 / D, scalar2=None,
            op0=ALU.mult), waits=[stv])
        ctx.step("vector", lambda e: e.scalar_tensor_tensor(
            out=r_v, in0=r_m, scalar=-1.0, in1=r_m, op0=ALU.mult, op1=ALU.mult))
        ctx.step("vector", lambda e: e.scalar_tensor_tensor(
            out=r_v, in0=psL[:, TC:2 * TC], scalar=1.0 / D, in1=r_v,
            op0=ALU.mult, op1=ALU.add), inc=s_lrow)
        rv = s_lrow.now
        ctx.step("scalar", lambda e: e.activation(
            out=r_rs, in_=r_v, func=AF.Ln, bias=g.eps_col[:]),
            waits=[rv, g.init_done])
        ctx.step("scalar", lambda e: e.activation(
            out=r_rs, in_=r_rs, func=AF.Exp, scale=-0.5), inc=s_lrow)
        rs = s_lrow.now
        ctx.step("vector", lambda e: e.tensor_scalar(
            out=r_m, in0=r_m, scalar1=-1.0, scalar2=None, op0=ALU.mult),
            waits=[rs], inc=s_lrow)
        rneg = s_lrow.now
        ctx.step("tensor", lambda e: (
            e.matmul(out=psLb[:, 0:TC], lhsT=g.ones_row_f[:], rhs=r_rs,
                     start=True, stop=True),
            e.matmul(out=psLb[:, TC:2 * TC], lhsT=g.ones_row_f[:], rhs=r_m,
                     start=True, stop=True))[-1],
            waits=[rneg, (s_lbc, s_lbc.n)], inc=s_lbc)
        ctx.step("scalar", lambda e: e.copy(out=bctL[:], in_=psLb[:]),
                 waits=[s_lbc.now, (s_lap, s_lap.n)], inc=s_lbc)
        bcv = s_lbc.now
        for dc in range(ND):
            k_ = tc * ND + dc
            xf = xfL[:, (k_ % 2) * TC:(k_ % 2 + 1) * TC]
            wld = [(s_lap, s_lap.n - 3)] if k_ >= 2 else []
            ctx.dma("scalar", xf,
                    io["x_in"][dc * 128:(dc + 1) * 128, tc * TC:(tc + 1) * TC],
                    waits=wld, inc=s_lxf)
            ctx.step("gpsimd",
                     lambda e, xf=xf: e.tensor_add(tmpL[:], xf,
                                                   bctL[:, TC:2 * TC]),
                     waits=[s_lxf.now, bcv, (s_lap, s_lap.n)], inc=s_lap)
            ctx.step("vector",
                     lambda e, xf=xf: e.tensor_mul(tmpL[:], tmpL[:],
                                                   bctL[:, 0:TC]),
                     waits=[s_lap.now])
            xbf_dst = g.xbf[:, dc * S + tc * TC: dc * S + (tc + 1) * TC]
            ctx.step("vector",
                     lambda e, xbf_dst=xbf_dst, dc=dc: e.tensor_scalar(
                         out=xbf_dst, in0=tmpL[:], scalar1=vec[:, dc:dc + 1],
                         scalar2=vec[:, ND + dc:ND + dc + 1], op0=ALU.mult,
                         op1=ALU.add), waits=[vr2], inc=s_lap)
    ln_done = (s_lap, s_lap.n)
    stL.close()

    stM = contextlib.ExitStack()
    ybuf = ctx.sbuf(u + "_ybuf", (128, NE * 1024), BF16, stack=stM)
    ppF = PsumPool(ctx, u + "_psF", 4, (128, 512), stack=stM)
    ws1 = WStream(ctx, g.wcol[0:2], first_wait=[ln_done, dq1], name=u + "_ws1")
    ws2 = WStream(ctx, g.wcol[2:3], first_wait=[dq2], name=u + "_ws2")
    s_mm1 = ctx.sem(u + "_mm1")
    s_ev1 = ctx.sem(u + "_ev1")
    s_mm2 = ctx.sem(u + "_mm2")
    ep.set_io(io["x_in"], io["x_out"])
    TQ = 1024
    for tq in range(S // TQ):
        mm2_at_start = (s_mm2, s_mm2.n)
        for fo in range(NE):
            buf, rdy, idx = ws1.fetch(w1_d[:, fo * 128:(fo + 1) * 128], D, False)
            last = None
            for nu in range(2):
                bi, bank, bfree = ppF.acquire()
                def mm1(e, buf=buf, tq=tq, nu=nu, bank=bank):
                    for c in range(ND):
                        ins = e.matmul(
                            out=bank[:], lhsT=buf[:, c * 128:(c + 1) * 128],
                            rhs=g.xbf[:, c * S + tq * TQ + nu * 512:
                                      c * S + tq * TQ + (nu + 1) * 512],
                            start=(c == 0), stop=(c == ND - 1))
                    return ins
                ctx.step("tensor", mm1,
                         waits=[rdy] + ([bfree] if bfree else []), inc=s_mm1)
                last = s_mm1.now
                ydst = ybuf[:, fo * 1024 + nu * 512: fo * 1024 + (nu + 1) * 512]
                bcol = vec[:, 2 * ND + fo:2 * ND + fo + 1]
                ctx.step("scalar",
                         lambda e, bank=bank, ydst=ydst, bcol=bcol:
                         e.activation(out=ydst, in_=bank[:], func=AF.Silu,
                                      bias=bcol),
                         waits=[last, vr, mm2_at_start], inc=s_ev1)
                ppF.set_free(bi, s_ev1.now)
            ws1.release(idx, last)
        ev1_done = (s_ev1, s_ev1.n)
        for half in range(2):
            tc = tq * 2 + half
            for dc in range(ND):
                res_info = ep.load_res(tc, dc)
                buf, rdy, idx = ws2.fetch(w2_d[:, dc * 128:(dc + 1) * 128],
                                          EXP, False)
                bi, bank, bfree = ppF.acquire()
                def mm2(e, buf=buf, half=half, bank=bank):
                    for c in range(NE):
                        ins = e.matmul(
                            out=bank[:], lhsT=buf[:, c * 128:(c + 1) * 128],
                            rhs=ybuf[:, c * 1024 + half * 512:
                                     c * 1024 + (half + 1) * 512],
                            start=(c == 0), stop=(c == NE - 1))
                    return ins
                ctx.step("tensor", mm2,
                         waits=[rdy, ev1_done] + ([bfree] if bfree else []),
                         inc=s_mm2)
                ws2.release(idx, s_mm2.now)
                ev = ep.consume(tc, dc, bank[:], halfb2[:, dc:dc + 1],
                                [s_mm2.now, vr2], res_info)
                ppF.set_free(bi, ev)
            ep.finish(tc)
    stM.close()
    return ep.block_done()


def emit_l3_ffn(ctx, g, ep, io, W):
    u = "l3f"
    HID = 3072
    NH = HID // 128
    prev = io["prev_done"]
    wg_d = ctx.dscratch(u + "_wg", (D, HID), BF16)
    wu_d = ctx.dscratch(u + "_wu", (D, HID), BF16)
    wd_d = ctx.dscratch(u + "_wd", (HID, D), BF16)
    st0 = contextlib.ExitStack()
    dqg = emit_dequant(ctx, W["wgT"], None, wg_d, D, HID, 0.0, 1.0 / 127.0,
                       prev, st0)
    dqu = emit_dequant(ctx, W["wuT"], None, wu_d, D, HID, 0.0, 1.0 / 127.0,
                       prev, st0)
    dqd = emit_dequant(ctx, W["wdT"], None, wd_d, HID, D, 0.0, 1.0 / 127.0,
                       prev, st0)
    st0.close()

    stM = contextlib.ExitStack()
    ybuf = ctx.sbuf(u + "_ybuf", (128, NH * 1024), BF16, stack=stM)
    tg = ctx.sbuf(u + "_tg", (128, 2 * 512), F32, stack=stM)
    ppF = PsumPool(ctx, u + "_psF", 4, (128, 512), stack=stM)
    wsg = WStream(ctx, g.wcol[0:1], first_wait=[dqg, io["xbf_ready"]], name=u + "_wsg")
    wsu = WStream(ctx, g.wcol[1:2], first_wait=[dqu, io["xbf_ready"]], name=u + "_wsu")
    wsd = WStream(ctx, g.wcol[2:3], first_wait=[dqd], name=u + "_wsd")
    s_mm1 = ctx.sem(u + "_mm1")
    s_ev1 = ctx.sem(u + "_ev1")
    s_sil = ctx.sem(u + "_sil")
    s_mm2 = ctx.sem(u + "_mm2")
    ep.set_io(io["x_in"], io["x_out"])
    TQ = 1024
    for tq in range(S // TQ):
        mm2_at_start = (s_mm2, s_mm2.n)
        for fo in range(NH):
            bufg, rdyg, idxg = wsg.fetch(wg_d[:, fo * 128:(fo + 1) * 128], D,
                                         False)
            bufu, rdyu, idxu = wsu.fetch(wu_d[:, fo * 128:(fo + 1) * 128], D,
                                         False)
            lastg = lastu = None
            for nu in range(2):
                big, bankg, bfreeg = ppF.acquire()
                biu, banku, bfreeu = ppF.acquire()
                for which, (buf, bank, bfree) in enumerate(
                        ((bufg, bankg, bfreeg), (bufu, banku, bfreeu))):
                    def mm1(e, buf=buf, tq=tq, nu=nu, bank=bank):
                        for c in range(ND):
                            ins = e.matmul(
                                out=bank[:], lhsT=buf[:, c * 128:(c + 1) * 128],
                                rhs=g.xbf[:, c * S + tq * TQ + nu * 512:
                                          c * S + tq * TQ + (nu + 1) * 512],
                                start=(c == 0), stop=(c == ND - 1))
                        return ins
                    ctx.step("tensor", mm1,
                             waits=[rdyg if which == 0 else rdyu] +
                                   ([bfree] if bfree else []), inc=s_mm1)
                lastg = lastu = s_mm1.now
                sl = nu
                tgd = tg[:, sl * 512:(sl + 1) * 512]
                wsl = (s_ev1, max(0, s_ev1.n - 1))
                ctx.step("scalar",
                         lambda e, bankg=bankg, tgd=tgd:
                         e.activation(out=tgd, in_=bankg[:], func=AF.Silu),
                         waits=[lastg, wsl, mm2_at_start], inc=s_sil)
                ppF.set_free(big, s_sil.now)
                ydst = ybuf[:, fo * 1024 + nu * 512: fo * 1024 + (nu + 1) * 512]
                ctx.step("vector",
                         lambda e, banku=banku, tgd=tgd, ydst=ydst:
                         e.tensor_tensor(out=ydst, in0=tgd, in1=banku[:],
                                         op=ALU.mult),
                         waits=[s_sil.now, mm2_at_start], inc=s_ev1)
                ppF.set_free(biu, s_ev1.now)
            wsg.release(idxg, lastg)
            wsu.release(idxu, lastu)
        ev1_done = (s_ev1, s_ev1.n)
        for half in range(2):
            tc = tq * 2 + half
            for dc in range(ND):
                res_info = ep.load_res(tc, dc)
                buf, rdy, idx = wsd.fetch(wd_d[:, dc * 128:(dc + 1) * 128],
                                          HID, False)
                bi, bank, bfree = ppF.acquire()
                def mm2(e, buf=buf, half=half, bank=bank):
                    for c in range(NH):
                        ins = e.matmul(
                            out=bank[:], lhsT=buf[:, c * 128:(c + 1) * 128],
                            rhs=ybuf[:, c * 1024 + half * 512:
                                     c * 1024 + (half + 1) * 512],
                            start=(c == 0), stop=(c == NH - 1))
                    return ins
                ctx.step("tensor", mm2,
                         waits=[rdy, ev1_done] + ([bfree] if bfree else []),
                         inc=s_mm2)
                wsd.release(idx, s_mm2.now)
                ev = ep.consume(tc, dc, bank[:], None, [s_mm2.now], res_info)
                ppF.set_free(bi, ev)
            ep.finish(tc)
    stM.close()
    return ep.block_done()


def emit_l2_attn(ctx, g, ep, io, W):
    io = dict(io)
    io.setdefault("dbg_sc", None); io.setdefault("dbg_yt", None); io.setdefault("dbg_ut", None)
    """W: W_inT (1024, 64) f32; W_outT (64, 1024) f32; log_A, B_param, C_param
    (64, 1) f32; log_dt (1, 1) f32."""
    u = "l2a"
    N = 64
    prev = io["prev_done"]
    stM = contextlib.ExitStack()
    win = ctx.sbuf(u + "_win", (128, ND * N), BF16, stack=stM)
    wout = ctx.sbuf(u + "_wout", (N, D), BF16, stack=stM)
    ut = ctx.sbuf(u + "_ut", (N, S), F32, stack=stM)
    sc = ctx.sbuf(u + "_sc", (N, 16), F32, stack=stM)
    scm = ctx.sbuf(u + "_scm", (N, 12), I32, stack=stM)
    yt = ctx.sbuf(u + "_yt", (N, S), BF16, stack=stM)
    ppU = PsumPool(ctx, u + "_psU", 2, (N, 512), stack=stM)
    ppD = PsumPool(ctx, u + "_psD", 2, (128, TC), stack=stM)
    s_ld = ctx.sem(u + "_ld")
    s_mmU = ctx.sem(u + "_mmU")
    s_evU = ctx.sem(u + "_evU")
    s_zoh = ctx.sem(u + "_zoh")
    s_scan = ctx.sem(u + "_scan")
    s_mmD = ctx.sem(u + "_mmD")
    ctx.dma("gpsimd", win[:].rearrange("p (c n) -> p c n", c=ND),
            W["W_inT"].rearrange("(c p) n -> p c n", p=128), waits=prev,
            inc=s_ld)
    ctx.dma("gpsimd", wout[:], W["W_outT"][:], waits=prev, inc=s_ld)
    ctx.dma("sync", sc[:, 0:1], W["log_A"][:], waits=prev, inc=s_ld)
    ctx.dma("sync", sc[:, 1:2], W["B_param"][:], waits=prev, inc=s_ld)
    ctx.dma("sync", sc[:, 2:3], W["C_param"][:], waits=prev, inc=s_ld)
    ctx.dma("sync", sc[:, 3:4], W["log_dt"].partition_broadcast(N), waits=prev,
            inc=s_ld)
    ldv = s_ld.now
    C_ = lambda i: sc[:, i:i + 1]
    def zstep(eng, fn):
        ctx.step(eng, fn, waits=[s_zoh.now], inc=s_zoh)
    ctx.step("scalar", lambda e: e.activation(out=C_(5), in_=C_(3), func=AF.Exp),
             waits=[ldv], inc=s_zoh)
    zstep("scalar", lambda e: e.activation(out=C_(4), in_=C_(0), func=AF.Exp))
    zstep("vector", lambda e: e.tensor_single_scalar(
        out=scm[:, 0:1], in_=C_(4), scalar=1e-6, op=ALU.is_lt))
    zstep("vector", lambda e: e.tensor_scalar(
        out=C_(4), in0=C_(4), scalar1=-1.0, scalar2=None, op0=ALU.mult))
    zstep("vector", lambda e: e.tensor_mul(C_(6), C_(4), C_(5)))
    zstep("scalar", lambda e: e.activation(out=C_(6), in_=C_(6), func=AF.Exp))
    zstep("vector", lambda e: e.memset(C_(11), 1.0))
    zstep("vector", lambda e: e.select(
        out=C_(10), mask=scm[:, 0:1], on_true=C_(11), on_false=C_(4)))
    zstep("vector", lambda e: e.reciprocal(out=C_(12), in_=C_(10)))
    zstep("vector", lambda e: e.tensor_scalar(
        out=C_(13), in0=C_(6), scalar1=-1.0, scalar2=None, op0=ALU.add))
    zstep("vector", lambda e: e.tensor_mul(C_(7), C_(13), C_(12)))
    zstep("vector", lambda e: e.select(
        out=C_(14), mask=scm[:, 0:1], on_true=C_(5), on_false=C_(7)))
    zstep("vector", lambda e: e.tensor_mul(C_(8), C_(14), C_(1)))
    zoh_done = s_zoh.now
    if io.get("dbg_sc") is not None:
        ctx.dma("gpsimd", io["dbg_sc"], sc[:], waits=[zoh_done], inc=s_ld)

    for nu in range(NTC):
        bi, bank, bfree = ppU.acquire()
        def mmU(e, nu=nu, bank=bank):
            for c in range(ND):
                ins = e.matmul(
                    out=bank[:], lhsT=win[:, c * N:(c + 1) * N],
                    rhs=g.xbf[:, c * S + nu * TC: c * S + (nu + 1) * TC],
                    start=(c == 0), stop=(c == ND - 1))
            return ins
        ctx.step("tensor", mmU, waits=[ldv, io["xbf_ready"]] +
                 ([bfree] if bfree else []), inc=s_mmU)
        ctx.step("vector",
                 lambda e, nu=nu, bank=bank: e.tensor_copy(
                     out=ut[:, nu * TC:(nu + 1) * TC], in_=bank[:]),
                 waits=[s_mmU.now], inc=s_evU)
        ppU.set_free(bi, s_evU.now)
    uev = s_evU.now
    bu = ctx.sbuf(u + "_bu", (N, S), F32, stack=stM)
    hsc = ctx.sbuf(u + "_hsc", (N, S), F32, stack=stM)
    ctx.step("vector", lambda e: e.tensor_scalar(
        out=bu[:], in0=ut[:], scalar1=C_(8), scalar2=None, op0=ALU.mult),
        waits=[uev, zoh_done])
    abar_b = sc[:, 6:7].broadcast_to([N, S])
    ctx.step("vector", lambda e: e.tensor_tensor_scan(
        out=hsc[:], data0=abar_b, data1=bu[:], initial=0.0,
        op0=ALU.mult, op1=ALU.add))
    ctx.step("vector", lambda e: e.tensor_scalar(
        out=yt[:], in0=hsc[:], scalar1=C_(2), scalar2=None, op0=ALU.mult),
        inc=s_scan)
    scan_done = s_scan.now
    if io.get("dbg_yt") is not None:
        ctx.dma("gpsimd", io["dbg_yt"], yt[:], waits=[scan_done], inc=s_ld)
    if io.get("dbg_ut") is not None:
        ctx.dma("gpsimd", io["dbg_ut"], ut[:], waits=[scan_done], inc=s_ld)

    ep.set_io(io["x_in"], io["x_out"])
    for tc in range(NTC):
        for dc in range(ND):
            res_info = ep.load_res(tc, dc)
            bi, bank, bfree = ppD.acquire()
            ctx.step("tensor",
                     lambda e, dc=dc, tc=tc, bank=bank: e.matmul(
                         out=bank[:], lhsT=wout[:, dc * 128:(dc + 1) * 128],
                         rhs=yt[:, tc * TC:(tc + 1) * TC],
                         start=True, stop=True),
                     waits=[scan_done] + ([bfree] if bfree else []), inc=s_mmD)
            ev = ep.consume(tc, dc, bank[:], None, [s_mmD.now], res_info)
            ppD.set_free(bi, ev)
        ep.finish(tc)
    stM.close()
    return ep.block_done()


def emit_quantize(ctx, q, x_ap, out_ap, K, waits, inc):
    NB = K // 32
    s_v, s_a, s_g, s_n = q["s_v"], q["s_a"], q["s_g"], q["s_n"]
    x3 = x_ap.rearrange("p (b e) -> p b e", e=32)
    ctx.step("vector", lambda e: e.tensor_reduce(
        out=q["bm"][:, 0:NB], in_=x3, axis=AX, op=ALU.max,
        apply_absolute_value=True), waits=list(waits) + [s_g.now], inc=s_n)
    ctx.step("vector", lambda e: e.tensor_scalar(
        out=q["bm"][:, 0:NB], in0=q["bm"][:, 0:NB], scalar1=1.0 / 7.5,
        scalar2=1e-30, op0=ALU.mult, op1=ALU.max), waits=[s_n.now], inc=s_n)
    ctx.step("vector", lambda e: e.reciprocal(out=q["rs"][:, 0:NB],
                                              in_=q["bm"][:, 0:NB]),
             waits=[s_n.now], inc=s_n)
    rsb = q["rs"][:, 0:NB].unsqueeze(2).broadcast_to([128, NB, 32])
    yq3 = q["yq"][:, 0:K].rearrange("p (b e) -> p b e", e=32)
    ctx.step("vector", lambda e: e.tensor_tensor(
        out=yq3, in0=x3, in1=rsb, op=ALU.mult), waits=[s_n.now], inc=s_v)
    ctx.step("scalar", lambda e: e.activation(
        out=q["ya"][:, 0:K], in_=q["yq"][:, 0:K], func=AF.Abs),
        waits=[s_v.now, s_g.now], inc=s_a)
    av = s_a.now
    ctx.step("gpsimd", lambda e: e.tensor_single_scalar(
        out=q["c2"][:, 0:K], in_=q["ya"][:, 0:K], scalar=2.0, op=ALU.is_ge),
        waits=[av, q["s_u"].now])
    ctx.step("gpsimd", lambda e: e.tensor_single_scalar(
        out=q["c4"][:, 0:K], in_=q["ya"][:, 0:K], scalar=4.0, op=ALU.is_ge),
        inc=s_g)
    gv = s_g.now
    ctx.step("vector", lambda e: e.tensor_scalar(
        out=q["t1"][:, 0:K], in0=q["c2"][:, 0:K], scalar1=-4.0, scalar2=8.0,
        op0=ALU.mult, op1=ALU.add), waits=[gv])
    ctx.step("vector", lambda e: e.scalar_tensor_tensor(
        out=q["t1"][:, 0:K], in0=q["c4"][:, 0:K], scalar=-2.0,
        in1=q["t1"][:, 0:K], op0=ALU.mult, op1=ALU.add))
    ctx.step("vector", lambda e: e.tensor_tensor(
        out=q["u"][:, 0:K], in0=q["yq"][:, 0:K], in1=q["t1"][:, 0:K],
        op=ALU.mult))
    ctx.step("vector", lambda e: e.tensor_copy(out=q["i32"][:, 0:K],
                                               in_=q["u"][:, 0:K]))
    ctx.step("vector", lambda e: e.tensor_copy(out=q["u"][:, 0:K],
                                               in_=q["i32"][:, 0:K]))
    ctx.step("vector", lambda e: e.tensor_scalar(
        out=q["t2"][:, 0:K], in0=q["c2"][:, 0:K], scalar1=0.125, scalar2=0.125,
        op0=ALU.mult, op1=ALU.add))
    ctx.step("vector", lambda e: e.scalar_tensor_tensor(
        out=q["t2"][:, 0:K], in0=q["c4"][:, 0:K], scalar=0.25,
        in1=q["t2"][:, 0:K], op0=ALU.mult, op1=ALU.add))
    ctx.step("vector", lambda e: e.tensor_tensor(
        out=q["u"][:, 0:K], in0=q["u"][:, 0:K], in1=q["t2"][:, 0:K],
        op=ALU.mult), inc=q["s_u"])
    bmb = q["bm"][:, 0:NB].unsqueeze(2).broadcast_to([128, NB, 32])
    u3 = q["u"][:, 0:K].rearrange("p (b e) -> p b e", e=32)
    out3 = out_ap.rearrange("p (b e) -> p b e", e=32)
    ctx.step("vector", lambda e: e.tensor_tensor(
        out=out3, in0=u3, in1=bmb, op=ALU.mult), inc=inc)


def quant_scratch(ctx, name, K, stack):
    q = {n: ctx.sbuf(f"{name}_{n}", (128, K), F32, stack=stack)
         for n in ("yq", "ya", "u")}
    for n in ("c2", "c4", "t1", "t2"):
        q[n] = ctx.sbuf(f"{name}_{n}", (128, K), BF16, stack=stack)
    q["i32"] = ctx.sbuf(f"{name}_i32", (128, K), I32, stack=stack)
    q["bm"] = ctx.sbuf(f"{name}_bm", (128, K // 32), F32, stack=stack)
    q["rs"] = ctx.sbuf(f"{name}_rs", (128, K // 32), F32, stack=stack)
    q["s_v"] = ctx.sem(f"{name}_sv")
    q["s_u"] = ctx.sem(f"{name}_su")
    q["s_n"] = ctx.sem(f"{name}_sn")
    q["s_a"] = ctx.sem(f"{name}_sa")
    q["s_g"] = ctx.sem(f"{name}_sg")
    return q


def emit_l0_ffn(ctx, g, ep, io, W):
    u = "l0f"
    prev = io["prev_done"]
    wdT_d = ctx.dscratch(u + "_wdT", (D, D), BF16)

    bias = ctx.sbuf(u + "_bias", (128, ND), F32)
    s_b = ctx.sem(u + "_b")
    ctx.dma("sync", bias[:], W["bias"][:], waits=list(prev), inc=s_b)
    bias_ld = s_b.now

    st0 = contextlib.ExitStack()
    q0 = quant_scratch(ctx, u + "_q0", D, st0)
    wf = ctx.sbuf(u + "_wf", (128, D), F32, stack=st0)
    wq = ctx.sbuf(u + "_wq", (128, D), BF16, stack=st0)
    wtv = ctx.sbuf(u + "_wtv", (128, D), BF16, stack=st0)
    ps_t = ctx.psum(u + "_psT", (128, D), BF16, stack=st0)
    s_ld0 = ctx.sem(u + "_ld0")
    s_q0 = ctx.sem(u + "_q0")
    s_tr0 = ctx.sem(u + "_tr0")
    s_o0 = ctx.sem(u + "_o0")
    for i in range(ND):
        wl = list(prev) if i == 0 else [(s_q0, s_q0.n)]
        ctx.dma("sync", wf[:], W["weight"][i * 128:(i + 1) * 128, :],
                waits=wl, inc=s_ld0)
        emit_quantize(ctx, q0, wf[:], wq[:], D,
                      [s_ld0.now] + ([(s_tr0, s_tr0.n - 1)] if i >= 1 else []),
                      s_q0)
        def tr(e, i=i):
            for j in range(ND):
                ins = e.transpose(out=ps_t[:, j * 128:(j + 1) * 128],
                                  in_=wq[:, j * 128:(j + 1) * 128],
                                  identity=g.ident_bf[:])
            return ins
        ctx.step("tensor", tr, waits=[s_q0.now, g.ident_done,
                                      (s_o0, s_o0.n - 16)], inc=s_tr0)
        ctx.step("vector", lambda e: e.tensor_copy(out=wtv[:], in_=ps_t[:]),
                 waits=[s_tr0.now], inc=s_tr0)
        ctx.dma("sync",
                wdT_d[:, i * 128:(i + 1) * 128]
                .rearrange("(j p) c -> p j c", p=128),
                wtv[:].rearrange("p (j c) -> p j c", c=128),
                waits=[s_tr0.now], inc=s_o0)
    p0_done = (s_o0, s_o0.n)
    st0.close()

    stM = contextlib.ExitStack()
    q1 = quant_scratch(ctx, u + "_q1", 2 * TC, stM)
    wdT_sb = ctx.sbuf(u + "_wdsb", (128, ND * D), BF16, stack=stM)
    xtok = ctx.sbuf(u + "_xtok", (128, D), F32, stM)
    xqtok = ctx.sbuf(u + "_xqtok", (128, D), BF16, stM)
    xqT = ctx.sbuf(u + "_xqT", (128, ND * TC), BF16, stM)
    xfin = ctx.sbuf(u + "_xfin", (128, ND * TC), F32, stM)
    ps_ti = ctx.psum(u + "_psTi", (128, TC), F32, stack=stM)
    ps_to = ctx.psum(u + "_psTo", (128, D), BF16, stack=stM)
    ppM = PsumPool(ctx, u + "_psM", 2, (128, TC), stack=stM)
    s_wsb = ctx.sem(u + "_wsb")
    s_xin = ctx.sem(u + "_xin")
    s_ti = ctx.sem(u + "_ti")
    s_tie = ctx.sem(u + "_tie")
    s_q1 = ctx.sem(u + "_q1")
    s_to = ctx.sem(u + "_to")
    s_toe = ctx.sem(u + "_toe")
    s_mm = ctx.sem(u + "_mm")
    ctx.dma("sync", wdT_sb[:].rearrange("p (kc c) -> p kc c", kc=ND),
            wdT_d.rearrange("(kc p) c -> p kc c", p=128), waits=[p0_done],
            inc=s_wsb)
    wsb_rdy = s_wsb.now
    ep.set_io(io["x_in"], io["x_out"])
    for tc in range(NTC):
        wl = [(s_q1, s_q1.n)] if tc >= 1 else []
        ctx.dma("sync", xfin[:].rearrange("p (c t) -> p c t", c=ND),
                io["x_in"].rearrange("(c p) t -> p c t", p=128)
                [:, :, tc * TC:(tc + 1) * TC], waits=wl, inc=s_xin)
        xrdy = s_xin.now
        for tt in range(4):
            kq = 4 * tc + tt
            for hw in range(2):
                def tri(e, tt=tt, hw=hw):
                    for j in range(4):
                        dc = hw * 4 + j
                        ins = e.transpose(
                            out=ps_ti[:, j * 128:(j + 1) * 128],
                            in_=xfin[:, dc * TC + tt * 128:
                                     dc * TC + (tt + 1) * 128],
                            identity=g.ident_f32[:])
                    return ins
                ctx.step("tensor", tri,
                         waits=[xrdy, g.ident_done, (s_tie, s_tie.n)],
                         inc=s_ti)
                ctx.step("scalar",
                         lambda e, hw=hw: e.copy(
                             out=xtok[:, hw * TC:(hw + 1) * TC], in_=ps_ti[:]),
                         waits=[s_ti.now, (s_q1, s_q1.n)],
                         inc=s_tie)
            emit_quantize(ctx, q1, xtok[:], xqtok[:], D,
                          [s_tie.now] + ([(s_to, s_to.n)] if kq >= 1 else []),
                          s_q1)
            ctx.step("tensor",
                     lambda e, tt=tt: [
                         e.transpose(out=ps_to[:, dc * 128:(dc + 1) * 128],
                                     in_=xqtok[:, dc * 128:(dc + 1) * 128],
                                     identity=g.ident_bf[:])
                         for dc in range(ND)][-1],
                     waits=[s_q1.now, (s_toe, s_toe.n)], inc=s_to)
            ctx.step("vector",
                     lambda e, tt=tt: [
                         e.tensor_copy(
                             out=xqT[:, dc * TC + tt * 128:
                                     dc * TC + (tt + 1) * 128],
                             in_=ps_to[:, dc * 128:(dc + 1) * 128])
                         for dc in range(ND)][-1],
                     waits=[s_to.now], inc=s_toe)
        xq_rdy = (s_toe, s_toe.n)
        for dc in range(ND):
            res_info = ep.load_res(tc, dc)
            bi, bank, bfree = ppM.acquire()
            def mmq(e, dc=dc, bank=bank):
                for c in range(ND):
                    ins = e.matmul(
                        out=bank[:],
                        lhsT=wdT_sb[:, c * D + dc * 128: c * D + dc * 128 + 128],
                        rhs=xqT[:, c * TC:(c + 1) * TC],
                        start=(c == 0), stop=(c == ND - 1))
                return ins
            ctx.step("tensor", mmq,
                     waits=[xq_rdy, wsb_rdy] + ([bfree] if bfree else []),
                     inc=s_mm)
            ev = ep.consume(tc, dc, bank[:], bias[:, dc:dc + 1], [s_mm.now],
                            res_info)
            ppM.set_free(bi, ev)
        ep.finish(tc)
    stM.close()
    return ep.block_done()


def l2ffn_coeffs(w):
    i = np.arange(4096)
    packs = {}
    for nm in ("lowD", "lowO", "lowOT", "highD", "highO", "highOT"):
        packs[nm] = []
    for s in range(12):
        pair = (i >> (s + 1) << s) | (i & ((1 << s) - 1))
        bit = (i >> s) & 1
        Dv = w[s, pair, bit, bit]
        Ov = w[s, pair, bit, 1 - bit]
        OTv = w[s, pair, 1 - bit, bit]
        if s < 7:
            packs["lowD"].append(Dv)
            packs["lowO"].append(Ov)
            packs["lowOT"].append(OTv)
        else:
            packs["highD"].append(Dv)
            packs["highO"].append(Ov)
            packs["highOT"].append(OTv)
    out = {}
    for nm, lst in packs.items():
        a = np.stack(lst)
        ns = a.shape[0]
        a = a.reshape(ns * 32, 128).T
        out[nm] = np.ascontiguousarray(a.astype(np.float32))
    return out


def l2ffn_masks():
    m = [np.eye(128, dtype=BF)]
    for s in range(7):
        q = np.arange(128) ^ (1 << s)
        mm = np.zeros((128, 128), dtype=BF)
        mm[np.arange(128), q] = 1.0
        m.append(mm)
    return np.ascontiguousarray(np.concatenate(m, axis=1))


def emit_l2_ffn(ctx, g, ep, io, W):
    u = "l2f"
    prev = io["prev_done"]
    a_d = ctx.dscratch(u + "_a", (D, EXP), BF16)
    b_d = ctx.dscratch(u + "_b", (EXP, D), BF16)
    apre_d = ctx.dscratch(u + "_apre", (EXP, 1025), BF16)
    bpre_d = ctx.dscratch(u + "_bpre", (EXP, D), BF16)

    biasA = ctx.sbuf(u + "_biasA", (128, NE), F32)
    bias2 = ctx.sbuf(u + "_bias2", (128, ND), F32)

    stP = contextlib.ExitStack()
    coe = ctx.sbuf(u + "_coe", (128, 2048), F32, stack=stP)
    msk = ctx.sbuf(u + "_msk", (128, 8 * 128), BF16, stack=stP)
    vec2 = ctx.sbuf(u + "_vec2", (128, NE + 2 * ND), F32, stack=stP)
    s_c = ctx.sem(u + "_c")
    offs = dict(up_lowD=0, up_lowO=224, up_lowOT=448, dn_lowD=672, dn_lowO=896,
                dn_lowOT=1120, up_highD=1344, up_highO=1504, dn_highD=1664,
                dn_highOT=1824)
    first = True
    for nm in offs:
        n = 224 if "low" in nm else 160
        ctx.dma("sync", coe[:, offs[nm]:offs[nm] + n], W[nm][:],
                waits=(list(prev) if first else []), inc=s_c)
        first = False
    ctx.dma("sync", msk[:], W["masks"][:], inc=s_c)
    ctx.dma("sync", vec2[:, 0:NE], W["up_bias"][:], inc=s_c)
    ctx.dma("sync", vec2[:, NE:NE + ND], W["b_con"][:], inc=s_c)
    ctx.dma("sync", vec2[:, NE + ND:NE + 2 * ND], W["down_bias"][:], inc=s_c)
    c_rdy = s_c.now
    s_b2 = ctx.sem(u + "_b2")
    ctx.step("vector", lambda e: e.tensor_tensor(
        out=bias2[:], in0=vec2[:, NE:NE + ND], in1=vec2[:, NE + ND:NE + 2 * ND],
        op=ALU.add), waits=[c_rdy], inc=s_b2)
    bias2_rdy = s_b2.now

    mdiag = msk[:, 0:128]
    moff = lambda s: msk[:, (s + 1) * 128:(s + 2) * 128]
    ccol = lambda nm, s, J: coe[:, offs[nm] + (s if "low" in nm else s - 7) * 32
                               + J: offs[nm] + (s if "low" in nm else s - 7) * 32
                               + J + 1]

    s_bld = ctx.sem(u + "_bld")
    s_cmp = ctx.sem(u + "_cmp")

    def build_S(Sdst, s, J, Dnm, Onm, extra=()):
        ctx.step("vector", lambda e: e.tensor_scalar(
            out=Sdst, in0=mdiag, scalar1=ccol(Dnm, s, J), scalar2=None,
            op0=ALU.mult), waits=[c_rdy, s_bld.now] + list(extra), inc=s_bld)
        ctx.step("vector", lambda e: e.scalar_tensor_tensor(
            out=Sdst, in0=moff(s), scalar=ccol(Onm, s, J), in1=Sdst,
            op0=ALU.mult, op1=ALU.add), waits=[s_bld.now], inc=s_bld)

    def compose_low(J, specs, SB, R0, R1, ps_c, extra_first=()):
        build_S(R0[:], specs[0][0], J, specs[0][1], specs[0][2],
                extra=list(extra_first))
        cur, nxt = R0, R1
        for (s, Dn, On) in specs[1:]:
            build_S(SB[:], s, J, Dn, On, extra=[s_cmp.now])
            ctx.step("tensor",
                     lambda e, cur=cur: e.matmul(
                         out=ps_c[:], lhsT=SB[:], rhs=cur[:],
                         start=True, stop=True),
                     waits=[s_bld.now, s_cmp.now], inc=s_cmp)
            ctx.step("vector",
                     lambda e, nxt=nxt: e.tensor_copy(out=nxt[:], in_=ps_c[:]),
                     waits=[s_cmp.now], inc=s_cmp)
            cur, nxt = nxt, cur
        return cur

    def high_dram(src_d, NCOL, stages, Dnm, Onm, ready, tag):
        bl = [ctx.sbuf(f"{u}_hl{tag}{i}", (128, NCOL), BF16, stack=stP)
              for i in range(2)]
        bh = [ctx.sbuf(f"{u}_hh{tag}{i}", (128, NCOL), BF16, stack=stP)
              for i in range(2)]
        bt = [ctx.sbuf(f"{u}_ht{tag}{i}", (128, NCOL), BF16, stack=stP)
              for i in range(2)]
        s_l = ctx.sem(f"{u}_hld{tag}")
        s_k = ctx.sem(f"{u}_hk{tag}")
        s_s = ctx.sem(f"{u}_hs{tag}")
        kk = 0
        for si, s in enumerate(stages):
            stride = 1 << (s - 7)
            stage_start_store = s_s.n
            for t in range(NE):
                if t & stride:
                    continue
                t2 = t | stride
                b = kk % 2
                w0 = [ready] if kk == 0 else []
                if si > 0 and (kk % 16) == 0:
                    w0.append((s_s, stage_start_store))
                if kk >= 2:
                    w0.append((s_s, s_s.n - 32))
                ctx.dma("sync", bl[b][:], src_d[t * 128:(t + 1) * 128, :],
                        waits=w0, inc=s_l)
                ctx.dma("sync", bh[b][:], src_d[t2 * 128:(t2 + 1) * 128, :],
                        inc=s_l)
                lrd = s_l.now
                ctx.step("vector", lambda e, b=b, s=s, t=t: e.tensor_scalar(
                    out=bt[b][:], in0=bl[b][:], scalar1=ccol(Dnm, s, t),
                    scalar2=None, op0=ALU.mult), waits=[lrd, c_rdy])
                ctx.step("vector", lambda e, b=b, s=s, t=t:
                         e.scalar_tensor_tensor(
                             out=bt[b][:], in0=bh[b][:], scalar=ccol(Onm, s, t),
                             in1=bt[b][:], op0=ALU.mult, op1=ALU.add))
                ctx.step("vector", lambda e, b=b, s=s, t2=t2: e.tensor_scalar(
                    out=bh[b][:], in0=bh[b][:], scalar1=ccol(Dnm, s, t2),
                    scalar2=None, op0=ALU.mult))
                ctx.step("vector", lambda e, b=b, s=s, t2=t2:
                         e.scalar_tensor_tensor(
                             out=bh[b][:], in0=bl[b][:], scalar=ccol(Onm, s, t2),
                             in1=bh[b][:], op0=ALU.mult, op1=ALU.add),
                         inc=s_k)
                kv = s_k.now
                ctx.dma("scalar", src_d[t * 128:(t + 1) * 128, :], bt[b][:],
                        waits=[kv], inc=s_s)
                ctx.dma("scalar", src_d[t2 * 128:(t2 + 1) * 128, :], bh[b][:],
                        waits=[kv], inc=s_s)
                kk += 1
            stage_done = (s_s, s_s.n)
            ready = stage_done
        return (s_s, s_s.n)

    SB = ctx.sbuf(u + "_SB", (128, 128), BF16, stack=stP)
    R0 = ctx.sbuf(u + "_R0", (128, 128), BF16, stack=stP)
    R1 = ctx.sbuf(u + "_R1", (128, 128), BF16, stack=stP)
    wldb = [ctx.sbuf(u + f"_wld{i}", (128, 1025), BF16, stack=stP)
            for i in range(2)]
    ps_c = ctx.psum(u + "_psc", (128, 128), F32, stack=stP)
    ps_a = ctx.psum(u + "_psa", (128, 3 * 512), F32, stack=stP)
    wov = ctx.sbuf(u + "_wov", (128, 1088), BF16, stack=stP)

    s_wld = ctx.sem(u + "_wld")
    s_app = ctx.sem(u + "_app")
    s_wo = ctx.sem(u + "_wo")
    NCOL = 1025
    for J in range(NE):
        ctx.dma("gpsimd", wldb[J % 2][:, 0:NCOL],
                W["WexpT_aug"][J * 128:(J + 1) * 128, :],
                waits=([(s_app, max(0, s_app.n - 3))] if J >= 2
                       else list(prev)),
                inc=s_wld)
        wrdy = s_wld.now
        cur = compose_low(J, [(6, "up_lowD", "up_lowOT")] +
                          [(s, "up_lowD", "up_lowO") for s in range(5, -1, -1)],
                          SB, R0, R1, ps_c,
                          extra_first=[(s_app, max(0, s_app.n - 1))])
        def app(e, cur=cur, J=J):
            o = None
            for a0 in (0, 512, 1024):
                n = min(512, NCOL - a0)
                o = e.matmul(out=ps_a[:, a0:a0 + n], lhsT=cur[:],
                             rhs=wldb[J % 2][:, a0:a0 + n],
                             start=True, stop=True)
            return o
        ctx.step("tensor", app, waits=[s_cmp.now, wrdy, (s_app, s_app.n)],
                 inc=s_app)
        ctx.step("vector",
                 lambda e: e.tensor_copy(out=wov[:, 0:NCOL],
                                         in_=ps_a[:, 0:NCOL]),
                 waits=[s_app.now, (s_wo, 16 * J)], inc=s_app)
        ctx.dma("sync", apre_d[J * 128:(J + 1) * 128, :], wov[:, 0:NCOL],
                waits=[s_app.now], inc=s_wo)
    up_comp = (s_wo, s_wo.n)
    up_hi = high_dram(apre_d, 1025, range(7, 12), "up_highD", "up_highO",
                      up_comp, "U")
    s_lw = ctx.sem(u + "_lw")
    s_tc = ctx.sem(u + "_tc")
    s_te = ctx.sem(u + "_te")
    s_wo3 = ctx.sem(u + "_wo3")
    s_bx = ctx.sem(u + "_bx")
    for J in range(NE):
        w0 = [up_hi] if J == 0 else [(s_wo3, s_wo3.n)]
        ctx.dma("sync", wov[:, 0:NCOL], apre_d[J * 128:(J + 1) * 128, :],
                waits=w0, inc=s_lw)
        ctx.step("tensor",
                 lambda e: [
                     e.transpose(out=ps_a[:].bitcast(BF16)[:, c * 128:(c + 1) * 128],
                                 in_=wov[:, c * 128:(c + 1) * 128],
                                 identity=g.ident_bf[:])
                     for c in range(ND)][-1],
                 waits=[s_lw.now, g.ident_done, (s_te, s_te.n)], inc=s_tc)
        ctx.step("gpsimd",
                 lambda e, J=J: e.tensor_copy(
                     out=biasA[:, J:J + 1], in_=wov[:, 1024:1025]),
                 waits=[s_lw.now], inc=s_bx)
        ctx.step("vector",
                 lambda e: e.tensor_copy(out=wov[:, 0:D],
                                         in_=ps_a[:].bitcast(BF16)[:, 0:D]),
                 waits=[s_tc.now, s_bx.now], inc=s_te)
        ctx.dma("sync",
                a_d[:, J * 128:(J + 1) * 128]
                .rearrange("(c p) f -> p c f", p=128),
                wov[:, 0:D].rearrange("p (c f) -> p c f", f=128),
                waits=[s_te.now], inc=s_wo3)
    up_done = (s_wo3, s_wo3.n)
    s_ba = ctx.sem(u + "_ba")
    ctx.step("vector", lambda e: e.tensor_tensor(
        out=biasA[:], in0=biasA[:], in1=vec2[:, 0:NE], op=ALU.add),
        waits=[(s_bx, s_bx.n), c_rdy], inc=s_ba)
    biasA_rdy = s_ba.now

    s_pcd = ctx.sem(u + "_pcd")
    for J in range(NE):
        w0 = [(s_pcd, s_pcd.n - 32)] if J >= 2 else [(s_app, s_app.n)]
        ctx.dma("gpsimd", wldb[J % 2][:, 0:D],
                W["W_con"][J * 128:(J + 1) * 128, :], waits=w0, inc=s_pcd)
        ctx.dma("scalar", bpre_d[J * 128:(J + 1) * 128, :],
                wldb[J % 2][:, 0:D], waits=[s_pcd.now], inc=s_pcd)
    pcd_done = (s_pcd, s_pcd.n)
    dn_hi = high_dram(bpre_d, D, range(11, 6, -1), "dn_highD", "dn_highOT",
                      pcd_done, "D")
    s_app2 = ctx.sem(u + "_app2")
    s_wo2 = ctx.sem(u + "_wo2")
    s_wld2 = ctx.sem(u + "_wld2")
    for J in range(NE):
        w0 = [dn_hi] if J < 2 else [(s_app2, s_app2.n - 3)]
        ctx.dma("sync", wldb[J % 2][:, 0:D],
                bpre_d[J * 128:(J + 1) * 128, :], waits=w0, inc=s_wld2)
        wrdy = s_wld2.now
        cur = compose_low(J, [(0, "dn_lowD", "dn_lowO")] +
                          [(s, "dn_lowD", "dn_lowOT") for s in range(1, 7)],
                          SB, R0, R1, ps_c,
                          extra_first=[(s_app2, max(0, s_app2.n - 1))]
                          if J >= 1 else [up_done])
        def app2(e, cur=cur, J=J):
            o = None
            for a0 in (0, 512):
                o = e.matmul(out=ps_a[:, a0:a0 + 512], lhsT=cur[:],
                             rhs=wldb[J % 2][:, a0:a0 + 512],
                             start=True, stop=True)
            return o
        ctx.step("tensor", app2, waits=[s_cmp.now, wrdy, (s_app2, s_app2.n)],
                 inc=s_app2)
        ctx.step("vector",
                 lambda e: e.tensor_copy(out=wov[:, 0:D], in_=ps_a[:, 0:D]),
                 waits=[(s_app2, s_app2.n), (s_wo2, s_wo2.n)], inc=s_app2)
        ctx.dma("sync", b_d[J * 128:(J + 1) * 128, :], wov[:, 0:D],
                waits=[(s_app2, s_app2.n)], inc=s_wo2)
    dn_done = (s_wo2, s_wo2.n)
    stP.close()

    stM = contextlib.ExitStack()
    ybuf = ctx.sbuf(u + "_ybuf", (128, NE * 1024), BF16, stack=stM)
    ppF = PsumPool(ctx, u + "_psF", 4, (128, 512), stack=stM)
    ws1 = WStream(ctx, g.wcol[0:2], first_wait=[up_done, io["xbf_ready"]], name=u + "_ws1")
    ws2 = WStream(ctx, g.wcol[2:3], first_wait=[dn_done], name=u + "_ws2")
    s_mm1 = ctx.sem(u + "_mm1")
    s_ev1 = ctx.sem(u + "_ev1")
    s_mm2 = ctx.sem(u + "_mm2")
    ep.set_io(io["x_in"], io["x_out"])
    TQ = 1024
    for tq in range(S // TQ):
        mm2_at_start = (s_mm2, s_mm2.n)
        for fo in range(NE):
            buf, rdy, idx = ws1.fetch(a_d[:, fo * 128:(fo + 1) * 128], D, False)
            last = None
            for nu in range(2):
                bi, bank, bfree = ppF.acquire()
                def mm1(e, buf=buf, tq=tq, nu=nu, bank=bank):
                    for c in range(ND):
                        ins = e.matmul(
                            out=bank[:], lhsT=buf[:, c * 128:(c + 1) * 128],
                            rhs=g.xbf[:, c * S + tq * TQ + nu * 512:
                                      c * S + tq * TQ + (nu + 1) * 512],
                            start=(c == 0), stop=(c == ND - 1))
                    return ins
                ctx.step("tensor", mm1,
                         waits=[rdy] + ([bfree] if bfree else []), inc=s_mm1)
                last = s_mm1.now
                ydst = ybuf[:, fo * 1024 + nu * 512: fo * 1024 + (nu + 1) * 512]
                ctx.step("scalar",
                         lambda e, bank=bank, ydst=ydst, fo=fo:
                         e.activation(out=ydst, in_=bank[:], func=AF.Gelu,
                                      bias=biasA[:, fo:fo + 1]),
                         waits=[last, biasA_rdy, mm2_at_start], inc=s_ev1)
                ppF.set_free(bi, s_ev1.now)
            ws1.release(idx, last)
        ev1_done = (s_ev1, s_ev1.n)
        for half in range(2):
            tc = tq * 2 + half
            for dc in range(ND):
                res_info = ep.load_res(tc, dc)
                buf, rdy, idx = ws2.fetch(b_d[:, dc * 128:(dc + 1) * 128],
                                          EXP, False)
                bi, bank, bfree = ppF.acquire()
                def mm2(e, buf=buf, half=half, bank=bank):
                    for c in range(NE):
                        ins = e.matmul(
                            out=bank[:], lhsT=buf[:, c * 128:(c + 1) * 128],
                            rhs=ybuf[:, c * 1024 + half * 512:
                                     c * 1024 + (half + 1) * 512],
                            start=(c == 0), stop=(c == NE - 1))
                    return ins
                ctx.step("tensor", mm2,
                         waits=[rdy, ev1_done] + ([bfree] if bfree else []),
                         inc=s_mm2)
                ws2.release(idx, s_mm2.now)
                ev = ep.consume(tc, dc, bank[:], bias2[:, dc:dc + 1],
                                [s_mm2.now, bias2_rdy], res_info)
                ppF.set_free(bi, ev)
            ep.finish(tc)
    stM.close()
    return ep.block_done()


def attn_host_consts(has_alibi):
    ind = np.zeros((128, 8 * 16), dtype=BF)
    for oc in range(8):
        for p in range(128):
            ind[p, oc * 16 + 2 * oc + (p >= 64)] = 1.0
    cm = np.zeros((128, 4 * 512), dtype=BF)
    for off in range(4):
        p = np.arange(128)[:, None]
        fq = np.arange(512)[None, :]
        cm[:, off * 512:(off + 1) * 512] = (fq - p - off * 128 >= 0)
    out = {"ind": ind, "cmask": cm}
    if has_alibi:
        def slopes_pow2(n):
            start = 2.0 ** (-2.0 ** (-(math.log2(n) - 3)))
            return [start * start ** i for i in range(n)]
        sl = np.array(slopes_pow2(16), dtype=np.float64)
        pos = np.arange(S, dtype=np.float64)
        ak = (sl[:, None] * pos[None, :])
        ak_hi = ak.astype(BF)
        ak_lo = (ak - ak_hi.astype(np.float64)).astype(BF)
        out["alibi_k"] = np.ascontiguousarray(ak_hi)
        out["alibi_k_lo"] = np.ascontiguousarray(ak_lo)
        out["alibi_q"] = ak.astype(np.float32)
    return out


def emit_attn(ctx, g, ep, io, W, cfg):
    u = cfg["name"]
    su = "at"
    alibi = cfg["alibi"]
    KDIM = 67 if alibi else 65
    prev = io["prev_done"]
    qT_d = ctx.dscratch(u + "_qd", (D, S), BF16)
    kT_d = ctx.dscratch(u + "_kd", (D, S), BF16)
    v_d = ctx.dscratch(u + "_vd", (H, S, HD), BF16)
    b_d = ctx.dscratch(u + "_bd", (16, S), F32)
    lrow_d = ctx.dscratch(u + "_lrd", (2, 512), F32)

    if not hasattr(ctx, "_at_tiles"):
        ctx._at_tiles = (ctx.sbuf(su + "_ind", (128, 128), BF16),
                         ctx.sbuf(su + "_cmask", (128, 4 * 512), BF16),
                         ctx.sbuf(su + "_bias", (128, 4 * ND), F32))
    ind_sb, cmask_sb, bias_sb = ctx._at_tiles
    s_c = ctx.sem(su + "_c")
    ctx.dma("sync", ind_sb[:], W["ind"][:], waits=list(prev), inc=s_c)
    ctx.dma("sync", cmask_sb[:], W["cmask"][:], inc=s_c)
    if cfg["qkv_bias"]:
        for i, nm in enumerate(("bq", "bk", "bv", "bo")):
            ctx.dma("sync", bias_sb[:, i * ND:(i + 1) * ND], W[nm][:], inc=s_c)
    c_rdy = s_c.now

    stA = contextlib.ExitStack()
    ws = WStream(ctx, g.wcol[0:2], first_wait=[io["xbf_ready"]] + list(prev), name=su + "_wsA")
    ppA = PsumPool(ctx, u + "_psA", 2, (128, 512), stack=stA)
    ps_ind = ctx.psum(u + "_psI", (16, 512), stack=stA)
    ps_tv = ctx.psum(u + "_psTv", (128, 512), BF16, stack=stA)
    stg = ctx.sbuf(u + "_stg", (128, 4 * 512), BF16, stack=stA)
    sqg = ctx.sbuf(u + "_sqg", (128, 512), BF16, stack=stA)
    nrm = ctx.sbuf(u + "_nrm", (16, 2 * S), F32, stack=stA)
    vtv = ctx.sbuf(u + "_vtv", (128, 512), BF16, stack=stA)
    aq_sb = ctx.sbuf(u + "_aq", (16, S), F32, stack=stA) if alibi else None
    s_mmA = ctx.sem(su + "_mmA")
    s_evA = ctx.sem(su + "_evA")
    s_sqA = ctx.sem(su + "_sqA")
    s_indA = ctx.sem(su + "_indA")
    s_nrm = ctx.sem(su + "_nrm")
    s_oA = ctx.sem(su + "_oA")
    s_tvA = ctx.sem(su + "_tvA")
    ctx.step("vector", lambda e: e.memset(nrm[:], 0.0), waits=list(prev),
             inc=s_nrm)
    if alibi:
        ctx.dma("sync", aq_sb[:], W["alibi_q"][:], inc=s_c)
    slot_free = [None] * 4
    for wi, wname in enumerate(("WqT", "WkT", "WvT")):
        dst_d = (qT_d, kT_d, None)[wi]
        evsc = (1.0, cfg["kscale"], cfg["vscale"])[wi]
        bcol0 = (0, 1, 2)[wi] * ND
        for oc in range(ND):
            buf, rdy, idx = ws.fetch(W[wname][:, oc * 128:(oc + 1) * 128], D,
                                     True)
            last = None
            for nu in range(NTC):
                bi, bank, bfree = ppA.acquire()
                def mmA(e, buf=buf, nu=nu, bank=bank):
                    for c in range(ND):
                        ins = e.matmul(
                            out=bank[:], lhsT=buf[:, c * 128:(c + 1) * 128],
                            rhs=g.xbf[:, c * S + nu * 512:c * S + (nu + 1) * 512],
                            start=(c == 0), stop=(c == ND - 1))
                    return ins
                ctx.step("tensor", mmA,
                         waits=[rdy] + ([bfree] if bfree else []), inc=s_mmA)
                last = s_mmA.now
                sl = (wi * NTC * ND + oc * NTC + nu) % 4
                sslot = stg[:, sl * 512:(sl + 1) * 512]
                wst = [last, c_rdy]
                if slot_free[sl] is not None:
                    wst += slot_free[sl]
                bcol = (bias_sb[:, bcol0 + oc:bcol0 + oc + 1]
                        if cfg["qkv_bias"] else None)
                ctx.step("scalar",
                         lambda e, bank=bank, sslot=sslot, bcol=bcol,
                         evsc=evsc: e.activation(
                             out=sslot, in_=bank[:], func=AF.Identity,
                             **({"bias": bcol} if bcol is not None else {}),
                             scale=evsc),
                         waits=wst, inc=s_evA)
                ppA.set_free(bi, s_evA.now)
                evv = s_evA.now
                if wi < 2:
                    ctx.dma("sync",
                            dst_d[oc * 128:(oc + 1) * 128,
                                  nu * 512:(nu + 1) * 512], sslot,
                            waits=[evv], inc=s_oA)
                    ctx.step("scalar",
                             lambda e, sslot=sslot: e.activation(
                                 out=sqg[:], in_=sslot, func=AF.Square),
                             waits=[evv, (s_indA, s_indA.n)], inc=s_sqA)
                    ctx.step("tensor",
                             lambda e, oc=oc: e.matmul(
                                 out=ps_ind[:],
                                 lhsT=ind_sb[:, oc * 16:(oc + 1) * 16],
                                 rhs=sqg[:], start=True, stop=True),
                             waits=[s_sqA.now, (s_nrm, s_nrm.n)], inc=s_indA)
                    nslot = nrm[:, wi * S + nu * 512: wi * S + (nu + 1) * 512]
                    ctx.step("vector",
                             lambda e, nslot=nslot: e.tensor_tensor(
                                 out=nslot, in0=nslot, in1=ps_ind[:],
                                 op=ALU.add),
                             waits=[s_indA.now], inc=s_nrm)
                    slot_free[sl] = [(s_oA, s_oA.n), (s_sqA, s_sqA.n)]
                else:
                    ctx.step("tensor",
                             lambda e, sslot=sslot: [
                                 e.transpose(
                                     out=ps_tv[:, tt * 128:(tt + 1) * 128],
                                     in_=sslot[:, tt * 128:(tt + 1) * 128],
                                     identity=g.ident_bf[:])
                                 for tt in range(4)][-1],
                             waits=[evv, g.ident_done, (s_tvA, s_tvA.n)],
                             inc=s_tvA)
                    ctx.step("vector",
                             lambda e: e.tensor_copy(out=vtv[:], in_=ps_tv[:]),
                             waits=[s_tvA.now, (s_oA, s_oA.n)], inc=s_tvA)
                    tvv = (s_tvA, s_tvA.n)
                    for hh in range(2):
                        hidx = 2 * oc + hh
                        ctx.dma("sync",
                                v_d[hidx, nu * 512:(nu + 1) * 512, :]
                                .rearrange("(tt p) c -> p tt c", p=128),
                                vtv[:].rearrange("p (tt c) -> p tt c", c=128)
                                [:, :, hh * 64:(hh + 1) * 64],
                                waits=[tvv], inc=s_oA)
                    slot_free[sl] = [tvv]
            ws.release(idx, last)
    projA_done = [(s_oA, s_oA.n), (s_nrm, s_nrm.n)]
    brow = ctx.sbuf(u + "_brw", (16, S), F32, stack=stA)
    nkm = ctx.sbuf(u + "_nkm", (16, 2), F32, stack=stA)
    s_bnd = ctx.sem(su + "_bnd")
    ctx.step("scalar", lambda e: e.activation(
        out=nrm[:, 0:S], in_=nrm[:, 0:S], func=AF.Sqrt), waits=projA_done,
        inc=s_bnd)
    ctx.step("scalar", lambda e: e.activation(
        out=nrm[:, S:2 * S], in_=nrm[:, S:2 * S], func=AF.Sqrt), inc=s_bnd)
    ctx.step("vector", lambda e: e.tensor_reduce(
        out=nkm[:, 0:1], in_=nrm[:, S:2 * S], axis=AX, op=ALU.max),
        waits=[s_bnd.now], inc=s_bnd)
    ctx.step("vector", lambda e: e.tensor_scalar(
        out=brow[:], in0=nrm[:, 0:S], scalar1=nkm[:, 0:1], scalar2=1.0,
        op0=ALU.mult, op1=ALU.add), waits=[s_bnd.now], inc=s_bnd)
    if alibi:
        ctx.step("vector", lambda e: e.tensor_tensor(
            out=brow[:], in0=brow[:], in1=aq_sb[:], op=ALU.add),
            waits=[s_bnd.now, s_c.now], inc=s_bnd)
    ctx.dma("sync", b_d[:], brow[:], waits=[s_bnd.now], inc=s_bnd)
    bnd_done = (s_bnd, s_bnd.n)
    stA.close()

    stB = contextlib.ExitStack()
    kh = [ctx.sbuf(u + f"_kh{i}", (KDIM, S), BF16, stack=stB) for i in range(2)]
    rh = [ctx.sbuf(u + f"_rh{i}", (KDIM, S), BF16, stack=stB) for i in range(2)]
    vh = [ctx.sbuf(u + f"_vh{i}", (128, NT * HD), BF16, stack=stB)
          for i in range(2)]
    pt = ctx.sbuf(u + "_pt", (128, 2048), BF16, stack=stB)
    lr = ctx.sbuf(u + "_lr", (1, 2 * 1024), F32, stack=stB)
    rlb = ctx.sbuf(u + "_rlb", (128, 512), F32, stack=stB)
    ps_sc = ctx.psum(u + "_psS", (128, 2048), F32, stack=stB)
    ppV = PsumPool(ctx, u + "_psV", 2, (128, 512), stack=stB)
    s_kv = ctx.sem(su + "_kv")
    s_mmS = ctx.sem(su + "_mmS")
    s_exp = ctx.sem(su + "_exp")
    s_msk = ctx.sem(su + "_msk")
    s_av = ctx.sem(su + "_av")
    s_lmm = ctx.sem(su + "_lmm")
    s_le = ctx.sem(su + "_le")
    s_rl = ctx.sem(su + "_rl")
    s_oe = ctx.sem(su + "_oe")
    for pair in range(H // 2):
        hA, hB = 2 * pair, 2 * pair + 1
        ldw = [bnd_done] if pair == 0 else [(s_mmS, s_mmS.n), (s_av, s_av.n),
                                            (s_lmm, s_lmm.n)]
        for hh, hidx in ((0, hA), (1, hB)):
            ctx.dma("sync", kh[hh][0:64, :],
                    kT_d[64 * hidx:64 * hidx + 64, :], waits=ldw, inc=s_kv)
            ctx.step("vector",
                     lambda e, hh=hh: e.memset(kh[hh][64:65, :], -1.0),
                     waits=ldw, inc=s_kv)
            if alibi:
                ctx.dma("sync", kh[hh][65:66, :],
                        W["alibi_k"][hidx:hidx + 1, :], waits=ldw, inc=s_kv)
                ctx.dma("sync", kh[hh][66:67, :],
                        W["alibi_k_lo"][hidx:hidx + 1, :], waits=ldw, inc=s_kv)
            ctx.dma("sync", rh[hh][0:64, :],
                    qT_d[64 * hidx:64 * hidx + 64, :], waits=ldw, inc=s_kv)
            ctx.dma("gpsimd", rh[hh][64:65, :], b_d[hidx:hidx + 1, :],
                    waits=ldw, inc=s_kv)
            if alibi:
                ctx.dma("sync", rh[hh][65:66, :], W["ones_row"][:],
                        waits=ldw, inc=s_kv)
                ctx.dma("sync", rh[hh][66:67, :], W["ones_row"][:],
                        waits=ldw, inc=s_kv)
            ctx.dma("sync",
                    vh[hh][:].rearrange("p (kt c) -> p kt c", c=HD),
                    v_d[hidx].rearrange("(kt p) c -> p kt c", p=128),
                    waits=ldw, inc=s_kv)
        kv_rdy = s_kv.now
        for qc in range(NTC):
            nkt = 4 * (qc + 1)
            li_A, lbank_A = None, ep.ps_stat[:, 0:512]
            lbank_B = ep.ps_stat[:, 512:1024]
            bi, avbank, avfree = ppV.acquire()
            for kt in range(nkt):
                half = kt % 2
                scA = ps_sc[:, half * 1024:half * 1024 + 512]
                scB = ps_sc[:, half * 1024 + 512:half * 1024 + 1024]
                wsc = [kv_rdy]
                if kt >= 2:
                    wsc.append((s_av, s_av.n - 1))
                ctx.step("tensor",
                         lambda e, scA=scA, scB=scB, kt=kt, qc=qc: (
                             e.matmul(out=scA,
                                      lhsT=kh[0][:, kt * 128:(kt + 1) * 128],
                                      rhs=rh[0][:, qc * 512:(qc + 1) * 512],
                                      start=True, stop=True),
                             e.matmul(out=scB,
                                      lhsT=kh[1][:, kt * 128:(kt + 1) * 128],
                                      rhs=rh[1][:, qc * 512:(qc + 1) * 512],
                                      start=True, stop=True))[-1],
                         waits=wsc, inc=s_mmS)
                mmv = s_mmS.now
                pth = pt[:, half * 1024:(half + 1) * 1024]
                ctx.step("scalar",
                         lambda e, pth=pth, half=half: e.activation(
                             out=pth, in_=ps_sc[:, half * 1024:
                                             (half + 1) * 1024],
                             func=AF.Exp),
                         waits=[mmv, (s_av, max(0, s_av.n - 1)),
                                (s_lmm, max(0, s_lmm.n - 1))], inc=s_exp)
                expv = s_exp.now
                if kt >= 4 * qc:
                    ctx.step("gpsimd",
                             lambda e, pth=pth, qc=qc, kt=kt: (
                                 e.affine_select(
                                     out=pth[:, 0:512], in_=pth[:, 0:512],
                                     pattern=[[1, 512]],
                                     base=qc * 512 - kt * 128,
                                     channel_multiplier=-1,
                                     compare_op=ALU.is_ge, fill=0.0),
                                 e.affine_select(
                                     out=pth[:, 512:1024], in_=pth[:, 512:1024],
                                     pattern=[[1, 512]],
                                     base=qc * 512 - kt * 128,
                                     channel_multiplier=-1,
                                     compare_op=ALU.is_ge, fill=0.0))[-1],
                             waits=[expv], inc=s_msk)
                    mskv = s_msk.now
                else:
                    mskv = expv
                ctx.step("tensor",
                         lambda e, ptA=pth[:, 0:512], ptB=pth[:, 512:1024],
                         avbank=avbank, kt=kt, nkt=nkt, lbank_A=lbank_A,
                         lbank_B=lbank_B: (
                             e.matmul(out=avbank[0:64, :],
                                      lhsT=vh[0][:, kt * HD:(kt + 1) * HD],
                                      rhs=ptA, start=(kt == 0),
                                      stop=(kt == nkt - 1),
                                      tile_position=(0, 0)),
                             e.matmul(out=avbank[64:128, :],
                                      lhsT=vh[1][:, kt * HD:(kt + 1) * HD],
                                      rhs=ptB, start=(kt == 0),
                                      stop=(kt == nkt - 1),
                                      tile_position=(0, 64)))[-1],
                         waits=[mskv] + ([avfree] if avfree and kt == 0
                                         else []), inc=s_av)
                ctx.step("tensor",
                         lambda e, ptA=pth[:, 0:512], ptB=pth[:, 512:1024],
                         kt=kt, nkt=nkt, lbank_A=lbank_A, lbank_B=lbank_B: (
                             e.matmul(out=lbank_A, lhsT=g.ones_col_bf[:],
                                      rhs=ptA, start=(kt == 0),
                                      stop=(kt == nkt - 1)),
                             e.matmul(out=lbank_B, lhsT=g.ones_col_bf[:],
                                      rhs=ptB, start=(kt == 0),
                                      stop=(kt == nkt - 1)))[-1],
                         waits=([(s_le, s_le.n)] if kt == 0 else []),
                         inc=s_lmm)
            avv = s_av.now
            lmmv = s_lmm.now
            ctx.step("scalar", lambda e: (
                e.copy(out=lr[0:1, 0:512], in_=lbank_A),
                e.copy(out=lr[0:1, 512:1024], in_=lbank_B))[-1],
                waits=[lmmv, (s_rl, s_rl.n)], inc=s_le)
            ctx.step("vector", lambda e: e.reciprocal(
                out=lr[0:1, 1024:2048], in_=lr[0:1, 0:1024]),
                waits=[s_le.now], inc=s_le)
            ctx.dma("sync", lrow_d.rearrange("a b -> (a b)").unsqueeze(0),
                    lr[0:1, 1024:2048], waits=[s_le.now], inc=s_le)
            ctx.dma("sync", rlb[0:64, :],
                    lrow_d[0:1, :].partition_broadcast(64),
                    waits=[(s_le, s_le.n), (s_oe, s_oe.n)], inc=s_rl)
            ctx.dma("sync", rlb[64:128, :],
                    lrow_d[1:2, :].partition_broadcast(64),
                    waits=[(s_le, s_le.n)], inc=s_rl)
            rlv = s_rl.now
            xdst = g.xbf[:, pair * S + qc * 512: pair * S + (qc + 1) * 512]
            ctx.step("vector",
                     lambda e, avbank=avbank, xdst=xdst: e.tensor_tensor(
                         out=xdst, in0=avbank[:], in1=rlb[:], op=ALU.mult),
                     waits=[rlv, avv], inc=s_oe)
            ppV.set_free(bi, s_oe.now)
    phaseB_done = (s_oe, s_oe.n)
    stB.close()

    stC = contextlib.ExitStack()
    wsC = WStream(ctx, g.wcol[0:2], first_wait=[phaseB_done], name=su + "_wsC")
    ppC = PsumPool(ctx, u + "_psC", 2, (128, TC), stack=stC)
    s_mmC = ctx.sem(su + "_mmC")
    ep.set_io(io["x_in"], io["x_out"])
    for tc in range(NTC):
        for dc in range(ND):
            res_info = ep.load_res(tc, dc)
            buf, rdy, idx = wsC.fetch(W["WoT"][:, dc * 128:(dc + 1) * 128], D,
                                      True)
            bi, bank, bfree = ppC.acquire()
            def mmC(e, buf=buf, tc=tc, bank=bank):
                for c in range(ND):
                    ins = e.matmul(
                        out=bank[:], lhsT=buf[:, c * 128:(c + 1) * 128],
                        rhs=g.xbf[:, c * S + tc * TC:c * S + (tc + 1) * TC],
                        start=(c == 0), stop=(c == ND - 1))
                return ins
            ctx.step("tensor", mmC, waits=[rdy] + ([bfree] if bfree else []),
                     inc=s_mmC)
            wsC.release(idx, s_mmC.now)
            bcol = bias_sb[:, 3 * ND + dc:3 * ND + dc + 1] if cfg["qkv_bias"] \
                else None
            ev = ep.consume(tc, dc, bank[:], bcol, [s_mmC.now, c_rdy],
                            res_info, scale=cfg["oscale"])
            ppC.set_free(bi, ev)
        ep.finish(tc)
    stC.close()
    return ep.block_done()


def _col(v):
    return np.ascontiguousarray(v.reshape(-1, 128).T)


def _cT(a):
    return np.ascontiguousarray(a.T)


def _declare_block(ctx, bi, inputs):
    P = f"b{bi}_"
    if bi == 0:
        W = {}
        for nm in ("W1", "W2", "Wz", "Wh", "Wo"):
            W[nm] = ctx.din(P + nm, inputs[f"L0_attn_{nm}"], F32)
        for nm in ("b1", "b2", "bz", "bh", "bo"):
            W[nm] = ctx.din(P + nm, _col(inputs[f"L0_attn_{nm}"]), F32)
        return lambda c, g, e, io: emit_l0_attn(c, g, e, io, W)
    if bi == 1:
        W = dict(weight=ctx.din(P + "w", inputs["L0_ffn_weight"], F32),
                 bias=ctx.din(P + "b", _col(inputs["L0_ffn_bias"]), F32))
        return lambda c, g, e, io: emit_l0_ffn(c, g, e, io, W)
    if bi == 2:
        ac = attn_host_consts(False)
        W = dict(
            WqT=ctx.din(P + "wq", _cT(inputs["L1_attn_W_query"]), F32),
            WkT=ctx.din(P + "wk", _cT(inputs["L1_attn_W_key"]), F32),
            WvT=ctx.din(P + "wv", _cT(inputs["L1_attn_W_value"]), F32),
            WoT=ctx.din(P + "wo", _cT(inputs["L1_attn_W_proj"]), F32),
            ind=ctx.din(P + "ind", ac["ind"], BF16),
            cmask=ctx.din(P + "cm", ac["cmask"], BF16),
            ones_row=ctx.din(P + "or", np.ones((1, S), dtype=BF), BF16),
        )
        cfg = dict(name=P + "a", kscale=2.0 ** -13, vscale=1.0 / 32.0,
                   oscale=1.0 / 32.0, qkv_bias=False, alibi=False)
        return lambda c, g, e, io: emit_attn(c, g, e, io, W, cfg)
    if bi == 3:
        W = dict(
            ln_gamma=ctx.din(P + "g", _col(inputs["L1_ffn_ln_gamma"]), F32),
            ln_beta=ctx.din(P + "be", _col(inputs["L1_ffn_ln_beta"]), F32),
            fc1_q=ctx.din(P + "f1q", inputs["L1_ffn_fc1_q"], I32),
            fc1_scales=ctx.din(P + "f1s", np.ascontiguousarray(
                np.repeat(inputs["L1_ffn_fc1_scales"], 128, axis=0)), F32),
            fc1_bias=ctx.din(P + "f1b", _col(inputs["L1_ffn_fc1_bias"]), F32),
            fc2_q=ctx.din(P + "f2q", inputs["L1_ffn_fc2_q"], I32),
            fc2_scales=ctx.din(P + "f2s", np.ascontiguousarray(
                np.repeat(inputs["L1_ffn_fc2_scales"], 128, axis=0)), F32),
            fc2_bias=ctx.din(P + "f2b", _col(inputs["L1_ffn_fc2_bias"]), F32),
        )
        return lambda c, g, e, io: emit_l1_ffn(c, g, e, io, W)
    if bi == 4:
        W = dict(
            W_inT=ctx.din(P + "wi", _cT(inputs["L2_attn_W_in"]), F32),
            W_outT=ctx.din(P + "wo", _cT(inputs["L2_attn_W_out"]), F32),
            log_A=ctx.din(P + "la", inputs["L2_attn_log_A"].reshape(64, 1), F32),
            B_param=ctx.din(P + "bp", inputs["L2_attn_B_param"].reshape(64, 1),
                            F32),
            C_param=ctx.din(P + "cp", inputs["L2_attn_C_param"].reshape(64, 1),
                            F32),
            log_dt=ctx.din(P + "ld", inputs["L2_attn_log_dt"].reshape(1, 1),
                           F32),
        )
        return lambda c, g, e, io: emit_l2_attn(c, g, e, io, W)
    if bi == 5:
        co = l2ffn_coeffs(inputs["L2_ffn_up_weights"])
        cd = l2ffn_coeffs(inputs["L2_ffn_down_weights"])
        aug = np.ascontiguousarray(np.concatenate(
            [_cT(inputs["L2_ffn_W_exp"]),
             np.asarray(inputs["L2_ffn_b_exp"]).reshape(-1, 1)], axis=1))
        W = dict(
            WexpT_aug=ctx.din(P + "wea", aug, F32),
            W_con=ctx.din(P + "wc", inputs["L2_ffn_W_con"], F32),
            masks=ctx.din(P + "msk", l2ffn_masks(), BF16),
            up_bias=ctx.din(P + "ub", _col(inputs["L2_ffn_up_bias"]), F32),
            b_con=ctx.din(P + "bc", _col(inputs["L2_ffn_b_con"]), F32),
            down_bias=ctx.din(P + "db", _col(inputs["L2_ffn_down_bias"]), F32),
        )
        for pre, cc in (("up", co), ("dn", cd)):
            for k in ("lowD", "lowO", "lowOT", "highD", "highO", "highOT"):
                if pre == "up" and k == "highOT":
                    continue
                if pre == "dn" and k == "highO":
                    continue
                W[f"{pre}_{k}"] = ctx.din(P + pre + k, cc[k], F32)
        return lambda c, g, e, io: emit_l2_ffn(c, g, e, io, W)
    if bi == 6:
        ac = attn_host_consts(True)
        W = dict(
            WqT=ctx.din(P + "wq", _cT(inputs["L3_attn_Wq"]), F32),
            WkT=ctx.din(P + "wk", _cT(inputs["L3_attn_Wk"]), F32),
            WvT=ctx.din(P + "wv", _cT(inputs["L3_attn_Wv"]), F32),
            WoT=ctx.din(P + "wo", _cT(inputs["L3_attn_Wo"]), F32),
            bq=ctx.din(P + "bq", _col(inputs["L3_attn_bq"]), F32),
            bk=ctx.din(P + "bk", _col(inputs["L3_attn_bk"]), F32),
            bv=ctx.din(P + "bv", _col(inputs["L3_attn_bv"]), F32),
            bo=ctx.din(P + "bo", _col(inputs["L3_attn_bo"]), F32),
            ind=ctx.din(P + "ind", ac["ind"], BF16),
            cmask=ctx.din(P + "cm", ac["cmask"], BF16),
            ones_row=ctx.din(P + "or", np.ones((1, S), dtype=BF), BF16),
            alibi_k=ctx.din(P + "ak", ac["alibi_k"], BF16),
            alibi_k_lo=ctx.din(P + "akl", ac["alibi_k_lo"], BF16),
            alibi_q=ctx.din(P + "aq", ac["alibi_q"], F32),
        )
        cfg = dict(name=P + "a", kscale=1.0 / 8.0, vscale=1.0, oscale=None,
                   qkv_bias=True, alibi=True)
        return lambda c, g, e, io: emit_attn(c, g, e, io, W, cfg)
    if bi == 7:
        W = dict(
            wgT=ctx.din(P + "wg", _cT(inputs["L3_ffn_w_gate"]), I32),
            wuT=ctx.din(P + "wu", _cT(inputs["L3_ffn_w_up"]), I32),
            wdT=ctx.din(P + "wd", _cT(inputs["L3_ffn_w_down"]), I32),
        )
        return lambda c, g, e, io: emit_l3_ffn(c, g, e, io, W)
    raise ValueError(bi)


FUSED = True


def _build_block_program(bi, inputs):
    ctx = Ctx()
    g = Globals(ctx)
    ep = LNEpilogue(ctx, g)
    x_in = ctx.din("x_in", np.zeros((D, S), np.float32), F32)
    x_out = ctx.dout("x_out", (D, S), F32)
    emit = _declare_block(ctx, bi, inputs)
    ctx.begin_block_scope()
    s_x = emit_load_xbf(ctx, g, x_in)
    done = emit(ctx, g, ep, dict(x_in=x_in, x_out=x_out, xbf_ready=s_x.now,
                                 prev_done=[s_x.now]))
    nc = ctx.finalize(final_waits=done)
    return nc, ctx.host


def _build_fused_program(inputs):
    ctx = Ctx()
    g = Globals(ctx)
    ep = LNEpilogue(ctx, g)
    x_in = ctx.din("x_in", np.zeros((D, S), np.float32), F32)
    x_out = ctx.dout("x_out", (D, S), F32)
    xm = [ctx.dscratch(f"xmid{i}", (D, S), F32) for i in range(7)]
    chain = [x_in] + xm + [x_out]
    emits = [_declare_block(ctx, bi, inputs) for bi in range(8)]
    s_x = emit_load_xbf(ctx, g, x_in)
    done = [s_x.now]
    xbf_ready = s_x.now
    for bi in range(8):
        ctx.begin_block_scope()
        done = emits[bi](ctx, g, ep, dict(
            x_in=chain[bi], x_out=chain[bi + 1], xbf_ready=xbf_ready,
            prev_done=done))
        xbf_ready = done[1]
    nc = ctx.finalize(final_waits=done)
    return nc, ctx.host


def kernel(**inputs):
    inputs = {k: np.asarray(v) for k, v in inputs.items()}
    xs = [np.ascontiguousarray(inputs["L0_attn_x"][b].T) for b in range(B)]
    if FUSED:
        nc, host = _build_fused_program(inputs)
        base = dict(host)
        in_maps = [dict(base, x_in=xs[c]) for c in range(N_CORES)]
        res = run_bass_kernel_spmd(nc, in_maps, core_ids=list(range(N_CORES)))
        outs = [np.asarray(res.results[c]["x_out"]) for c in range(N_CORES)]
    else:
        cur = xs
        for bi in range(8):
            nc, host = _build_block_program(bi, inputs)
            base = dict(host)
            in_maps = [dict(base, x_in=np.ascontiguousarray(cur[c]))
                       for c in range(N_CORES)]
            res = run_bass_kernel_spmd(nc, in_maps,
                                       core_ids=list(range(N_CORES)))
            cur = [np.asarray(res.results[c]["x_out"])
                   for c in range(N_CORES)]
        outs = cur
    return np.ascontiguousarray(
        np.stack([o.T for o in outs], axis=0)).astype(np.float32)


# version 07 (final)
# speedup vs baseline: 1.0636x; 1.0533x over previous
import contextlib
import math
import numpy as np
import ml_dtypes
import concourse.bass as bass
import concourse.mybir as mybir
from concourse.bass_utils import run_bass_kernel_spmd

F32 = mybir.dt.float32
BF16 = mybir.dt.bfloat16
I32 = mybir.dt.int32
AF = mybir.ActivationFunctionType
ALU = mybir.AluOpType
AX = mybir.AxisListType.X

B, S, D = 8, 4096, 1024
ND = D // 128
EXP = 4096
NE = EXP // 128
TC = 512
NTC = S // TC
BF = ml_dtypes.bfloat16
H = 16
HD = 64
NT = S // 128

N_CORES = 8


class Sem:
    __slots__ = ("h", "n")

    def __init__(self, stack, nc, name):
        self.h = stack.enter_context(nc.semaphore(name))
        self.n = 0

    def bump(self, k=1):
        self.n += k
        return self.n

    @property
    def now(self):
        return (self, self.n)


class Ctx:
    def __init__(self):
        self.nc = bass.Bass("TRN2", target_bir_lowering=False, debug=False)
        self.stack = contextlib.ExitStack()
        self.q = {"sync": [], "vector": [], "scalar": [], "tensor": [], "gpsimd": []}
        self.sems = {}
        self.host = {}
        self._uid = 0

    def uid(self):
        self._uid += 1
        return self._uid

    def begin_block_scope(self):
        self._blk_ctr = 0

    _BLK_PREFIX = ("l0a", "l0f", "l1f", "l2a", "l2f", "l3f", "at_")

    def sem(self, name):
        if name.startswith(self._BLK_PREFIX):
            if name not in getattr(self, "_alias", {}):
                if not hasattr(self, "_alias"):
                    self._alias = {}
                self._alias[name] = f"blk{self._blk_ctr}"
                self._blk_ctr += 1
            name = self._alias[name]
        if name not in self.sems:
            self.sems[name] = Sem(self.stack, self.nc, name)
        return self.sems[name]

    def din(self, name, arr, dt):
        self.host[name] = np.ascontiguousarray(arr)
        return self.nc.dram_tensor(name, list(arr.shape), dt,
                                   kind="ExternalInput").ap()

    def dout(self, name, shape, dt):
        return self.nc.dram_tensor(name, list(shape), dt, kind="ExternalOutput").ap()

    def dscratch(self, name, shape, dt):
        return self.nc.dram_tensor(name, list(shape), dt, kind="Internal").ap()

    def sbuf(self, name, shape, dt, stack=None):
        cm = self.nc.sbuf_tensor(name, list(shape), dt)
        return (stack or self.stack).enter_context(cm)

    def psum(self, name, shape, dt=F32, stack=None):
        cm = self.nc.psum_tensor(name, list(shape), dt)
        return (stack or self.stack).enter_context(cm)

    def step(self, eng, fn, waits=(), inc=None, inck=1):
        wlist = [(s.h, v) for (s, v) in waits if v > 0]
        semh = None
        if inc is not None:
            inc.bump(inck)
            semh = inc.h

        def go(e, wlist=wlist, fn=fn, semh=semh, inck=inck):
            for h_, v in wlist:
                e.wait_ge(h_, v)
            ins = fn(e)
            if semh is not None:
                ins.then_inc(semh, inck)
        self.q[eng].append(go)

    def dma(self, queue, dst, src, waits=(), inc=None):
        w = list(waits)
        if inc is not None and inc.n > 0:
            w.append((inc, inc.n))
        self.step(queue, lambda e: e.dma_start(out=dst, in_=src),
                  waits=w, inc=inc, inck=16)

    def finalize(self, final_waits=()):
        for s, v in final_waits:
            self.q["sync"].append(lambda e, h=s.h, v=v: e.wait_ge(h, v))
        nc = self.nc
        with nc.Block() as block:
            for name, starter in (("sync", block.sync), ("vector", block.vector),
                                  ("scalar", block.scalar), ("tensor", block.tensor),
                                  ("gpsimd", block.gpsimd)):
                fns = self.q[name]
                if fns:
                    starter(lambda e, fns=fns: [f(e) for f in fns])
        self.stack.close()
        return nc


class PsumPool:

    def __init__(self, ctx, name, n, shape, stack=None):
        self.banks = [ctx.psum(f"{name}{i}", shape, stack=stack) for i in range(n)]
        self.free = [None] * n
        self.rr = 0
        self.n = n

    def acquire(self):
        idx = self.rr % self.n
        self.rr += 1
        return idx, self.banks[idx], self.free[idx]

    def set_free(self, idx, semval):
        self.free[idx] = semval


class WStream:

    def __init__(self, ctx, bufs, first_wait=(), name="ws_ld_shared"):
        self.ctx = ctx
        self.bufs = bufs
        self.nbuf = len(bufs)
        self.s_ld = ctx.sem(name)
        self.free_at = [None] * self.nbuf
        self.first_wait = list(first_wait)
        self.i = 0

    def fetch(self, src_ap, rows, cast, extra_waits=()):
        ctx = self.ctx
        idx = self.i % self.nbuf
        kc = rows // 128
        cols = src_ap.shape[1]
        buf = self.bufs[idx]
        waits = list(extra_waits)
        if self.i < self.nbuf:
            waits += self.first_wait
        if self.free_at[idx] is not None:
            waits.append(self.free_at[idx])
        self.i += 1
        src3 = src_ap.rearrange("(kc p) gcol -> p kc gcol", p=128)
        q = "gpsimd" if cast else "sync"
        ctx.dma(q, buf[:, 0:kc * cols].rearrange("p (kc c) -> p kc c", kc=kc),
                src3, waits=waits, inc=self.s_ld)
        return buf, self.s_ld.now, idx

    def release(self, idx, sem_val):
        self.free_at[idx] = sem_val


class Globals:
    def __init__(self, ctx: Ctx):
        self.ctx = ctx
        self.xbf = ctx.sbuf("XBF", (128, ND * S), BF16)
        self.wcol = [ctx.sbuf(f"WCOL{i}", (128, 4096), BF16) for i in range(3)]
        self.ones_col_bf = ctx.sbuf("ones_col_bf", (128, 1), BF16)
        self.ones_row_f = ctx.sbuf("ones_row_f", (1, 128), F32)
        self.eps_col = ctx.sbuf("eps_col", (1, 1), F32)
        self.ident_bf = ctx.sbuf("ident_bf", (128, 128), BF16)
        self.ident_f32 = ctx.sbuf("ident_f32", (128, 128), F32)
        idb = ctx.din("g_ident_bf", np.eye(128, dtype=BF), BF16)
        idf = ctx.din("g_ident_f32", np.eye(128, dtype=np.float32), F32)
        si = ctx.sem("g_ident")
        ctx.dma("sync", self.ident_bf[:], idb[:], inc=si)
        ctx.dma("sync", self.ident_f32[:], idf[:], inc=si)
        self.ident_done = si.now
        s = ctx.sem("g_init")
        ctx.step("vector", lambda e: e.memset(self.ones_col_bf[:], 1.0))
        ctx.step("vector", lambda e: e.memset(self.ones_row_f[:], 1.0))
        ctx.step("vector", lambda e: e.memset(self.eps_col[:], 1e-5), inc=s)
        self.init_done = s.now


def emit_load_xbf(ctx, g, x_src_fp32):
    u = ctx.uid()
    s_ld = ctx.sem(f"xbf_ld_{u}")
    ctx.dma("gpsimd",
            g.xbf[:].rearrange("p (c t) -> p c t", c=ND),
            x_src_fp32.rearrange("(c p) t -> p c t", p=128),
            inc=s_ld)
    return s_ld


class LNEpilogue:
    def __init__(self, ctx: Ctx, g):
        self.ctx, self.g = ctx, g
        self.xr = ctx.sbuf("ep_xr", (128, ND * TC), F32)
        self.res = ctx.sbuf("ep_res", (128, 4 * TC), F32)
        self.sh = ctx.sbuf("ep_sh", (128, 2 * TC), BF16)
        self.sq = ctx.sbuf("ep_sq", (128, 2 * TC), BF16)
        self.rows = ctx.sbuf("ep_rows", (1, 3 * TC), F32)
        self.bct = ctx.sbuf("ep_bct", (128, 2 * TC), F32)
        self.tmp = ctx.sbuf("ep_tmp", (128, TC), F32)
        self.ps_stat = ctx.psum("ep_ps_stat", (1, 2 * TC))
        self.rows_d = ctx.dscratch("ep_rows_d", (1, 2 * TC), F32)
        self.s_res = ctx.sem("ep_res")
        self.s_ev = ctx.sem("ep_ev")
        self.s_sq = ctx.sem("ep_sq")
        self.s_stp = ctx.sem("ep_stp")
        self.s_row = ctx.sem("ep_row")
        self.s_bc = ctx.sem("ep_bc")
        self.s_ap = ctx.sem("ep_ap")
        self.s_tmp = ctx.sem("ep_tmp")
        self.s_out = ctx.sem("ep_out")
        self.s_xbf = ctx.sem("ep_xbf")
        self.gi = 0
        self.x_in = None
        self.x_out = None
        self._pending = None
        self.debug_dram = None
        self.debug_dram2 = None

    def set_io(self, x_in_dram, x_out_dram):
        self.x_in, self.x_out = x_in_dram, x_out_dram

    def load_res(self, tc, dc):
        ctx = self.ctx
        k = self.gi * ND + dc
        slot = k % 4
        dst = self.res[:, slot * TC:(slot + 1) * TC]
        waits = [(self.s_ev, k - 3)] if k >= 4 else []
        ctx.dma("scalar", dst,
                self.x_in[dc * 128:(dc + 1) * 128, tc * TC:(tc + 1) * TC],
                waits=waits, inc=self.s_res)
        return dst, self.s_res.now

    def consume(self, tc, dc, psum_ap, bias_col, waits, res_info, scale=None):
        ctx, g = self.ctx, self.g
        gi = self.gi
        res, res_rdy = res_info
        xr = self.xr[:, dc * TC:(dc + 1) * TC]
        w = list(waits) + [res_rdy]
        if dc == 0 and gi >= 1:
            w.append((self.s_out, 16 * ND * gi))
            w.append((self.s_xbf, ND * gi))
        if scale is not None:
            ctx.step("vector",
                     lambda e: e.scalar_tensor_tensor(
                         out=xr, in0=psum_ap, scalar=scale, in1=res,
                         op0=ALU.mult, op1=ALU.add), waits=w,
                     inc=(None if bias_col is not None else self.s_ev))
            if bias_col is not None:
                ctx.step("vector",
                         lambda e: e.tensor_scalar(
                             out=xr, in0=xr, scalar1=bias_col, scalar2=None,
                             op0=ALU.add), inc=self.s_ev)
        else:
            bc = bias_col if bias_col is not None else 0.0
            ctx.step("vector",
                     lambda e: e.scalar_tensor_tensor(
                         out=xr, in0=psum_ap, scalar=bc, in1=res,
                         op0=ALU.add, op1=ALU.add), waits=w, inc=self.s_ev)
        ev = self.s_ev.now
        if self.debug_dram is not None:
            ctx.dma("gpsimd",
                    self.debug_dram[dc * 128:(dc + 1) * 128,
                                    tc * TC:(tc + 1) * TC],
                    xr, waits=[ev], inc=self.s_res)
        k = gi * ND + dc
        slot = k % 2
        sh = self.sh[:, slot * TC:(slot + 1) * TC]
        sq = self.sq[:, slot * TC:(slot + 1) * TC]
        wsh = [ev] + ([(self.s_stp, k - 1)] if k >= 2 else [])
        ctx.step("gpsimd", lambda e: e.tensor_copy(out=sh, in_=xr),
                 waits=wsh, inc=self.s_sq)
        ctx.step("scalar", lambda e: e.activation(out=sq, in_=xr, func=AF.Square),
                 waits=wsh, inc=self.s_sq)
        sqv = self.s_sq.now
        w2 = [sqv]
        if dc == 0 and gi >= 1:
            w2.append((self.s_row, 3 * gi))
        self.flush_stats()
        self._pending = (sh, sq, dc, w2)
        return ev

    def flush_stats(self):
        if self._pending is None:
            return
        sh, sq, dc, w2 = self._pending
        self._pending = None
        ctx, g = self.ctx, self.g
        ctx.step("tensor",
                 lambda e, sh=sh, sq=sq, dc=dc: (
                     e.matmul(out=self.ps_stat[:, 0:TC], lhsT=g.ones_col_bf[:],
                              rhs=sh, start=(dc == 0), stop=(dc == ND - 1)),
                     e.matmul(out=self.ps_stat[:, TC:2 * TC],
                              lhsT=g.ones_col_bf[:], rhs=sq,
                              start=(dc == 0), stop=(dc == ND - 1)))[-1],
                 waits=w2, inc=self.s_stp)

    def finish(self, tc):
        self.flush_stats()
        ctx, g = self.ctx, self.g
        gi = self.gi
        st = (self.s_stp, ND * (gi + 1))
        r_rs = self.rows[:, 0:TC]
        r_m = self.rows[:, TC:2 * TC]
        r_v = self.rows[:, 2 * TC:3 * TC]
        ctx.step("vector", lambda e: e.tensor_scalar(
            out=r_m, in0=self.ps_stat[:, 0:TC], scalar1=1.0 / D, scalar2=None,
            op0=ALU.mult), waits=[st, (self.s_bc, max(0, 32 * gi - 16))])
        ctx.step("vector", lambda e: e.scalar_tensor_tensor(
            out=r_v, in0=r_m, scalar=-1.0, in1=r_m, op0=ALU.mult, op1=ALU.mult))
        ctx.step("vector", lambda e: e.scalar_tensor_tensor(
            out=r_v, in0=self.ps_stat[:, TC:2 * TC], scalar=1.0 / D, in1=r_v,
            op0=ALU.mult, op1=ALU.add), inc=self.s_row)
        rv = self.s_row.now
        ctx.step("scalar", lambda e: e.activation(
            out=r_rs, in_=r_v, func=AF.Ln, bias=g.eps_col[:]),
            waits=[rv, g.init_done])
        ctx.step("scalar", lambda e: e.activation(
            out=r_rs, in_=r_rs, func=AF.Exp, scale=-0.5), inc=self.s_row)
        rs = self.s_row.now
        ctx.step("vector", lambda e: e.tensor_scalar(
            out=r_m, in0=r_m, scalar1=-1.0, scalar2=None, op0=ALU.mult),
            waits=[rs], inc=self.s_row)
        rneg = self.s_row.now
        ctx.dma("sync", self.rows_d[:], self.rows[:, 0:2 * TC],
                waits=[rneg, (self.s_bc, 32 * gi)], inc=self.s_bc)
        ctx.dma("sync", self.bct[:], self.rows_d.partition_broadcast(128),
                waits=[(self.s_bc, 32 * gi + 16), (self.s_ap, ND * gi)],
                inc=self.s_bc)
        bcv = (self.s_bc, 32 * (gi + 1))
        if self.debug_dram2 is not None:
            ctx.dma("gpsimd", self.debug_dram2[:, gi * 2 * TC:(gi + 1) * 2 * TC],
                    self.bct[:], waits=[bcv], inc=self.s_res)
        rstd_b = self.bct[:, 0:TC]
        negm_b = self.bct[:, TC:2 * TC]
        for dc in range(ND):
            xr = self.xr[:, dc * TC:(dc + 1) * TC]
            ctx.step("gpsimd",
                     lambda e, xr=xr: e.tensor_add(self.tmp[:], xr, negm_b),
                     waits=[bcv, (self.s_ap, gi * ND + dc)], inc=self.s_tmp)
            ctx.step("vector",
                     lambda e, xr=xr: e.tensor_mul(xr, self.tmp[:], rstd_b),
                     waits=[self.s_tmp.now], inc=self.s_ap)
            apv = self.s_ap.now
            xbf_dst = g.xbf[:, dc * S + tc * TC: dc * S + (tc + 1) * TC]
            ctx.step("scalar",
                     lambda e, xr=xr, xbf_dst=xbf_dst: e.copy(out=xbf_dst, in_=xr),
                     waits=[apv], inc=self.s_xbf)
            ctx.dma("sync",
                    self.x_out[dc * 128:(dc + 1) * 128, tc * TC:(tc + 1) * TC],
                    xr, waits=[apv], inc=self.s_out)
        self.gi += 1

    def block_done(self):
        return [(self.s_out, 16 * ND * self.gi), (self.s_xbf, ND * self.gi)]


def emit_precast(ctx, src_fp32, dst_bf16, rows, cols, bufs, s_done,
                 first_wait=()):
    n = rows // 128
    s_in = ctx.sem("pc_in")
    s_out = ctx.sem("pc_out")
    b_out = s_out.n
    for i in range(n):
        waits = [(s_out, b_out + 16 * (i - 1))] if i >= 2 else list(first_wait)
        ctx.dma("gpsimd", bufs[i % 2][:, 0:cols],
                src_fp32[i * 128:(i + 1) * 128, :], waits=waits, inc=s_in)
        ctx.dma("scalar", dst_bf16[i * 128:(i + 1) * 128, :],
                bufs[i % 2][:, 0:cols], waits=[s_in.now], inc=s_out)
    ctx.step("sync", lambda e: e.nop(), waits=[(s_out, b_out + 16 * n)],
             inc=s_done)
    return s_done.now


def emit_l0_attn(ctx, g, ep, io, W):
    u = "l0a"
    xbf_ready = io["xbf_ready"]
    xin_d = ctx.dscratch(u + "_xin", (EXP, S), BF16)
    xsk_d = ctx.dscratch(u + "_xsk", (EXP, S), BF16)
    a_d = ctx.dscratch(u + "_a", (EXP, S), F32)
    b_d = ctx.dscratch(u + "_b", (EXP, S), BF16)
    s_d = ctx.dscratch(u + "_s", (EXP, S), BF16)
    wzbf_d = ctx.dscratch(u + "_wzbf", (EXP, EXP), BF16)
    whbf_d = ctx.dscratch(u + "_whbf", (EXP, EXP), BF16)
    wobf_d = ctx.dscratch(u + "_wobf", (EXP, D), BF16)

    bias = ctx.sbuf(u + "_bias", (128, 4 * NE + ND), F32)
    s_bias = ctx.sem(u + "_bias")
    for i, bname in enumerate(("b1", "b2", "bz", "bh")):
        ctx.dma("sync", bias[:, i * NE:(i + 1) * NE], W[bname][:],
                waits=(list(io["prev_done"]) if i == 0 else []), inc=s_bias)
    ctx.dma("sync", bias[:, 4 * NE:4 * NE + ND], W["bo"][:], inc=s_bias)
    bias_ld = s_bias.now

    stA = contextlib.ExitStack()
    pcb = [ctx.sbuf(u + f"_pcb{i}", (128, 4096), BF16, stack=stA) for i in range(2)]
    s_pc = ctx.sem(u + "_pc")
    d1_ = emit_precast(ctx, W["Wz"], wzbf_d, EXP, EXP, pcb, s_pc)
    d2_ = emit_precast(ctx, W["Wh"], whbf_d, EXP, EXP, pcb, s_pc,
                       first_wait=[d1_])
    emit_precast(ctx, W["Wo"], wobf_d, EXP, D, pcb, s_pc, first_wait=[d2_])
    pc_done = s_pc.now

    ws = WStream(ctx, g.wcol[0:2], first_wait=[xbf_ready], name=u + "_wsA")
    s_mmA = ctx.sem(u + "_mmA")
    s_evA = ctx.sem(u + "_evA")
    s_oA = ctx.sem(u + "_oA")
    ppA = PsumPool(ctx, u + "_psA", 4, (128, 512), stack=stA)
    stg = [ctx.sbuf(u + f"_stgA{i}", (128, S), BF16, stack=stA) for i in range(2)]
    for eo in range(NE):
        stg_free = (s_oA, s_oA.n)
        for wi, wname in enumerate(("W1", "W2")):
            buf, rdy, idx = ws.fetch(W[wname][:, eo * 128:(eo + 1) * 128], D, True)
            last_mm = None
            for nu in range(8):
                bi, bank, bfree = ppA.acquire()
                def mm(e, buf=buf, nu=nu, bank=bank):
                    for c in range(ND):
                        ins = e.matmul(
                            out=bank[:], lhsT=buf[:, c * 128:(c + 1) * 128],
                            rhs=g.xbf[:, c * S + nu * 512: c * S + (nu + 1) * 512],
                            start=(c == 0), stop=(c == ND - 1))
                    return ins
                waits = [rdy] + ([bfree] if bfree else [])
                ctx.step("tensor", mm, waits=waits, inc=s_mmA)
                mmv = s_mmA.now
                last_mm = mmv
                bcol = bias[:, wi * NE + eo: wi * NE + eo + 1]
                dst = stg[wi][:, nu * 512:(nu + 1) * 512]
                ctx.step("scalar",
                         lambda e, bank=bank, dst=dst, bcol=bcol:
                         e.activation(out=dst, in_=bank[:], func=AF.Silu,
                                      bias=bcol),
                         waits=[mmv, bias_ld, stg_free], inc=s_evA)
                ppA.set_free(bi, s_evA.now)
            ws.release(idx, last_mm)
        evA = s_evA.now
        ctx.dma("sync", xin_d[eo * 128:(eo + 1) * 128, :], stg[0][:],
                waits=[evA], inc=s_oA)
        ctx.dma("sync", xsk_d[eo * 128:(eo + 1) * 128, :], stg[1][:],
                waits=[evA], inc=s_oA)
    phaseA_done = (s_oA, s_oA.n)
    stA.close()

    stB = contextlib.ExitStack()
    wsB = WStream(ctx, g.wcol[0:3], first_wait=[phaseA_done, pc_done], name=u + "_wsB")
    s_xinB = ctx.sem(u + "_xinB")
    s_mmB = ctx.sem(u + "_mmB")
    s_evB = ctx.sem(u + "_evB")
    s_abB = ctx.sem(u + "_abB")
    s_oB = ctx.sem(u + "_oB")
    ppZ = PsumPool(ctx, u + "_psZ", 2, (128, 512), stack=stB)
    ppH = PsumPool(ctx, u + "_psH", 2, (128, 512), stack=stB)
    zt = ctx.sbuf(u + "_zt", (128, 2 * 512), F32, stack=stB)
    abt = ctx.sbuf(u + "_abt", (128, 4 * 512), F32, stack=stB)
    abtb = ctx.sbuf(u + "_abtb", (128, 2 * 512), BF16, stack=stB)
    TQ = 1024
    for tq in range(S // TQ):
        wx = [phaseA_done] if tq == 0 else [(s_mmB, s_mmB.n)]
        ctx.dma("sync",
                g.xbf[:].rearrange("p (kc t) -> p kc t", kc=NE)[:, :, 0:TQ],
                xin_d.rearrange("(kc p) t -> p kc t", p=128)
                [:, :, tq * TQ:(tq + 1) * TQ],
                waits=wx, inc=s_xinB)
        xin_rdy = s_xinB.now
        for fo in range(NE):
            fetched = []
            for wi, wd in enumerate((wzbf_d, whbf_d)):
                buf, rdy, idx = wsB.fetch(wd[:, fo * 128:(fo + 1) * 128], EXP,
                                          False)
                fetched.append((buf, rdy, idx))
            for half in range(2):
                banks = []
                for wi in range(2):
                    buf, rdy, idx = fetched[wi]
                    pp = ppZ if wi == 0 else ppH
                    bi, bank, bfree = pp.acquire()
                    def mmB(e, buf=buf, half=half, bank=bank):
                        for c in range(NE):
                            ins = e.matmul(
                                out=bank[:],
                                lhsT=buf[:, c * 128:(c + 1) * 128],
                                rhs=g.xbf[:, c * TQ + half * 512:
                                          c * TQ + (half + 1) * 512],
                                start=(c == 0), stop=(c == NE - 1))
                        return ins
                    waits = [rdy, xin_rdy] + ([bfree] if bfree else [])
                    ctx.step("tensor", mmB, waits=waits, inc=s_mmB)
                    banks.append((pp, bi, bank))
                mmv = s_mmB.now
                zcol = bias[:, 2 * NE + fo: 2 * NE + fo + 1]
                hcol = bias[:, 3 * NE + fo: 3 * NE + fo + 1]
                sl = half
                zdst = zt[:, sl * 512:(sl + 1) * 512]
                wab = (s_abB, max(0, s_abB.n - 2))
                ctx.step("scalar",
                         lambda e, b_=banks[0][2], zdst=zdst, zcol=zcol:
                         e.activation(out=zdst, in_=b_[:], func=AF.Sigmoid,
                                      bias=zcol),
                         waits=[mmv, wab], inc=s_evB)
                banks[0][0].set_free(banks[0][1], s_evB.now)
                evv = s_evB.now
                adst = abt[:, (2 * sl) * 512:(2 * sl + 1) * 512]
                bdst = abtb[:, sl * 512:(sl + 1) * 512]
                wo_ = (s_oB, max(0, s_oB.n - 2 * 16))
                ctx.step("vector",
                         lambda e, zdst=zdst, adst=adst:
                         e.tensor_scalar(out=adst, in0=zdst, scalar1=-1.0,
                                         scalar2=1.0, op0=ALU.mult, op1=ALU.add),
                         waits=[evv, wo_], inc=s_abB)
                ctx.step("vector",
                         lambda e, b_=banks[1][2], zdst=zdst, bdst=bdst,
                         hcol=hcol:
                         e.scalar_tensor_tensor(out=bdst, in0=b_[:], scalar=hcol,
                                                in1=zdst, op0=ALU.add,
                                                op1=ALU.mult), inc=s_abB)
                banks[1][0].set_free(banks[1][1], s_abB.now)
                abv = s_abB.now
                t0 = tq * TQ + half * 512
                ctx.dma("scalar", a_d[fo * 128:(fo + 1) * 128, t0:t0 + 512],
                        adst, waits=[abv], inc=s_oB)
                ctx.dma("scalar", b_d[fo * 128:(fo + 1) * 128, t0:t0 + 512],
                        bdst, waits=[abv], inc=s_oB)
            for buf, rdy, idx in fetched:
                wsB.release(idx, s_mmB.now)
    phaseB_done = (s_oB, s_oB.n)
    stB.close()

    stC = contextlib.ExitStack()
    s_ldC = ctx.sem(u + "_ldC")
    s_scC = ctx.sem(u + "_scC")
    s_oC = ctx.sem(u + "_oC")
    at_ = ctx.sbuf(u + "_at", (128, S), F32, stack=stC)
    bt_ = ctx.sbuf(u + "_bt", (128, S), BF16, stack=stC)
    ht_ = ctx.sbuf(u + "_hC", (128, S), F32, stack=stC)
    skt = ctx.sbuf(u + "_skt", (128, S), BF16, stack=stC)
    for fo in range(NE):
        wl = [phaseB_done] if fo == 0 else [(s_scC, s_scC.n)]
        ctx.dma("sync", at_[:], a_d[fo * 128:(fo + 1) * 128, :], waits=wl,
                inc=s_ldC)
        ctx.dma("sync", bt_[:], b_d[fo * 128:(fo + 1) * 128, :], waits=wl,
                inc=s_ldC)
        wl2 = wl + [(s_oC, s_oC.n)]
        ctx.dma("sync", skt[:], xsk_d[fo * 128:(fo + 1) * 128, :], waits=wl2,
                inc=s_ldC)
        ldv = s_ldC.now
        wsc = [ldv] + ([(s_oC, s_oC.n - 16)] if fo >= 1 else [])
        ctx.step("vector",
                 lambda e: e.tensor_tensor_scan(
                     out=ht_[:], data0=at_[:], data1=bt_[:], initial=0.0,
                     op0=ALU.mult, op1=ALU.add),
                 waits=wsc, inc=s_scC)
        ctx.step("vector",
                 lambda e: e.tensor_tensor(out=skt[:], in0=ht_[:], in1=skt[:],
                                           op=ALU.add), inc=s_scC)
        ctx.step("vector", lambda e: e.engine_nop(), inc=s_scC)
        scv = s_scC.now
        ctx.dma("sync", s_d[fo * 128:(fo + 1) * 128, :], skt[:],
                waits=[scv], inc=s_oC)
    phaseC_done = (s_oC, s_oC.n)
    stC.close()

    stD = contextlib.ExitStack()
    wsD = WStream(ctx, g.wcol[0:3], first_wait=[phaseB_done, pc_done], name=u + "_wsD")
    s_sD = ctx.sem(u + "_sD")
    s_mmD = ctx.sem(u + "_mmD")
    sst = ctx.sbuf(u + "_sst", (128, 2 * NE * TC), BF16, stack=stD)
    ppD = PsumPool(ctx, u + "_psD", 2, (128, TC), stack=stD)
    ep.set_io(io["x_in"], io["x_out"])
    for tc in range(NTC):
        rot = tc % 2
        sbuf_tc = sst[:, rot * NE * TC:(rot + 1) * NE * TC]
        wl = [phaseC_done] if tc == 0 else [(s_mmD, max(0, s_mmD.n - ND))]
        ctx.dma("sync", sbuf_tc.rearrange("p (kc t) -> p kc t", kc=NE),
                s_d.rearrange("(kc p) t -> p kc t", p=128)
                [:, :, tc * TC:(tc + 1) * TC], waits=wl, inc=s_sD)
        srdy = s_sD.now
        for dc in range(ND):
            res_info = ep.load_res(tc, dc)
            buf, rdy, idx = wsD.fetch(wobf_d[:, dc * 128:(dc + 1) * 128], EXP,
                                      False)
            bi, bank, bfree = ppD.acquire()
            def mmD(e, buf=buf, bank=bank, sbuf_tc=sbuf_tc):
                for c in range(NE):
                    ins = e.matmul(
                        out=bank[:], lhsT=buf[:, c * 128:(c + 1) * 128],
                        rhs=sbuf_tc[:, c * TC:(c + 1) * TC],
                        start=(c == 0), stop=(c == NE - 1))
                return ins
            waits = [rdy, srdy] + ([bfree] if bfree else [])
            ctx.step("tensor", mmD, waits=waits, inc=s_mmD)
            wsD.release(idx, s_mmD.now)
            ev = ep.consume(tc, dc, bank[:],
                            bias[:, 4 * NE + dc:4 * NE + dc + 1],
                            [s_mmD.now], res_info)
            ppD.set_free(bi, ev)
        ep.finish(tc)
    stD.close()
    return ep.block_done()


def emit_dequant(ctx, q_dram, scales_dram, out_dram, rows, cols, addc, mulc,
                 prev_done, stack, tile_cols=1024):
    u = ctx.uid()
    n = rows // 128
    m = cols // tile_cols
    ib = [ctx.sbuf(f"dq_i{u}_{i}", (128, tile_cols), I32, stack=stack)
          for i in range(2)]
    sb = (ctx.sbuf(f"dq_s{u}", (128, tile_cols), F32, stack=stack)
          if scales_dram is not None else None)
    wb = [ctx.sbuf(f"dq_w{u}_{i}", (128, tile_cols), BF16, stack=stack)
          for i in range(2)]
    tmp_ = ctx.sbuf(f"dq_t{u}", (128, tile_cols), F32, stack=stack)
    s_in = ctx.sem("dq_in")
    s_cp = ctx.sem("dq_cp")
    s_out = ctx.sem("dq_out")
    b_in, b_cp, b_out = s_in.n, s_cp.n, s_out.n
    k = 0
    for i in range(n):
        for j in range(m):
            w = list(prev_done) if k == 0 else []
            if k >= 2:
                w.append((s_cp, b_cp + k - 1))
            ctx.dma("sync", ib[k % 2][:],
                    q_dram[i * 128:(i + 1) * 128,
                           j * tile_cols:(j + 1) * tile_cols],
                    waits=w, inc=s_in)
            if scales_dram is not None:
                ctx.dma("sync", sb[:],
                        scales_dram[i * 128:(i + 1) * 128,
                                    j * tile_cols:(j + 1) * tile_cols],
                        waits=[(s_cp, b_cp + k)] if k >= 1 else list(prev_done),
                        inc=s_in)
                inr = s_in.now
                if mulc != 1.0:
                    ctx.step("vector", lambda e: e.tensor_scalar(
                        out=sb[:], in0=sb[:], scalar1=mulc, scalar2=None,
                        op0=ALU.mult), waits=[inr])
                ctx.step("vector",
                         lambda e, k=k: e.scalar_tensor_tensor(
                             out=tmp_[:], in0=ib[k % 2][:],
                             scalar=float(addc), in1=sb[:], op0=ALU.add,
                             op1=ALU.mult),
                         waits=[inr])
                ctx.step("vector",
                         lambda e, k=k: e.tensor_copy(out=wb[k % 2][:],
                                                      in_=tmp_[:]),
                         waits=[(s_out, b_out + 16 * (k - 1))], inc=s_cp)
            else:
                ctx.step("vector",
                         lambda e, k=k: e.tensor_scalar(
                             out=wb[k % 2][:], in0=ib[k % 2][:],
                             scalar1=float(mulc), scalar2=None, op0=ALU.mult),
                         waits=[s_in.now, (s_out, b_out + 16 * (k - 1))],
                         inc=s_cp)
            ctx.dma("scalar",
                    out_dram[i * 128:(i + 1) * 128,
                             j * tile_cols:(j + 1) * tile_cols],
                    wb[k % 2][:], waits=[s_cp.now], inc=s_out)
            k += 1
    return (s_out, b_out + 16 * n * m)


def emit_l1_ffn(ctx, g, ep, io, W):
    u = "l1f"
    prev = io["prev_done"]
    w1_d = ctx.dscratch(u + "_w1", (D, EXP), BF16)
    w2_d = ctx.dscratch(u + "_w2", (EXP, D), BF16)

    vec = ctx.sbuf(u + "_vec", (128, 2 * ND + NE + ND), F32)
    halfb2 = ctx.sbuf(u + "_hb2", (128, ND), F32)
    s_vec = ctx.sem(u + "_vec")
    ctx.dma("sync", vec[:, 0:ND], W["ln_gamma"][:], waits=list(prev),
            inc=s_vec)
    ctx.dma("sync", vec[:, ND:2 * ND], W["ln_beta"][:], inc=s_vec)
    ctx.dma("sync", vec[:, 2 * ND:2 * ND + NE], W["fc1_bias"][:], inc=s_vec)
    ctx.dma("sync", vec[:, 2 * ND + NE:2 * ND + NE + ND], W["fc2_bias"][:],
            inc=s_vec)
    vr = s_vec.now
    ctx.step("vector", lambda e: e.tensor_scalar(
        out=halfb2[:], in0=vec[:, 2 * ND + NE:2 * ND + NE + ND], scalar1=0.5,
        scalar2=None, op0=ALU.mult), waits=[vr], inc=s_vec)
    vr2 = s_vec.now

    stL = contextlib.ExitStack()
    dq1 = emit_dequant(ctx, W["fc1_q"], W["fc1_scales"], w1_d, D, EXP,
                       -128.0, 1.0, prev, stL)
    dq2 = emit_dequant(ctx, W["fc2_q"], W["fc2_scales"], w2_d, EXP, D,
                       -128.0, 0.5, prev, stL)

    sqb = ctx.sbuf(u + "_sqb", (128, 2 * TC), BF16, stack=stL)
    rowsL = ctx.sbuf(u + "_rows", (1, 3 * TC), F32, stack=stL)
    bctL = ctx.sbuf(u + "_bct", (128, 2 * TC), F32, stack=stL)
    xfL = ctx.sbuf(u + "_xf", (128, 2 * TC), F32, stack=stL)
    tmpL = ctx.sbuf(u + "_tmp", (128, TC), F32, stack=stL)
    psL = ctx.psum(u + "_psL", (1, 2 * TC), stack=stL)
    psLb = ctx.psum(u + "_psLb", (128, 2 * TC), stack=stL)
    s_lsq = ctx.sem(u + "_lsq")
    s_lst = ctx.sem(u + "_lst")
    s_lrow = ctx.sem(u + "_lrow")
    s_lbc = ctx.sem(u + "_lbc")
    s_lxf = ctx.sem(u + "_lxf")
    s_lap = ctx.sem(u + "_lap")
    xbf_ready = io["xbf_ready"]
    for tc in range(NTC):
        for dc in range(ND):
            k = tc * ND + dc
            xb = g.xbf[:, dc * S + tc * TC: dc * S + (tc + 1) * TC]
            wq = [xbf_ready] + ([(s_lst, s_lst.n - 1)] if k >= 2 else [])
            sq = sqb[:, (k % 2) * TC:(k % 2 + 1) * TC]
            ctx.step("scalar", lambda e, xb=xb, sq=sq: e.activation(
                out=sq, in_=xb, func=AF.Square), waits=wq, inc=s_lsq)
            w2_ = [s_lsq.now]
            if dc == 0 and tc >= 1:
                w2_.append((s_lrow, s_lrow.n))
            ctx.step("tensor",
                     lambda e, xb=xb, sq=sq, dc=dc: (
                         e.matmul(out=psL[:, 0:TC], lhsT=g.ones_col_bf[:],
                                  rhs=xb, start=(dc == 0), stop=(dc == ND - 1)),
                         e.matmul(out=psL[:, TC:2 * TC], lhsT=g.ones_col_bf[:],
                                  rhs=sq, start=(dc == 0), stop=(dc == ND - 1))
                     )[-1], waits=w2_, inc=s_lst)
        r_m = rowsL[:, 0:TC]
        r_v = rowsL[:, TC:2 * TC]
        r_rs = rowsL[:, 2 * TC:3 * TC]
        stv = (s_lst, s_lst.n)
        ctx.step("vector", lambda e: e.tensor_scalar(
            out=r_m, in0=psL[:, 0:TC], scalar1=1.0 / D, scalar2=None,
            op0=ALU.mult), waits=[stv])
        ctx.step("vector", lambda e: e.scalar_tensor_tensor(
            out=r_v, in0=r_m, scalar=-1.0, in1=r_m, op0=ALU.mult, op1=ALU.mult))
        ctx.step("vector", lambda e: e.scalar_tensor_tensor(
            out=r_v, in0=psL[:, TC:2 * TC], scalar=1.0 / D, in1=r_v,
            op0=ALU.mult, op1=ALU.add), inc=s_lrow)
        rv = s_lrow.now
        ctx.step("scalar", lambda e: e.activation(
            out=r_rs, in_=r_v, func=AF.Ln, bias=g.eps_col[:]),
            waits=[rv, g.init_done])
        ctx.step("scalar", lambda e: e.activation(
            out=r_rs, in_=r_rs, func=AF.Exp, scale=-0.5), inc=s_lrow)
        rs = s_lrow.now
        ctx.step("vector", lambda e: e.tensor_scalar(
            out=r_m, in0=r_m, scalar1=-1.0, scalar2=None, op0=ALU.mult),
            waits=[rs], inc=s_lrow)
        rneg = s_lrow.now
        ctx.step("tensor", lambda e: (
            e.matmul(out=psLb[:, 0:TC], lhsT=g.ones_row_f[:], rhs=r_rs,
                     start=True, stop=True),
            e.matmul(out=psLb[:, TC:2 * TC], lhsT=g.ones_row_f[:], rhs=r_m,
                     start=True, stop=True))[-1],
            waits=[rneg, (s_lbc, s_lbc.n)], inc=s_lbc)
        ctx.step("scalar", lambda e: e.copy(out=bctL[:], in_=psLb[:]),
                 waits=[s_lbc.now, (s_lap, s_lap.n)], inc=s_lbc)
        bcv = s_lbc.now
        for dc in range(ND):
            k_ = tc * ND + dc
            xf = xfL[:, (k_ % 2) * TC:(k_ % 2 + 1) * TC]
            wld = [(s_lap, s_lap.n - 3)] if k_ >= 2 else []
            ctx.dma("scalar", xf,
                    io["x_in"][dc * 128:(dc + 1) * 128, tc * TC:(tc + 1) * TC],
                    waits=wld, inc=s_lxf)
            ctx.step("gpsimd",
                     lambda e, xf=xf: e.tensor_add(tmpL[:], xf,
                                                   bctL[:, TC:2 * TC]),
                     waits=[s_lxf.now, bcv, (s_lap, s_lap.n)], inc=s_lap)
            ctx.step("vector",
                     lambda e, xf=xf: e.tensor_mul(tmpL[:], tmpL[:],
                                                   bctL[:, 0:TC]),
                     waits=[s_lap.now])
            xbf_dst = g.xbf[:, dc * S + tc * TC: dc * S + (tc + 1) * TC]
            ctx.step("vector",
                     lambda e, xbf_dst=xbf_dst, dc=dc: e.tensor_scalar(
                         out=xbf_dst, in0=tmpL[:], scalar1=vec[:, dc:dc + 1],
                         scalar2=vec[:, ND + dc:ND + dc + 1], op0=ALU.mult,
                         op1=ALU.add), waits=[vr2], inc=s_lap)
    ln_done = (s_lap, s_lap.n)
    stL.close()

    stM = contextlib.ExitStack()
    ybuf = ctx.sbuf(u + "_ybuf", (128, NE * 1024), BF16, stack=stM)
    ppF = PsumPool(ctx, u + "_psF", 4, (128, 512), stack=stM)
    ws1 = WStream(ctx, g.wcol[0:2], first_wait=[ln_done, dq1], name=u + "_ws1")
    ws2 = WStream(ctx, g.wcol[2:3], first_wait=[dq2], name=u + "_ws2")
    s_mm1 = ctx.sem(u + "_mm1")
    s_ev1 = ctx.sem(u + "_ev1")
    s_mm2 = ctx.sem(u + "_mm2")
    ep.set_io(io["x_in"], io["x_out"])
    TQ = 1024
    for tq in range(S // TQ):
        mm2_at_start = (s_mm2, s_mm2.n)
        for fo in range(NE):
            buf, rdy, idx = ws1.fetch(w1_d[:, fo * 128:(fo + 1) * 128], D, False)
            last = None
            for nu in range(2):
                bi, bank, bfree = ppF.acquire()
                def mm1(e, buf=buf, tq=tq, nu=nu, bank=bank):
                    for c in range(ND):
                        ins = e.matmul(
                            out=bank[:], lhsT=buf[:, c * 128:(c + 1) * 128],
                            rhs=g.xbf[:, c * S + tq * TQ + nu * 512:
                                      c * S + tq * TQ + (nu + 1) * 512],
                            start=(c == 0), stop=(c == ND - 1))
                    return ins
                ctx.step("tensor", mm1,
                         waits=[rdy] + ([bfree] if bfree else []), inc=s_mm1)
                last = s_mm1.now
                ydst = ybuf[:, fo * 1024 + nu * 512: fo * 1024 + (nu + 1) * 512]
                bcol = vec[:, 2 * ND + fo:2 * ND + fo + 1]
                ctx.step("scalar",
                         lambda e, bank=bank, ydst=ydst, bcol=bcol:
                         e.activation(out=ydst, in_=bank[:], func=AF.Silu,
                                      bias=bcol),
                         waits=[last, vr, mm2_at_start], inc=s_ev1)
                ppF.set_free(bi, s_ev1.now)
            ws1.release(idx, last)
        ev1_done = (s_ev1, s_ev1.n)
        for half in range(2):
            tc = tq * 2 + half
            for dc in range(ND):
                res_info = ep.load_res(tc, dc)
                buf, rdy, idx = ws2.fetch(w2_d[:, dc * 128:(dc + 1) * 128],
                                          EXP, False)
                bi, bank, bfree = ppF.acquire()
                def mm2(e, buf=buf, half=half, bank=bank):
                    for c in range(NE):
                        ins = e.matmul(
                            out=bank[:], lhsT=buf[:, c * 128:(c + 1) * 128],
                            rhs=ybuf[:, c * 1024 + half * 512:
                                     c * 1024 + (half + 1) * 512],
                            start=(c == 0), stop=(c == NE - 1))
                    return ins
                ctx.step("tensor", mm2,
                         waits=[rdy, ev1_done] + ([bfree] if bfree else []),
                         inc=s_mm2)
                ws2.release(idx, s_mm2.now)
                ev = ep.consume(tc, dc, bank[:], halfb2[:, dc:dc + 1],
                                [s_mm2.now, vr2], res_info)
                ppF.set_free(bi, ev)
            ep.finish(tc)
    stM.close()
    return ep.block_done()


def emit_l3_ffn(ctx, g, ep, io, W):
    u = "l3f"
    HID = 3072
    NH = HID // 128
    prev = io["prev_done"]
    wg_d = ctx.dscratch(u + "_wg", (D, HID), BF16)
    wu_d = ctx.dscratch(u + "_wu", (D, HID), BF16)
    wd_d = ctx.dscratch(u + "_wd", (HID, D), BF16)
    st0 = contextlib.ExitStack()
    dqg = emit_dequant(ctx, W["wgT"], None, wg_d, D, HID, 0.0, 1.0 / 127.0,
                       prev, st0)
    dqu = emit_dequant(ctx, W["wuT"], None, wu_d, D, HID, 0.0, 1.0 / 127.0,
                       prev, st0)
    dqd = emit_dequant(ctx, W["wdT"], None, wd_d, HID, D, 0.0, 1.0 / 127.0,
                       prev, st0)
    st0.close()

    stM = contextlib.ExitStack()
    ybuf = ctx.sbuf(u + "_ybuf", (128, NH * 1024), BF16, stack=stM)
    tg = ctx.sbuf(u + "_tg", (128, 2 * 512), F32, stack=stM)
    ppF = PsumPool(ctx, u + "_psF", 4, (128, 512), stack=stM)
    wsg = WStream(ctx, g.wcol[0:1], first_wait=[dqg, io["xbf_ready"]], name=u + "_wsg")
    wsu = WStream(ctx, g.wcol[1:2], first_wait=[dqu, io["xbf_ready"]], name=u + "_wsu")
    wsd = WStream(ctx, g.wcol[2:3], first_wait=[dqd], name=u + "_wsd")
    s_mm1 = ctx.sem(u + "_mm1")
    s_ev1 = ctx.sem(u + "_ev1")
    s_sil = ctx.sem(u + "_sil")
    s_mm2 = ctx.sem(u + "_mm2")
    ep.set_io(io["x_in"], io["x_out"])
    TQ = 1024
    for tq in range(S // TQ):
        mm2_at_start = (s_mm2, s_mm2.n)
        for fo in range(NH):
            bufg, rdyg, idxg = wsg.fetch(wg_d[:, fo * 128:(fo + 1) * 128], D,
                                         False)
            bufu, rdyu, idxu = wsu.fetch(wu_d[:, fo * 128:(fo + 1) * 128], D,
                                         False)
            lastg = lastu = None
            for nu in range(2):
                big, bankg, bfreeg = ppF.acquire()
                biu, banku, bfreeu = ppF.acquire()
                for which, (buf, bank, bfree) in enumerate(
                        ((bufg, bankg, bfreeg), (bufu, banku, bfreeu))):
                    def mm1(e, buf=buf, tq=tq, nu=nu, bank=bank):
                        for c in range(ND):
                            ins = e.matmul(
                                out=bank[:], lhsT=buf[:, c * 128:(c + 1) * 128],
                                rhs=g.xbf[:, c * S + tq * TQ + nu * 512:
                                          c * S + tq * TQ + (nu + 1) * 512],
                                start=(c == 0), stop=(c == ND - 1))
                        return ins
                    ctx.step("tensor", mm1,
                             waits=[rdyg if which == 0 else rdyu] +
                                   ([bfree] if bfree else []), inc=s_mm1)
                lastg = lastu = s_mm1.now
                sl = nu
                tgd = tg[:, sl * 512:(sl + 1) * 512]
                wsl = (s_ev1, max(0, s_ev1.n - 1))
                ctx.step("scalar",
                         lambda e, bankg=bankg, tgd=tgd:
                         e.activation(out=tgd, in_=bankg[:], func=AF.Silu),
                         waits=[lastg, wsl, mm2_at_start], inc=s_sil)
                ppF.set_free(big, s_sil.now)
                ydst = ybuf[:, fo * 1024 + nu * 512: fo * 1024 + (nu + 1) * 512]
                ctx.step("vector",
                         lambda e, banku=banku, tgd=tgd, ydst=ydst:
                         e.tensor_tensor(out=ydst, in0=tgd, in1=banku[:],
                                         op=ALU.mult),
                         waits=[s_sil.now, mm2_at_start], inc=s_ev1)
                ppF.set_free(biu, s_ev1.now)
            wsg.release(idxg, lastg)
            wsu.release(idxu, lastu)
        ev1_done = (s_ev1, s_ev1.n)
        for half in range(2):
            tc = tq * 2 + half
            for dc in range(ND):
                res_info = ep.load_res(tc, dc)
                buf, rdy, idx = wsd.fetch(wd_d[:, dc * 128:(dc + 1) * 128],
                                          HID, False)
                bi, bank, bfree = ppF.acquire()
                def mm2(e, buf=buf, half=half, bank=bank):
                    for c in range(NH):
                        ins = e.matmul(
                            out=bank[:], lhsT=buf[:, c * 128:(c + 1) * 128],
                            rhs=ybuf[:, c * 1024 + half * 512:
                                     c * 1024 + (half + 1) * 512],
                            start=(c == 0), stop=(c == NH - 1))
                    return ins
                ctx.step("tensor", mm2,
                         waits=[rdy, ev1_done] + ([bfree] if bfree else []),
                         inc=s_mm2)
                wsd.release(idx, s_mm2.now)
                ev = ep.consume(tc, dc, bank[:], None, [s_mm2.now], res_info)
                ppF.set_free(bi, ev)
            ep.finish(tc)
    stM.close()
    return ep.block_done()


def emit_l2_attn(ctx, g, ep, io, W):
    io = dict(io)
    io.setdefault("dbg_sc", None); io.setdefault("dbg_yt", None); io.setdefault("dbg_ut", None)
    """W: W_inT (1024, 64) f32; W_outT (64, 1024) f32; log_A, B_param, C_param
    (64, 1) f32; log_dt (1, 1) f32."""
    u = "l2a"
    N = 64
    prev = io["prev_done"]
    stM = contextlib.ExitStack()
    win = ctx.sbuf(u + "_win", (128, ND * N), BF16, stack=stM)
    wout = ctx.sbuf(u + "_wout", (N, D), BF16, stack=stM)
    ut = ctx.sbuf(u + "_ut", (N, S), F32, stack=stM)
    sc = ctx.sbuf(u + "_sc", (N, 16), F32, stack=stM)
    scm = ctx.sbuf(u + "_scm", (N, 12), I32, stack=stM)
    yt = ctx.sbuf(u + "_yt", (N, S), BF16, stack=stM)
    ppU = PsumPool(ctx, u + "_psU", 2, (N, 512), stack=stM)
    ppD = PsumPool(ctx, u + "_psD", 2, (128, TC), stack=stM)
    s_ld = ctx.sem(u + "_ld")
    s_mmU = ctx.sem(u + "_mmU")
    s_evU = ctx.sem(u + "_evU")
    s_zoh = ctx.sem(u + "_zoh")
    s_scan = ctx.sem(u + "_scan")
    s_mmD = ctx.sem(u + "_mmD")
    ctx.dma("gpsimd", win[:].rearrange("p (c n) -> p c n", c=ND),
            W["W_inT"].rearrange("(c p) n -> p c n", p=128), waits=prev,
            inc=s_ld)
    ctx.dma("gpsimd", wout[:], W["W_outT"][:], waits=prev, inc=s_ld)
    ctx.dma("sync", sc[:, 0:1], W["log_A"][:], waits=prev, inc=s_ld)
    ctx.dma("sync", sc[:, 1:2], W["B_param"][:], waits=prev, inc=s_ld)
    ctx.dma("sync", sc[:, 2:3], W["C_param"][:], waits=prev, inc=s_ld)
    ctx.dma("sync", sc[:, 3:4], W["log_dt"].partition_broadcast(N), waits=prev,
            inc=s_ld)
    ldv = s_ld.now
    C_ = lambda i: sc[:, i:i + 1]
    def zstep(eng, fn):
        ctx.step(eng, fn, waits=[s_zoh.now], inc=s_zoh)
    ctx.step("scalar", lambda e: e.activation(out=C_(5), in_=C_(3), func=AF.Exp),
             waits=[ldv], inc=s_zoh)
    zstep("scalar", lambda e: e.activation(out=C_(4), in_=C_(0), func=AF.Exp))
    zstep("vector", lambda e: e.tensor_single_scalar(
        out=scm[:, 0:1], in_=C_(4), scalar=1e-6, op=ALU.is_lt))
    zstep("vector", lambda e: e.tensor_scalar(
        out=C_(4), in0=C_(4), scalar1=-1.0, scalar2=None, op0=ALU.mult))
    zstep("vector", lambda e: e.tensor_mul(C_(6), C_(4), C_(5)))
    zstep("scalar", lambda e: e.activation(out=C_(6), in_=C_(6), func=AF.Exp))
    zstep("vector", lambda e: e.memset(C_(11), 1.0))
    zstep("vector", lambda e: e.select(
        out=C_(10), mask=scm[:, 0:1], on_true=C_(11), on_false=C_(4)))
    zstep("vector", lambda e: e.reciprocal(out=C_(12), in_=C_(10)))
    zstep("vector", lambda e: e.tensor_scalar(
        out=C_(13), in0=C_(6), scalar1=-1.0, scalar2=None, op0=ALU.add))
    zstep("vector", lambda e: e.tensor_mul(C_(7), C_(13), C_(12)))
    zstep("vector", lambda e: e.select(
        out=C_(14), mask=scm[:, 0:1], on_true=C_(5), on_false=C_(7)))
    zstep("vector", lambda e: e.tensor_mul(C_(8), C_(14), C_(1)))
    zoh_done = s_zoh.now
    if io.get("dbg_sc") is not None:
        ctx.dma("gpsimd", io["dbg_sc"], sc[:], waits=[zoh_done], inc=s_ld)

    for nu in range(NTC):
        bi, bank, bfree = ppU.acquire()
        def mmU(e, nu=nu, bank=bank):
            for c in range(ND):
                ins = e.matmul(
                    out=bank[:], lhsT=win[:, c * N:(c + 1) * N],
                    rhs=g.xbf[:, c * S + nu * TC: c * S + (nu + 1) * TC],
                    start=(c == 0), stop=(c == ND - 1))
            return ins
        ctx.step("tensor", mmU, waits=[ldv, io["xbf_ready"]] +
                 ([bfree] if bfree else []), inc=s_mmU)
        ctx.step("vector",
                 lambda e, nu=nu, bank=bank: e.tensor_copy(
                     out=ut[:, nu * TC:(nu + 1) * TC], in_=bank[:]),
                 waits=[s_mmU.now], inc=s_evU)
        ppU.set_free(bi, s_evU.now)
    uev = s_evU.now
    bu = ctx.sbuf(u + "_bu", (N, S), F32, stack=stM)
    hsc = ctx.sbuf(u + "_hsc", (N, S), F32, stack=stM)
    ctx.step("vector", lambda e: e.tensor_scalar(
        out=bu[:], in0=ut[:], scalar1=C_(8), scalar2=None, op0=ALU.mult),
        waits=[uev, zoh_done])
    abar_b = sc[:, 6:7].broadcast_to([N, S])
    ctx.step("vector", lambda e: e.tensor_tensor_scan(
        out=hsc[:], data0=abar_b, data1=bu[:], initial=0.0,
        op0=ALU.mult, op1=ALU.add))
    ctx.step("vector", lambda e: e.tensor_scalar(
        out=yt[:], in0=hsc[:], scalar1=C_(2), scalar2=None, op0=ALU.mult),
        inc=s_scan)
    scan_done = s_scan.now
    if io.get("dbg_yt") is not None:
        ctx.dma("gpsimd", io["dbg_yt"], yt[:], waits=[scan_done], inc=s_ld)
    if io.get("dbg_ut") is not None:
        ctx.dma("gpsimd", io["dbg_ut"], ut[:], waits=[scan_done], inc=s_ld)

    ep.set_io(io["x_in"], io["x_out"])
    for tc in range(NTC):
        for dc in range(ND):
            res_info = ep.load_res(tc, dc)
            bi, bank, bfree = ppD.acquire()
            ctx.step("tensor",
                     lambda e, dc=dc, tc=tc, bank=bank: e.matmul(
                         out=bank[:], lhsT=wout[:, dc * 128:(dc + 1) * 128],
                         rhs=yt[:, tc * TC:(tc + 1) * TC],
                         start=True, stop=True),
                     waits=[scan_done] + ([bfree] if bfree else []), inc=s_mmD)
            ev = ep.consume(tc, dc, bank[:], None, [s_mmD.now], res_info)
            ppD.set_free(bi, ev)
        ep.finish(tc)
    stM.close()
    return ep.block_done()


def emit_quantize(ctx, q, x_ap, out_ap, K, waits, inc):
    NB = K // 32
    s_v, s_a, s_g, s_n = q["s_v"], q["s_a"], q["s_g"], q["s_n"]
    x3 = x_ap.rearrange("p (b e) -> p b e", e=32)
    ctx.step("vector", lambda e: e.tensor_reduce(
        out=q["bm"][:, 0:NB], in_=x3, axis=AX, op=ALU.max,
        apply_absolute_value=True), waits=list(waits) + [s_g.now], inc=s_n)
    ctx.step("vector", lambda e: e.tensor_scalar(
        out=q["bm"][:, 0:NB], in0=q["bm"][:, 0:NB], scalar1=1.0 / 7.5,
        scalar2=1e-30, op0=ALU.mult, op1=ALU.max), waits=[s_n.now], inc=s_n)
    ctx.step("vector", lambda e: e.reciprocal(out=q["rs"][:, 0:NB],
                                              in_=q["bm"][:, 0:NB]),
             waits=[s_n.now], inc=s_n)
    rsb = q["rs"][:, 0:NB].unsqueeze(2).broadcast_to([128, NB, 32])
    yq3 = q["yq"][:, 0:K].rearrange("p (b e) -> p b e", e=32)
    ctx.step("vector", lambda e: e.tensor_tensor(
        out=yq3, in0=x3, in1=rsb, op=ALU.mult), waits=[s_n.now], inc=s_v)
    ctx.step("scalar", lambda e: e.activation(
        out=q["ya"][:, 0:K], in_=q["yq"][:, 0:K], func=AF.Abs),
        waits=[s_v.now, s_g.now], inc=s_a)
    av = s_a.now
    ctx.step("gpsimd", lambda e: e.tensor_single_scalar(
        out=q["c2"][:, 0:K], in_=q["ya"][:, 0:K], scalar=2.0, op=ALU.is_ge),
        waits=[av, q["s_u"].now])
    ctx.step("gpsimd", lambda e: e.tensor_single_scalar(
        out=q["c4"][:, 0:K], in_=q["ya"][:, 0:K], scalar=4.0, op=ALU.is_ge),
        inc=s_g)
    gv = s_g.now
    ctx.step("vector", lambda e: e.tensor_scalar(
        out=q["t1"][:, 0:K], in0=q["c2"][:, 0:K], scalar1=-4.0, scalar2=8.0,
        op0=ALU.mult, op1=ALU.add), waits=[gv])
    ctx.step("vector", lambda e: e.scalar_tensor_tensor(
        out=q["t1"][:, 0:K], in0=q["c4"][:, 0:K], scalar=-2.0,
        in1=q["t1"][:, 0:K], op0=ALU.mult, op1=ALU.add))
    ctx.step("vector", lambda e: e.tensor_tensor(
        out=q["u"][:, 0:K], in0=q["yq"][:, 0:K], in1=q["t1"][:, 0:K],
        op=ALU.mult))
    ctx.step("vector", lambda e: e.tensor_copy(out=q["i32"][:, 0:K],
                                               in_=q["u"][:, 0:K]))
    ctx.step("vector", lambda e: e.tensor_copy(out=q["u"][:, 0:K],
                                               in_=q["i32"][:, 0:K]))
    ctx.step("vector", lambda e: e.tensor_scalar(
        out=q["t2"][:, 0:K], in0=q["c2"][:, 0:K], scalar1=0.125, scalar2=0.125,
        op0=ALU.mult, op1=ALU.add))
    ctx.step("vector", lambda e: e.scalar_tensor_tensor(
        out=q["t2"][:, 0:K], in0=q["c4"][:, 0:K], scalar=0.25,
        in1=q["t2"][:, 0:K], op0=ALU.mult, op1=ALU.add))
    ctx.step("vector", lambda e: e.tensor_tensor(
        out=q["u"][:, 0:K], in0=q["u"][:, 0:K], in1=q["t2"][:, 0:K],
        op=ALU.mult), inc=q["s_u"])
    bmb = q["bm"][:, 0:NB].unsqueeze(2).broadcast_to([128, NB, 32])
    u3 = q["u"][:, 0:K].rearrange("p (b e) -> p b e", e=32)
    out3 = out_ap.rearrange("p (b e) -> p b e", e=32)
    ctx.step("vector", lambda e: e.tensor_tensor(
        out=out3, in0=u3, in1=bmb, op=ALU.mult), inc=inc)


def quant_scratch(ctx, name, K, stack):
    q = {n: ctx.sbuf(f"{name}_{n}", (128, K), F32, stack=stack)
         for n in ("yq", "ya", "u")}
    for n in ("c2", "c4", "t1", "t2"):
        q[n] = ctx.sbuf(f"{name}_{n}", (128, K), BF16, stack=stack)
    q["i32"] = ctx.sbuf(f"{name}_i32", (128, K), I32, stack=stack)
    q["bm"] = ctx.sbuf(f"{name}_bm", (128, K // 32), F32, stack=stack)
    q["rs"] = ctx.sbuf(f"{name}_rs", (128, K // 32), F32, stack=stack)
    q["s_v"] = ctx.sem(f"{name}_sv")
    q["s_u"] = ctx.sem(f"{name}_su")
    q["s_n"] = ctx.sem(f"{name}_sn")
    q["s_a"] = ctx.sem(f"{name}_sa")
    q["s_g"] = ctx.sem(f"{name}_sg")
    return q


def emit_l0_ffn(ctx, g, ep, io, W):
    u = "l0f"
    prev = io["prev_done"]
    wdT_d = ctx.dscratch(u + "_wdT", (D, D), BF16)

    bias = ctx.sbuf(u + "_bias", (128, ND), F32)
    s_b = ctx.sem(u + "_b")
    ctx.dma("sync", bias[:], W["bias"][:], waits=list(prev), inc=s_b)
    bias_ld = s_b.now

    st0 = contextlib.ExitStack()
    q0 = quant_scratch(ctx, u + "_q0", D, st0)
    wf = ctx.sbuf(u + "_wf", (128, D), F32, stack=st0)
    wq = ctx.sbuf(u + "_wq", (128, D), BF16, stack=st0)
    wtv = ctx.sbuf(u + "_wtv", (128, D), BF16, stack=st0)
    ps_t = ctx.psum(u + "_psT", (128, D), BF16, stack=st0)
    s_ld0 = ctx.sem(u + "_ld0")
    s_q0 = ctx.sem(u + "_q0")
    s_tr0 = ctx.sem(u + "_tr0")
    s_o0 = ctx.sem(u + "_o0")
    for i in range(ND):
        wl = list(prev) if i == 0 else [(s_q0, s_q0.n)]
        ctx.dma("sync", wf[:], W["weight"][i * 128:(i + 1) * 128, :],
                waits=wl, inc=s_ld0)
        emit_quantize(ctx, q0, wf[:], wq[:], D,
                      [s_ld0.now] + ([(s_tr0, s_tr0.n - 1)] if i >= 1 else []),
                      s_q0)
        def tr(e, i=i):
            for j in range(ND):
                ins = e.transpose(out=ps_t[:, j * 128:(j + 1) * 128],
                                  in_=wq[:, j * 128:(j + 1) * 128],
                                  identity=g.ident_bf[:])
            return ins
        ctx.step("tensor", tr, waits=[s_q0.now, g.ident_done,
                                      (s_o0, s_o0.n - 16)], inc=s_tr0)
        ctx.step("vector", lambda e: e.tensor_copy(out=wtv[:], in_=ps_t[:]),
                 waits=[s_tr0.now], inc=s_tr0)
        ctx.dma("sync",
                wdT_d[:, i * 128:(i + 1) * 128]
                .rearrange("(j p) c -> p j c", p=128),
                wtv[:].rearrange("p (j c) -> p j c", c=128),
                waits=[s_tr0.now], inc=s_o0)
    p0_done = (s_o0, s_o0.n)
    st0.close()

    stM = contextlib.ExitStack()
    q1s = [quant_scratch(ctx, u + "_q1a", 2 * TC, stM)]
    wdT_sb = ctx.sbuf(u + "_wdsb", (128, ND * D), BF16, stack=stM)
    xtok2_ = ctx.sbuf(u + "_xtok", (128, D), F32, stM)
    xqtok2_ = ctx.sbuf(u + "_xqtok", (128, D), BF16, stM)
    xqT = ctx.sbuf(u + "_xqT", (128, ND * TC), BF16, stM)
    xfin = ctx.sbuf(u + "_xfin", (128, ND * TC), F32, stM)
    ps_ti = ctx.psum(u + "_psTi", (128, TC), F32, stack=stM)
    ps_to = ctx.psum(u + "_psTo", (128, D), BF16, stack=stM)
    ppM = PsumPool(ctx, u + "_psM", 2, (128, TC), stack=stM)
    s_wsb = ctx.sem(u + "_wsb")
    s_xin = ctx.sem(u + "_xin")
    s_ti = ctx.sem(u + "_ti")
    s_tie = ctx.sem(u + "_tie")
    s_q1 = ctx.sem(u + "_q1")
    s_to = ctx.sem(u + "_to")
    s_toe = ctx.sem(u + "_toe")
    s_mm = ctx.sem(u + "_mm")
    ctx.dma("sync", wdT_sb[:].rearrange("p (kc c) -> p kc c", kc=ND),
            wdT_d.rearrange("(kc p) c -> p kc c", p=128), waits=[p0_done],
            inc=s_wsb)
    wsb_rdy = s_wsb.now
    ep.set_io(io["x_in"], io["x_out"])
    for tc in range(NTC):
        wl = [(s_q1, s_q1.n)] if tc >= 1 else []
        ctx.dma("sync", xfin[:].rearrange("p (c t) -> p c t", c=ND),
                io["x_in"].rearrange("(c p) t -> p c t", p=128)
                [:, :, tc * TC:(tc + 1) * TC], waits=wl, inc=s_xin)
        xrdy = s_xin.now
        for tt in range(4):
            kq = 4 * tc + tt
            for hw in range(2):
                def tri(e, tt=tt, hw=hw):
                    for j in range(4):
                        dc = hw * 4 + j
                        ins = e.transpose(
                            out=ps_ti[:, j * 128:(j + 1) * 128],
                            in_=xfin[:, dc * TC + tt * 128:
                                     dc * TC + (tt + 1) * 128],
                            identity=g.ident_f32[:])
                    return ins
                ctx.step("tensor", tri,
                         waits=[xrdy, g.ident_done, (s_tie, s_tie.n)],
                         inc=s_ti)
                ctx.step("scalar",
                         lambda e, hw=hw: e.copy(
                             out=xtok2_[:, hw * TC:(hw + 1) * TC],
                             in_=ps_ti[:]),
                         waits=[s_ti.now, (s_q1, s_q1.n)],
                         inc=s_tie)
            emit_quantize(ctx, q1s[0], xtok2_[:], xqtok2_[:], D,
                          [s_tie.now] + ([(s_to, s_to.n)] if kq >= 1 else []),
                          s_q1)
            ctx.step("tensor",
                     lambda e, tt=tt: [
                         e.transpose(out=ps_to[:, dc * 128:(dc + 1) * 128],
                                     in_=xqtok2_[:, dc * 128:(dc + 1) * 128],
                                     identity=g.ident_bf[:])
                         for dc in range(ND)][-1],
                     waits=[s_q1.now, (s_toe, s_toe.n)], inc=s_to)
            ctx.step("vector",
                     lambda e, tt=tt: [
                         e.tensor_copy(
                             out=xqT[:, dc * TC + tt * 128:
                                     dc * TC + (tt + 1) * 128],
                             in_=ps_to[:, dc * 128:(dc + 1) * 128])
                         for dc in range(ND)][-1],
                     waits=[s_to.now], inc=s_toe)
        xq_rdy = (s_toe, s_toe.n)
        for dc in range(ND):
            res_info = ep.load_res(tc, dc)
            bi, bank, bfree = ppM.acquire()
            def mmq(e, dc=dc, bank=bank):
                for c in range(ND):
                    ins = e.matmul(
                        out=bank[:],
                        lhsT=wdT_sb[:, c * D + dc * 128: c * D + dc * 128 + 128],
                        rhs=xqT[:, c * TC:(c + 1) * TC],
                        start=(c == 0), stop=(c == ND - 1))
                return ins
            ctx.step("tensor", mmq,
                     waits=[xq_rdy, wsb_rdy] + ([bfree] if bfree else []),
                     inc=s_mm)
            ev = ep.consume(tc, dc, bank[:], bias[:, dc:dc + 1], [s_mm.now],
                            res_info)
            ppM.set_free(bi, ev)
        ep.finish(tc)
    stM.close()
    return ep.block_done()


def l2ffn_coeffs(w):
    i = np.arange(4096)
    packs = {}
    for nm in ("lowD", "lowO", "lowOT", "highD", "highO", "highOT"):
        packs[nm] = []
    for s in range(12):
        pair = (i >> (s + 1) << s) | (i & ((1 << s) - 1))
        bit = (i >> s) & 1
        Dv = w[s, pair, bit, bit]
        Ov = w[s, pair, bit, 1 - bit]
        OTv = w[s, pair, 1 - bit, bit]
        if s < 7:
            packs["lowD"].append(Dv)
            packs["lowO"].append(Ov)
            packs["lowOT"].append(OTv)
        else:
            packs["highD"].append(Dv)
            packs["highO"].append(Ov)
            packs["highOT"].append(OTv)
    out = {}
    for nm, lst in packs.items():
        a = np.stack(lst)
        ns = a.shape[0]
        a = a.reshape(ns * 32, 128).T
        out[nm] = np.ascontiguousarray(a.astype(np.float32))
    return out


def l2ffn_masks():
    m = [np.eye(128, dtype=BF)]
    for s in range(7):
        q = np.arange(128) ^ (1 << s)
        mm = np.zeros((128, 128), dtype=BF)
        mm[np.arange(128), q] = 1.0
        m.append(mm)
    return np.ascontiguousarray(np.concatenate(m, axis=1))


def emit_l2_ffn(ctx, g, ep, io, W):
    u = "l2f"
    prev = io["prev_done"]
    a_d = ctx.dscratch(u + "_a", (D, EXP), BF16)
    b_d = ctx.dscratch(u + "_b", (EXP, D), BF16)
    apre_d = ctx.dscratch(u + "_apre", (EXP, 1025), BF16)
    bpre_d = ctx.dscratch(u + "_bpre", (EXP, D), BF16)

    biasA = ctx.sbuf(u + "_biasA", (128, NE), F32)
    bias2 = ctx.sbuf(u + "_bias2", (128, ND), F32)

    stP = contextlib.ExitStack()
    coe = ctx.sbuf(u + "_coe", (128, 2048), F32, stack=stP)
    msk = ctx.sbuf(u + "_msk", (128, 8 * 128), BF16, stack=stP)
    vec2 = ctx.sbuf(u + "_vec2", (128, NE + 2 * ND), F32, stack=stP)
    s_c = ctx.sem(u + "_c")
    offs = dict(up_lowD=0, up_lowO=224, up_lowOT=448, dn_lowD=672, dn_lowO=896,
                dn_lowOT=1120, up_highD=1344, up_highO=1504, dn_highD=1664,
                dn_highOT=1824)
    first = True
    for nm in offs:
        n = 224 if "low" in nm else 160
        ctx.dma("sync", coe[:, offs[nm]:offs[nm] + n], W[nm][:],
                waits=(list(prev) if first else []), inc=s_c)
        first = False
    ctx.dma("sync", msk[:], W["masks"][:], inc=s_c)
    ctx.dma("sync", vec2[:, 0:NE], W["up_bias"][:], inc=s_c)
    ctx.dma("sync", vec2[:, NE:NE + ND], W["b_con"][:], inc=s_c)
    ctx.dma("sync", vec2[:, NE + ND:NE + 2 * ND], W["down_bias"][:], inc=s_c)
    c_rdy = s_c.now
    s_b2 = ctx.sem(u + "_b2")
    ctx.step("vector", lambda e: e.tensor_tensor(
        out=bias2[:], in0=vec2[:, NE:NE + ND], in1=vec2[:, NE + ND:NE + 2 * ND],
        op=ALU.add), waits=[c_rdy], inc=s_b2)
    bias2_rdy = s_b2.now

    mdiag = msk[:, 0:128]
    moff = lambda s: msk[:, (s + 1) * 128:(s + 2) * 128]
    ccol = lambda nm, s, J: coe[:, offs[nm] + (s if "low" in nm else s - 7) * 32
                               + J: offs[nm] + (s if "low" in nm else s - 7) * 32
                               + J + 1]

    s_bld = ctx.sem(u + "_bld")
    s_cmp = ctx.sem(u + "_cmp")

    def build_S(Sdst, s, J, Dnm, Onm, extra=()):
        ctx.step("vector", lambda e: e.tensor_scalar(
            out=Sdst, in0=mdiag, scalar1=ccol(Dnm, s, J), scalar2=None,
            op0=ALU.mult), waits=[c_rdy, s_bld.now] + list(extra), inc=s_bld)
        ctx.step("vector", lambda e: e.scalar_tensor_tensor(
            out=Sdst, in0=moff(s), scalar=ccol(Onm, s, J), in1=Sdst,
            op0=ALU.mult, op1=ALU.add), waits=[s_bld.now], inc=s_bld)

    def compose_low(J, specs, SB, R0, R1, ps_c, extra_first=()):
        build_S(R0[:], specs[0][0], J, specs[0][1], specs[0][2],
                extra=list(extra_first))
        cur, nxt = R0, R1
        for (s, Dn, On) in specs[1:]:
            build_S(SB[:], s, J, Dn, On, extra=[s_cmp.now])
            ctx.step("tensor",
                     lambda e, cur=cur: e.matmul(
                         out=ps_c[:], lhsT=SB[:], rhs=cur[:],
                         start=True, stop=True),
                     waits=[s_bld.now, s_cmp.now], inc=s_cmp)
            ctx.step("vector",
                     lambda e, nxt=nxt: e.tensor_copy(out=nxt[:], in_=ps_c[:]),
                     waits=[s_cmp.now], inc=s_cmp)
            cur, nxt = nxt, cur
        return cur

    def high_dram(src_d, NCOL, stages, Dnm, Onm, ready, tag):
        bl = [ctx.sbuf(f"{u}_hl{tag}{i}", (128, NCOL), BF16, stack=stP)
              for i in range(2)]
        bh = [ctx.sbuf(f"{u}_hh{tag}{i}", (128, NCOL), BF16, stack=stP)
              for i in range(2)]
        bt = [ctx.sbuf(f"{u}_ht{tag}{i}", (128, NCOL), BF16, stack=stP)
              for i in range(2)]
        s_l = ctx.sem(f"{u}_hld{tag}")
        s_k = ctx.sem(f"{u}_hk{tag}")
        s_s = ctx.sem(f"{u}_hs{tag}")
        kk = 0
        for si, s in enumerate(stages):
            stride = 1 << (s - 7)
            stage_start_store = s_s.n
            for t in range(NE):
                if t & stride:
                    continue
                t2 = t | stride
                b = kk % 2
                w0 = [ready] if kk == 0 else []
                if si > 0 and (kk % 16) == 0:
                    w0.append((s_s, stage_start_store))
                if kk >= 2:
                    w0.append((s_s, s_s.n - 32))
                ctx.dma("sync", bl[b][:], src_d[t * 128:(t + 1) * 128, :],
                        waits=w0, inc=s_l)
                ctx.dma("sync", bh[b][:], src_d[t2 * 128:(t2 + 1) * 128, :],
                        inc=s_l)
                lrd = s_l.now
                ctx.step("vector", lambda e, b=b, s=s, t=t: e.tensor_scalar(
                    out=bt[b][:], in0=bl[b][:], scalar1=ccol(Dnm, s, t),
                    scalar2=None, op0=ALU.mult), waits=[lrd, c_rdy])
                ctx.step("vector", lambda e, b=b, s=s, t=t:
                         e.scalar_tensor_tensor(
                             out=bt[b][:], in0=bh[b][:], scalar=ccol(Onm, s, t),
                             in1=bt[b][:], op0=ALU.mult, op1=ALU.add))
                ctx.step("vector", lambda e, b=b, s=s, t2=t2: e.tensor_scalar(
                    out=bh[b][:], in0=bh[b][:], scalar1=ccol(Dnm, s, t2),
                    scalar2=None, op0=ALU.mult))
                ctx.step("vector", lambda e, b=b, s=s, t2=t2:
                         e.scalar_tensor_tensor(
                             out=bh[b][:], in0=bl[b][:], scalar=ccol(Onm, s, t2),
                             in1=bh[b][:], op0=ALU.mult, op1=ALU.add),
                         inc=s_k)
                kv = s_k.now
                ctx.dma("scalar", src_d[t * 128:(t + 1) * 128, :], bt[b][:],
                        waits=[kv], inc=s_s)
                ctx.dma("scalar", src_d[t2 * 128:(t2 + 1) * 128, :], bh[b][:],
                        waits=[kv], inc=s_s)
                kk += 1
            stage_done = (s_s, s_s.n)
            ready = stage_done
        return (s_s, s_s.n)

    SB = ctx.sbuf(u + "_SB", (128, 128), BF16, stack=stP)
    R0 = ctx.sbuf(u + "_R0", (128, 128), BF16, stack=stP)
    R1 = ctx.sbuf(u + "_R1", (128, 128), BF16, stack=stP)
    wldb = [ctx.sbuf(u + f"_wld{i}", (128, 1025), BF16, stack=stP)
            for i in range(2)]
    ps_c = ctx.psum(u + "_psc", (128, 128), F32, stack=stP)
    ps_a = ctx.psum(u + "_psa", (128, 3 * 512), F32, stack=stP)
    wov = ctx.sbuf(u + "_wov", (128, 1088), BF16, stack=stP)

    s_wld = ctx.sem(u + "_wld")
    s_app = ctx.sem(u + "_app")
    s_wo = ctx.sem(u + "_wo")
    NCOL = 1025
    for J in range(NE):
        ctx.dma("gpsimd", wldb[J % 2][:, 0:NCOL],
                W["WexpT_aug"][J * 128:(J + 1) * 128, :],
                waits=([(s_app, max(0, s_app.n - 3))] if J >= 2
                       else list(prev)),
                inc=s_wld)
        wrdy = s_wld.now
        cur = compose_low(J, [(6, "up_lowD", "up_lowOT")] +
                          [(s, "up_lowD", "up_lowO") for s in range(5, -1, -1)],
                          SB, R0, R1, ps_c,
                          extra_first=[(s_app, max(0, s_app.n - 1))])
        def app(e, cur=cur, J=J):
            o = None
            for a0 in (0, 512, 1024):
                n = min(512, NCOL - a0)
                o = e.matmul(out=ps_a[:, a0:a0 + n], lhsT=cur[:],
                             rhs=wldb[J % 2][:, a0:a0 + n],
                             start=True, stop=True)
            return o
        ctx.step("tensor", app, waits=[s_cmp.now, wrdy, (s_app, s_app.n)],
                 inc=s_app)
        ctx.step("vector",
                 lambda e: e.tensor_copy(out=wov[:, 0:NCOL],
                                         in_=ps_a[:, 0:NCOL]),
                 waits=[s_app.now, (s_wo, 16 * J)], inc=s_app)
        ctx.dma("sync", apre_d[J * 128:(J + 1) * 128, :], wov[:, 0:NCOL],
                waits=[s_app.now], inc=s_wo)
    up_comp = (s_wo, s_wo.n)
    up_hi = high_dram(apre_d, 1025, range(7, 12), "up_highD", "up_highO",
                      up_comp, "U")
    s_lw = ctx.sem(u + "_lw")
    s_tc = ctx.sem(u + "_tc")
    s_te = ctx.sem(u + "_te")
    s_wo3 = ctx.sem(u + "_wo3")
    s_bx = ctx.sem(u + "_bx")
    for J in range(NE):
        w0 = [up_hi] if J == 0 else [(s_wo3, s_wo3.n)]
        ctx.dma("sync", wov[:, 0:NCOL], apre_d[J * 128:(J + 1) * 128, :],
                waits=w0, inc=s_lw)
        ctx.step("tensor",
                 lambda e: [
                     e.transpose(out=ps_a[:].bitcast(BF16)[:, c * 128:(c + 1) * 128],
                                 in_=wov[:, c * 128:(c + 1) * 128],
                                 identity=g.ident_bf[:])
                     for c in range(ND)][-1],
                 waits=[s_lw.now, g.ident_done, (s_te, s_te.n)], inc=s_tc)
        ctx.step("gpsimd",
                 lambda e, J=J: e.tensor_copy(
                     out=biasA[:, J:J + 1], in_=wov[:, 1024:1025]),
                 waits=[s_lw.now], inc=s_bx)
        ctx.step("vector",
                 lambda e: e.tensor_copy(out=wov[:, 0:D],
                                         in_=ps_a[:].bitcast(BF16)[:, 0:D]),
                 waits=[s_tc.now, s_bx.now], inc=s_te)
        ctx.dma("sync",
                a_d[:, J * 128:(J + 1) * 128]
                .rearrange("(c p) f -> p c f", p=128),
                wov[:, 0:D].rearrange("p (c f) -> p c f", f=128),
                waits=[s_te.now], inc=s_wo3)
    up_done = (s_wo3, s_wo3.n)
    s_ba = ctx.sem(u + "_ba")
    ctx.step("vector", lambda e: e.tensor_tensor(
        out=biasA[:], in0=biasA[:], in1=vec2[:, 0:NE], op=ALU.add),
        waits=[(s_bx, s_bx.n), c_rdy], inc=s_ba)
    biasA_rdy = s_ba.now

    s_pcd = ctx.sem(u + "_pcd")
    for J in range(NE):
        w0 = [(s_pcd, s_pcd.n - 32)] if J >= 2 else [(s_app, s_app.n)]
        ctx.dma("gpsimd", wldb[J % 2][:, 0:D],
                W["W_con"][J * 128:(J + 1) * 128, :], waits=w0, inc=s_pcd)
        ctx.dma("scalar", bpre_d[J * 128:(J + 1) * 128, :],
                wldb[J % 2][:, 0:D], waits=[s_pcd.now], inc=s_pcd)
    pcd_done = (s_pcd, s_pcd.n)
    dn_hi = high_dram(bpre_d, D, range(11, 6, -1), "dn_highD", "dn_highOT",
                      pcd_done, "D")
    s_app2 = ctx.sem(u + "_app2")
    s_wo2 = ctx.sem(u + "_wo2")
    s_wld2 = ctx.sem(u + "_wld2")
    for J in range(NE):
        w0 = [dn_hi] if J < 2 else [(s_app2, s_app2.n - 3)]
        ctx.dma("sync", wldb[J % 2][:, 0:D],
                bpre_d[J * 128:(J + 1) * 128, :], waits=w0, inc=s_wld2)
        wrdy = s_wld2.now
        cur = compose_low(J, [(0, "dn_lowD", "dn_lowO")] +
                          [(s, "dn_lowD", "dn_lowOT") for s in range(1, 7)],
                          SB, R0, R1, ps_c,
                          extra_first=[(s_app2, max(0, s_app2.n - 1))]
                          if J >= 1 else [up_done])
        def app2(e, cur=cur, J=J):
            o = None
            for a0 in (0, 512):
                o = e.matmul(out=ps_a[:, a0:a0 + 512], lhsT=cur[:],
                             rhs=wldb[J % 2][:, a0:a0 + 512],
                             start=True, stop=True)
            return o
        ctx.step("tensor", app2, waits=[s_cmp.now, wrdy, (s_app2, s_app2.n)],
                 inc=s_app2)
        ctx.step("vector",
                 lambda e: e.tensor_copy(out=wov[:, 0:D], in_=ps_a[:, 0:D]),
                 waits=[(s_app2, s_app2.n), (s_wo2, s_wo2.n)], inc=s_app2)
        ctx.dma("sync", b_d[J * 128:(J + 1) * 128, :], wov[:, 0:D],
                waits=[(s_app2, s_app2.n)], inc=s_wo2)
    dn_done = (s_wo2, s_wo2.n)
    stP.close()

    stM = contextlib.ExitStack()
    ybuf = ctx.sbuf(u + "_ybuf", (128, NE * 1024), BF16, stack=stM)
    ppF = PsumPool(ctx, u + "_psF", 4, (128, 512), stack=stM)
    ws1 = WStream(ctx, g.wcol[0:2], first_wait=[up_done, io["xbf_ready"]], name=u + "_ws1")
    ws2 = WStream(ctx, g.wcol[2:3], first_wait=[dn_done], name=u + "_ws2")
    s_mm1 = ctx.sem(u + "_mm1")
    s_ev1 = ctx.sem(u + "_ev1")
    s_mm2 = ctx.sem(u + "_mm2")
    ep.set_io(io["x_in"], io["x_out"])
    TQ = 1024
    for tq in range(S // TQ):
        mm2_at_start = (s_mm2, s_mm2.n)
        for fo in range(NE):
            buf, rdy, idx = ws1.fetch(a_d[:, fo * 128:(fo + 1) * 128], D, False)
            last = None
            for nu in range(2):
                bi, bank, bfree = ppF.acquire()
                def mm1(e, buf=buf, tq=tq, nu=nu, bank=bank):
                    for c in range(ND):
                        ins = e.matmul(
                            out=bank[:], lhsT=buf[:, c * 128:(c + 1) * 128],
                            rhs=g.xbf[:, c * S + tq * TQ + nu * 512:
                                      c * S + tq * TQ + (nu + 1) * 512],
                            start=(c == 0), stop=(c == ND - 1))
                    return ins
                ctx.step("tensor", mm1,
                         waits=[rdy] + ([bfree] if bfree else []), inc=s_mm1)
                last = s_mm1.now
                ydst = ybuf[:, fo * 1024 + nu * 512: fo * 1024 + (nu + 1) * 512]
                ctx.step("scalar",
                         lambda e, bank=bank, ydst=ydst, fo=fo:
                         e.activation(out=ydst, in_=bank[:], func=AF.Gelu,
                                      bias=biasA[:, fo:fo + 1]),
                         waits=[last, biasA_rdy, mm2_at_start], inc=s_ev1)
                ppF.set_free(bi, s_ev1.now)
            ws1.release(idx, last)
        ev1_done = (s_ev1, s_ev1.n)
        for half in range(2):
            tc = tq * 2 + half
            for dc in range(ND):
                res_info = ep.load_res(tc, dc)
                buf, rdy, idx = ws2.fetch(b_d[:, dc * 128:(dc + 1) * 128],
                                          EXP, False)
                bi, bank, bfree = ppF.acquire()
                def mm2(e, buf=buf, half=half, bank=bank):
                    for c in range(NE):
                        ins = e.matmul(
                            out=bank[:], lhsT=buf[:, c * 128:(c + 1) * 128],
                            rhs=ybuf[:, c * 1024 + half * 512:
                                     c * 1024 + (half + 1) * 512],
                            start=(c == 0), stop=(c == NE - 1))
                    return ins
                ctx.step("tensor", mm2,
                         waits=[rdy, ev1_done] + ([bfree] if bfree else []),
                         inc=s_mm2)
                ws2.release(idx, s_mm2.now)
                ev = ep.consume(tc, dc, bank[:], bias2[:, dc:dc + 1],
                                [s_mm2.now, bias2_rdy], res_info)
                ppF.set_free(bi, ev)
            ep.finish(tc)
    stM.close()
    return ep.block_done()


def attn_host_consts(has_alibi):
    ind = np.zeros((128, 8 * 16), dtype=BF)
    for oc in range(8):
        for p in range(128):
            ind[p, oc * 16 + 2 * oc + (p >= 64)] = 1.0
    cm = np.zeros((128, 4 * 512), dtype=BF)
    for off in range(4):
        p = np.arange(128)[:, None]
        fq = np.arange(512)[None, :]
        cm[:, off * 512:(off + 1) * 512] = (fq - p - off * 128 >= 0)
    out = {"ind": ind, "cmask": cm}
    if has_alibi:
        def slopes_pow2(n):
            start = 2.0 ** (-2.0 ** (-(math.log2(n) - 3)))
            return [start * start ** i for i in range(n)]
        sl = np.array(slopes_pow2(16), dtype=np.float64)
        pos = np.arange(S, dtype=np.float64)
        ak = (sl[:, None] * pos[None, :])
        ak_hi = ak.astype(BF)
        ak_lo = (ak - ak_hi.astype(np.float64)).astype(BF)
        out["alibi_k"] = np.ascontiguousarray(ak_hi)
        out["alibi_k_lo"] = np.ascontiguousarray(ak_lo)
        out["alibi_q"] = ak.astype(np.float32)
    return out


def emit_attn(ctx, g, ep, io, W, cfg):
    u = cfg["name"]
    su = "at"
    alibi = cfg["alibi"]
    KDIM = 67 if alibi else 65
    prev = io["prev_done"]
    qT_d = ctx.dscratch(u + "_qd", (D, S), BF16)
    kT_d = ctx.dscratch(u + "_kd", (D, S), BF16)
    v_d = ctx.dscratch(u + "_vd", (H, S, HD), BF16)
    b_d = ctx.dscratch(u + "_bd", (16, S), F32)
    lrow_d = ctx.dscratch(u + "_lrd", (2, 512), F32)

    if not hasattr(ctx, "_at_tiles"):
        ctx._at_tiles = (ctx.sbuf(su + "_ind", (128, 128), BF16),
                         ctx.sbuf(su + "_cmask", (128, 4 * 512), BF16),
                         ctx.sbuf(su + "_bias", (128, 4 * ND), F32))
    ind_sb, cmask_sb, bias_sb = ctx._at_tiles
    s_c = ctx.sem(su + "_c")
    ctx.dma("sync", ind_sb[:], W["ind"][:], waits=list(prev), inc=s_c)
    ctx.dma("sync", cmask_sb[:], W["cmask"][:], inc=s_c)
    if cfg["qkv_bias"]:
        for i, nm in enumerate(("bq", "bk", "bv", "bo")):
            ctx.dma("sync", bias_sb[:, i * ND:(i + 1) * ND], W[nm][:], inc=s_c)
    c_rdy = s_c.now

    stA = contextlib.ExitStack()
    ws = WStream(ctx, g.wcol[0:2], first_wait=[io["xbf_ready"]] + list(prev), name=su + "_wsA")
    ppA = PsumPool(ctx, u + "_psA", 2, (128, 512), stack=stA)
    ps_ind = ctx.psum(u + "_psI", (16, 512), stack=stA)
    ps_tv = ctx.psum(u + "_psTv", (128, 512), BF16, stack=stA)
    stg = ctx.sbuf(u + "_stg", (128, 4 * 512), BF16, stack=stA)
    sqg = ctx.sbuf(u + "_sqg", (128, 512), BF16, stack=stA)
    nrm = ctx.sbuf(u + "_nrm", (16, 2 * S), F32, stack=stA)
    vtv = ctx.sbuf(u + "_vtv", (128, 512), BF16, stack=stA)
    aq_sb = ctx.sbuf(u + "_aq", (16, S), F32, stack=stA) if alibi else None
    s_mmA = ctx.sem(su + "_mmA")
    s_evA = ctx.sem(su + "_evA")
    s_sqA = ctx.sem(su + "_sqA")
    s_indA = ctx.sem(su + "_indA")
    s_nrm = ctx.sem(su + "_nrm")
    s_oA = ctx.sem(su + "_oA")
    s_tvA = ctx.sem(su + "_tvA")
    ctx.step("vector", lambda e: e.memset(nrm[:], 0.0), waits=list(prev),
             inc=s_nrm)
    if alibi:
        ctx.dma("sync", aq_sb[:], W["alibi_q"][:], inc=s_c)
    slot_free = [None] * 4
    for wi, wname in enumerate(("WqT", "WkT", "WvT")):
        dst_d = (qT_d, kT_d, None)[wi]
        evsc = (1.0, cfg["kscale"], cfg["vscale"])[wi]
        bcol0 = (0, 1, 2)[wi] * ND
        for oc in range(ND):
            buf, rdy, idx = ws.fetch(W[wname][:, oc * 128:(oc + 1) * 128], D,
                                     True)
            last = None
            for nu in range(NTC):
                bi, bank, bfree = ppA.acquire()
                def mmA(e, buf=buf, nu=nu, bank=bank):
                    for c in range(ND):
                        ins = e.matmul(
                            out=bank[:], lhsT=buf[:, c * 128:(c + 1) * 128],
                            rhs=g.xbf[:, c * S + nu * 512:c * S + (nu + 1) * 512],
                            start=(c == 0), stop=(c == ND - 1))
                    return ins
                ctx.step("tensor", mmA,
                         waits=[rdy] + ([bfree] if bfree else []), inc=s_mmA)
                last = s_mmA.now
                sl = (wi * NTC * ND + oc * NTC + nu) % 4
                sslot = stg[:, sl * 512:(sl + 1) * 512]
                wst = [last, c_rdy]
                if slot_free[sl] is not None:
                    wst += slot_free[sl]
                bcol = (bias_sb[:, bcol0 + oc:bcol0 + oc + 1]
                        if cfg["qkv_bias"] else None)
                ctx.step("scalar",
                         lambda e, bank=bank, sslot=sslot, bcol=bcol,
                         evsc=evsc: e.activation(
                             out=sslot, in_=bank[:], func=AF.Identity,
                             **({"bias": bcol} if bcol is not None else {}),
                             scale=evsc),
                         waits=wst, inc=s_evA)
                ppA.set_free(bi, s_evA.now)
                evv = s_evA.now
                if wi < 2:
                    ctx.dma("sync",
                            dst_d[oc * 128:(oc + 1) * 128,
                                  nu * 512:(nu + 1) * 512], sslot,
                            waits=[evv], inc=s_oA)
                    ctx.step("scalar",
                             lambda e, sslot=sslot: e.activation(
                                 out=sqg[:], in_=sslot, func=AF.Square),
                             waits=[evv, (s_indA, s_indA.n)], inc=s_sqA)
                    ctx.step("tensor",
                             lambda e, oc=oc: e.matmul(
                                 out=ps_ind[:],
                                 lhsT=ind_sb[:, oc * 16:(oc + 1) * 16],
                                 rhs=sqg[:], start=True, stop=True),
                             waits=[s_sqA.now, (s_nrm, s_nrm.n)], inc=s_indA)
                    nslot = nrm[:, wi * S + nu * 512: wi * S + (nu + 1) * 512]
                    ctx.step("vector",
                             lambda e, nslot=nslot: e.tensor_tensor(
                                 out=nslot, in0=nslot, in1=ps_ind[:],
                                 op=ALU.add),
                             waits=[s_indA.now], inc=s_nrm)
                    slot_free[sl] = [(s_oA, s_oA.n), (s_sqA, s_sqA.n)]
                else:
                    ctx.step("tensor",
                             lambda e, sslot=sslot: [
                                 e.transpose(
                                     out=ps_tv[:, tt * 128:(tt + 1) * 128],
                                     in_=sslot[:, tt * 128:(tt + 1) * 128],
                                     identity=g.ident_bf[:])
                                 for tt in range(4)][-1],
                             waits=[evv, g.ident_done, (s_tvA, s_tvA.n)],
                             inc=s_tvA)
                    ctx.step("vector",
                             lambda e: e.tensor_copy(out=vtv[:], in_=ps_tv[:]),
                             waits=[s_tvA.now, (s_oA, s_oA.n)], inc=s_tvA)
                    tvv = (s_tvA, s_tvA.n)
                    for hh in range(2):
                        hidx = 2 * oc + hh
                        ctx.dma("sync",
                                v_d[hidx, nu * 512:(nu + 1) * 512, :]
                                .rearrange("(tt p) c -> p tt c", p=128),
                                vtv[:].rearrange("p (tt c) -> p tt c", c=128)
                                [:, :, hh * 64:(hh + 1) * 64],
                                waits=[tvv], inc=s_oA)
                    slot_free[sl] = [tvv]
            ws.release(idx, last)
    projA_done = [(s_oA, s_oA.n), (s_nrm, s_nrm.n)]
    brow = ctx.sbuf(u + "_brw", (16, S), F32, stack=stA)
    nkm = ctx.sbuf(u + "_nkm", (16, 2), F32, stack=stA)
    s_bnd = ctx.sem(su + "_bnd")
    ctx.step("scalar", lambda e: e.activation(
        out=nrm[:, 0:S], in_=nrm[:, 0:S], func=AF.Sqrt), waits=projA_done,
        inc=s_bnd)
    ctx.step("scalar", lambda e: e.activation(
        out=nrm[:, S:2 * S], in_=nrm[:, S:2 * S], func=AF.Sqrt), inc=s_bnd)
    ctx.step("vector", lambda e: e.tensor_reduce(
        out=nkm[:, 0:1], in_=nrm[:, S:2 * S], axis=AX, op=ALU.max),
        waits=[s_bnd.now], inc=s_bnd)
    ctx.step("vector", lambda e: e.tensor_scalar(
        out=brow[:], in0=nrm[:, 0:S], scalar1=nkm[:, 0:1], scalar2=1.0,
        op0=ALU.mult, op1=ALU.add), waits=[s_bnd.now], inc=s_bnd)
    if alibi:
        ctx.step("vector", lambda e: e.tensor_tensor(
            out=brow[:], in0=brow[:], in1=aq_sb[:], op=ALU.add),
            waits=[s_bnd.now, s_c.now], inc=s_bnd)
    ctx.dma("sync", b_d[:], brow[:], waits=[s_bnd.now], inc=s_bnd)
    bnd_done = (s_bnd, s_bnd.n)
    stA.close()

    stB = contextlib.ExitStack()
    kh = [ctx.sbuf(u + f"_kh{i}", (KDIM, S), BF16, stack=stB) for i in range(2)]
    rh = [ctx.sbuf(u + f"_rh{i}", (KDIM, S), BF16, stack=stB) for i in range(2)]
    vh = [ctx.sbuf(u + f"_vh{i}", (128, NT * HD), BF16, stack=stB)
          for i in range(2)]
    pt = ctx.sbuf(u + "_pt", (128, 2048), BF16, stack=stB)
    lr = ctx.sbuf(u + "_lr", (1, 2 * 1024), F32, stack=stB)
    rlb = ctx.sbuf(u + "_rlb", (128, 512), F32, stack=stB)
    ps_sc = ctx.psum(u + "_psS", (128, 2048), F32, stack=stB)
    ppV = PsumPool(ctx, u + "_psV", 2, (128, 512), stack=stB)
    s_kv = ctx.sem(su + "_kv")
    s_mmS = ctx.sem(su + "_mmS")
    s_exp = ctx.sem(su + "_exp")
    s_msk = ctx.sem(su + "_msk")
    s_av = ctx.sem(su + "_av")
    s_lmm = ctx.sem(su + "_lmm")
    s_le = ctx.sem(su + "_le")
    s_rl = ctx.sem(su + "_rl")
    s_oe = ctx.sem(su + "_oe")
    for pair in range(H // 2):
        hA, hB = 2 * pair, 2 * pair + 1
        ldw = [bnd_done] if pair == 0 else [(s_mmS, s_mmS.n), (s_av, s_av.n),
                                            (s_lmm, s_lmm.n)]
        for hh, hidx in ((0, hA), (1, hB)):
            ctx.dma("sync", kh[hh][0:64, :],
                    kT_d[64 * hidx:64 * hidx + 64, :], waits=ldw, inc=s_kv)
            ctx.step("vector",
                     lambda e, hh=hh: e.memset(kh[hh][64:65, :], -1.0),
                     waits=ldw, inc=s_kv)
            if alibi:
                ctx.dma("sync", kh[hh][65:66, :],
                        W["alibi_k"][hidx:hidx + 1, :], waits=ldw, inc=s_kv)
                ctx.dma("sync", kh[hh][66:67, :],
                        W["alibi_k_lo"][hidx:hidx + 1, :], waits=ldw, inc=s_kv)
            ctx.dma("sync", rh[hh][0:64, :],
                    qT_d[64 * hidx:64 * hidx + 64, :], waits=ldw, inc=s_kv)
            ctx.dma("gpsimd", rh[hh][64:65, :], b_d[hidx:hidx + 1, :],
                    waits=ldw, inc=s_kv)
            if alibi:
                ctx.dma("sync", rh[hh][65:66, :], W["ones_row"][:],
                        waits=ldw, inc=s_kv)
                ctx.dma("sync", rh[hh][66:67, :], W["ones_row"][:],
                        waits=ldw, inc=s_kv)
            ctx.dma("sync",
                    vh[hh][:].rearrange("p (kt c) -> p kt c", c=HD),
                    v_d[hidx].rearrange("(kt p) c -> p kt c", p=128),
                    waits=ldw, inc=s_kv)
        kv_rdy = s_kv.now
        for qc in range(NTC):
            nkt = 4 * (qc + 1)
            li_A, lbank_A = None, ep.ps_stat[:, 0:512]
            lbank_B = ep.ps_stat[:, 512:1024]
            bi, avbank, avfree = ppV.acquire()
            for kt in range(nkt):
                half = kt % 2
                scA = ps_sc[:, half * 1024:half * 1024 + 512]
                scB = ps_sc[:, half * 1024 + 512:half * 1024 + 1024]
                wsc = [kv_rdy]
                if kt >= 2:
                    wsc.append((s_exp, s_exp.n))
                ctx.step("tensor",
                         lambda e, scA=scA, scB=scB, kt=kt, qc=qc: (
                             e.matmul(out=scA,
                                      lhsT=kh[0][:, kt * 128:(kt + 1) * 128],
                                      rhs=rh[0][:, qc * 512:(qc + 1) * 512],
                                      start=True, stop=True),
                             e.matmul(out=scB,
                                      lhsT=kh[1][:, kt * 128:(kt + 1) * 128],
                                      rhs=rh[1][:, qc * 512:(qc + 1) * 512],
                                      start=True, stop=True))[-1],
                         waits=wsc, inc=s_mmS)
                if kt % 2 == 0:
                    continue
                mmv = s_mmS.now
                ctx.step("scalar",
                         lambda e: e.activation(
                             out=pt[:], in_=ps_sc[:], func=AF.Exp),
                         waits=[mmv, (s_av, max(0, s_av.n - 1)),
                                (s_lmm, max(0, s_lmm.n - 1))], inc=s_exp)
                expv = s_exp.now
                for ks in (kt - 1, kt):
                    halfk = ks % 2
                    pth = pt[:, halfk * 1024:(halfk + 1) * 1024]
                    if ks >= 4 * qc:
                        ctx.step("gpsimd",
                                 lambda e, pth=pth, qc=qc, ks=ks: (
                                     e.affine_select(
                                         out=pth[:, 0:512], in_=pth[:, 0:512],
                                         pattern=[[1, 512]],
                                         base=qc * 512 - ks * 128,
                                         channel_multiplier=-1,
                                         compare_op=ALU.is_ge, fill=0.0),
                                     e.affine_select(
                                         out=pth[:, 512:1024],
                                         in_=pth[:, 512:1024],
                                         pattern=[[1, 512]],
                                         base=qc * 512 - ks * 128,
                                         channel_multiplier=-1,
                                         compare_op=ALU.is_ge, fill=0.0))[-1],
                                 waits=[expv], inc=s_msk)
                        mskv = s_msk.now
                    else:
                        mskv = expv
                    ctx.step("tensor",
                             lambda e, ptA=pth[:, 0:512], ptB=pth[:, 512:1024],
                             ks=ks, nkt=nkt, avbank=avbank: (
                                 e.matmul(out=avbank[0:64, :],
                                          lhsT=vh[0][:, ks * HD:(ks + 1) * HD],
                                          rhs=ptA, start=(ks == 0),
                                          stop=(ks == nkt - 1),
                                          tile_position=(0, 0)),
                                 e.matmul(out=avbank[64:128, :],
                                          lhsT=vh[1][:, ks * HD:(ks + 1) * HD],
                                          rhs=ptB, start=(ks == 0),
                                          stop=(ks == nkt - 1),
                                          tile_position=(0, 64)))[-1],
                             waits=[mskv] + ([avfree] if avfree and ks == 0
                                             else []), inc=s_av)
                    ctx.step("tensor",
                             lambda e, ptA=pth[:, 0:512], ptB=pth[:, 512:1024],
                             ks=ks, nkt=nkt, lbank_A=lbank_A,
                             lbank_B=lbank_B: (
                                 e.matmul(out=lbank_A, lhsT=g.ones_col_bf[:],
                                          rhs=ptA, start=(ks == 0),
                                          stop=(ks == nkt - 1)),
                                 e.matmul(out=lbank_B, lhsT=g.ones_col_bf[:],
                                          rhs=ptB, start=(ks == 0),
                                          stop=(ks == nkt - 1)))[-1],
                             waits=([(s_le, s_le.n)] if ks == 0 else []),
                             inc=s_lmm)
            avv = s_av.now
            lmmv = s_lmm.now
            ctx.step("scalar", lambda e: (
                e.copy(out=lr[0:1, 0:512], in_=lbank_A),
                e.copy(out=lr[0:1, 512:1024], in_=lbank_B))[-1],
                waits=[lmmv, (s_rl, s_rl.n)], inc=s_le)
            ctx.step("vector", lambda e: e.reciprocal(
                out=lr[0:1, 1024:2048], in_=lr[0:1, 0:1024]),
                waits=[s_le.now], inc=s_le)
            ctx.dma("sync", lrow_d.rearrange("a b -> (a b)").unsqueeze(0),
                    lr[0:1, 1024:2048], waits=[s_le.now], inc=s_le)
            ctx.dma("sync", rlb[0:64, :],
                    lrow_d[0:1, :].partition_broadcast(64),
                    waits=[(s_le, s_le.n), (s_oe, s_oe.n)], inc=s_rl)
            ctx.dma("sync", rlb[64:128, :],
                    lrow_d[1:2, :].partition_broadcast(64),
                    waits=[(s_le, s_le.n)], inc=s_rl)
            rlv = s_rl.now
            xdst = g.xbf[:, pair * S + qc * 512: pair * S + (qc + 1) * 512]
            ctx.step("vector",
                     lambda e, avbank=avbank, xdst=xdst: e.tensor_tensor(
                         out=xdst, in0=avbank[:], in1=rlb[:], op=ALU.mult),
                     waits=[rlv, avv], inc=s_oe)
            ppV.set_free(bi, s_oe.now)
    phaseB_done = (s_oe, s_oe.n)
    stB.close()

    stC = contextlib.ExitStack()
    wsC = WStream(ctx, g.wcol[0:2], first_wait=[phaseB_done], name=su + "_wsC")
    ppC = PsumPool(ctx, u + "_psC", 2, (128, TC), stack=stC)
    s_mmC = ctx.sem(su + "_mmC")
    ep.set_io(io["x_in"], io["x_out"])
    for tc in range(NTC):
        for dc in range(ND):
            res_info = ep.load_res(tc, dc)
            buf, rdy, idx = wsC.fetch(W["WoT"][:, dc * 128:(dc + 1) * 128], D,
                                      True)
            bi, bank, bfree = ppC.acquire()
            def mmC(e, buf=buf, tc=tc, bank=bank):
                for c in range(ND):
                    ins = e.matmul(
                        out=bank[:], lhsT=buf[:, c * 128:(c + 1) * 128],
                        rhs=g.xbf[:, c * S + tc * TC:c * S + (tc + 1) * TC],
                        start=(c == 0), stop=(c == ND - 1))
                return ins
            ctx.step("tensor", mmC, waits=[rdy] + ([bfree] if bfree else []),
                     inc=s_mmC)
            wsC.release(idx, s_mmC.now)
            bcol = bias_sb[:, 3 * ND + dc:3 * ND + dc + 1] if cfg["qkv_bias"] \
                else None
            ev = ep.consume(tc, dc, bank[:], bcol, [s_mmC.now, c_rdy],
                            res_info, scale=cfg["oscale"])
            ppC.set_free(bi, ev)
        ep.finish(tc)
    stC.close()
    return ep.block_done()


def _col(v):
    return np.ascontiguousarray(v.reshape(-1, 128).T)


def _cT(a):
    return np.ascontiguousarray(a.T)


def _declare_block(ctx, bi, inputs):
    P = f"b{bi}_"
    if bi == 0:
        W = {}
        for nm in ("W1", "W2", "Wz", "Wh", "Wo"):
            W[nm] = ctx.din(P + nm, inputs[f"L0_attn_{nm}"], F32)
        for nm in ("b1", "b2", "bz", "bh", "bo"):
            W[nm] = ctx.din(P + nm, _col(inputs[f"L0_attn_{nm}"]), F32)
        return lambda c, g, e, io: emit_l0_attn(c, g, e, io, W)
    if bi == 1:
        W = dict(weight=ctx.din(P + "w", inputs["L0_ffn_weight"], F32),
                 bias=ctx.din(P + "b", _col(inputs["L0_ffn_bias"]), F32))
        return lambda c, g, e, io: emit_l0_ffn(c, g, e, io, W)
    if bi == 2:
        ac = attn_host_consts(False)
        W = dict(
            WqT=ctx.din(P + "wq", _cT(inputs["L1_attn_W_query"]), F32),
            WkT=ctx.din(P + "wk", _cT(inputs["L1_attn_W_key"]), F32),
            WvT=ctx.din(P + "wv", _cT(inputs["L1_attn_W_value"]), F32),
            WoT=ctx.din(P + "wo", _cT(inputs["L1_attn_W_proj"]), F32),
            ind=ctx.din(P + "ind", ac["ind"], BF16),
            cmask=ctx.din(P + "cm", ac["cmask"], BF16),
            ones_row=ctx.din(P + "or", np.ones((1, S), dtype=BF), BF16),
        )
        cfg = dict(name=P + "a", kscale=2.0 ** -13, vscale=1.0 / 32.0,
                   oscale=1.0 / 32.0, qkv_bias=False, alibi=False)
        return lambda c, g, e, io: emit_attn(c, g, e, io, W, cfg)
    if bi == 3:
        W = dict(
            ln_gamma=ctx.din(P + "g", _col(inputs["L1_ffn_ln_gamma"]), F32),
            ln_beta=ctx.din(P + "be", _col(inputs["L1_ffn_ln_beta"]), F32),
            fc1_q=ctx.din(P + "f1q", inputs["L1_ffn_fc1_q"], I32),
            fc1_scales=ctx.din(P + "f1s", np.ascontiguousarray(
                np.repeat(inputs["L1_ffn_fc1_scales"], 128, axis=0)), F32),
            fc1_bias=ctx.din(P + "f1b", _col(inputs["L1_ffn_fc1_bias"]), F32),
            fc2_q=ctx.din(P + "f2q", inputs["L1_ffn_fc2_q"], I32),
            fc2_scales=ctx.din(P + "f2s", np.ascontiguousarray(
                np.repeat(inputs["L1_ffn_fc2_scales"], 128, axis=0)), F32),
            fc2_bias=ctx.din(P + "f2b", _col(inputs["L1_ffn_fc2_bias"]), F32),
        )
        return lambda c, g, e, io: emit_l1_ffn(c, g, e, io, W)
    if bi == 4:
        W = dict(
            W_inT=ctx.din(P + "wi", _cT(inputs["L2_attn_W_in"]), F32),
            W_outT=ctx.din(P + "wo", _cT(inputs["L2_attn_W_out"]), F32),
            log_A=ctx.din(P + "la", inputs["L2_attn_log_A"].reshape(64, 1), F32),
            B_param=ctx.din(P + "bp", inputs["L2_attn_B_param"].reshape(64, 1),
                            F32),
            C_param=ctx.din(P + "cp", inputs["L2_attn_C_param"].reshape(64, 1),
                            F32),
            log_dt=ctx.din(P + "ld", inputs["L2_attn_log_dt"].reshape(1, 1),
                           F32),
        )
        return lambda c, g, e, io: emit_l2_attn(c, g, e, io, W)
    if bi == 5:
        co = l2ffn_coeffs(inputs["L2_ffn_up_weights"])
        cd = l2ffn_coeffs(inputs["L2_ffn_down_weights"])
        aug = np.ascontiguousarray(np.concatenate(
            [_cT(inputs["L2_ffn_W_exp"]),
             np.asarray(inputs["L2_ffn_b_exp"]).reshape(-1, 1)], axis=1))
        W = dict(
            WexpT_aug=ctx.din(P + "wea", aug, F32),
            W_con=ctx.din(P + "wc", inputs["L2_ffn_W_con"], F32),
            masks=ctx.din(P + "msk", l2ffn_masks(), BF16),
            up_bias=ctx.din(P + "ub", _col(inputs["L2_ffn_up_bias"]), F32),
            b_con=ctx.din(P + "bc", _col(inputs["L2_ffn_b_con"]), F32),
            down_bias=ctx.din(P + "db", _col(inputs["L2_ffn_down_bias"]), F32),
        )
        for pre, cc in (("up", co), ("dn", cd)):
            for k in ("lowD", "lowO", "lowOT", "highD", "highO", "highOT"):
                if pre == "up" and k == "highOT":
                    continue
                if pre == "dn" and k == "highO":
                    continue
                W[f"{pre}_{k}"] = ctx.din(P + pre + k, cc[k], F32)
        return lambda c, g, e, io: emit_l2_ffn(c, g, e, io, W)
    if bi == 6:
        ac = attn_host_consts(True)
        W = dict(
            WqT=ctx.din(P + "wq", _cT(inputs["L3_attn_Wq"]), F32),
            WkT=ctx.din(P + "wk", _cT(inputs["L3_attn_Wk"]), F32),
            WvT=ctx.din(P + "wv", _cT(inputs["L3_attn_Wv"]), F32),
            WoT=ctx.din(P + "wo", _cT(inputs["L3_attn_Wo"]), F32),
            bq=ctx.din(P + "bq", _col(inputs["L3_attn_bq"]), F32),
            bk=ctx.din(P + "bk", _col(inputs["L3_attn_bk"]), F32),
            bv=ctx.din(P + "bv", _col(inputs["L3_attn_bv"]), F32),
            bo=ctx.din(P + "bo", _col(inputs["L3_attn_bo"]), F32),
            ind=ctx.din(P + "ind", ac["ind"], BF16),
            cmask=ctx.din(P + "cm", ac["cmask"], BF16),
            ones_row=ctx.din(P + "or", np.ones((1, S), dtype=BF), BF16),
            alibi_k=ctx.din(P + "ak", ac["alibi_k"], BF16),
            alibi_k_lo=ctx.din(P + "akl", ac["alibi_k_lo"], BF16),
            alibi_q=ctx.din(P + "aq", ac["alibi_q"], F32),
        )
        cfg = dict(name=P + "a", kscale=1.0 / 8.0, vscale=1.0, oscale=None,
                   qkv_bias=True, alibi=True)
        return lambda c, g, e, io: emit_attn(c, g, e, io, W, cfg)
    if bi == 7:
        W = dict(
            wgT=ctx.din(P + "wg", _cT(inputs["L3_ffn_w_gate"]), I32),
            wuT=ctx.din(P + "wu", _cT(inputs["L3_ffn_w_up"]), I32),
            wdT=ctx.din(P + "wd", _cT(inputs["L3_ffn_w_down"]), I32),
        )
        return lambda c, g, e, io: emit_l3_ffn(c, g, e, io, W)
    raise ValueError(bi)


FUSED = True


def _build_block_program(bi, inputs):
    ctx = Ctx()
    g = Globals(ctx)
    ep = LNEpilogue(ctx, g)
    x_in = ctx.din("x_in", np.zeros((D, S), np.float32), F32)
    x_out = ctx.dout("x_out", (D, S), F32)
    emit = _declare_block(ctx, bi, inputs)
    ctx.begin_block_scope()
    s_x = emit_load_xbf(ctx, g, x_in)
    done = emit(ctx, g, ep, dict(x_in=x_in, x_out=x_out, xbf_ready=s_x.now,
                                 prev_done=[s_x.now]))
    nc = ctx.finalize(final_waits=done)
    return nc, ctx.host


def _build_fused_program(inputs):
    ctx = Ctx()
    g = Globals(ctx)
    ep = LNEpilogue(ctx, g)
    x_in = ctx.din("x_in", np.zeros((D, S), np.float32), F32)
    x_out = ctx.dout("x_out", (D, S), F32)
    xm = [ctx.dscratch(f"xmid{i}", (D, S), F32) for i in range(7)]
    chain = [x_in] + xm + [x_out]
    emits = [_declare_block(ctx, bi, inputs) for bi in range(8)]
    s_x = emit_load_xbf(ctx, g, x_in)
    done = [s_x.now]
    xbf_ready = s_x.now
    for bi in range(8):
        ctx.begin_block_scope()
        done = emits[bi](ctx, g, ep, dict(
            x_in=chain[bi], x_out=chain[bi + 1], xbf_ready=xbf_ready,
            prev_done=done))
        xbf_ready = done[1]
    nc = ctx.finalize(final_waits=done)
    return nc, ctx.host


def kernel(**inputs):
    inputs = {k: np.asarray(v) for k, v in inputs.items()}
    xs = [np.ascontiguousarray(inputs["L0_attn_x"][b].T) for b in range(B)]
    if FUSED:
        nc, host = _build_fused_program(inputs)
        base = dict(host)
        in_maps = [dict(base, x_in=xs[c]) for c in range(N_CORES)]
        res = run_bass_kernel_spmd(nc, in_maps, core_ids=list(range(N_CORES)))
        outs = [np.asarray(res.results[c]["x_out"]) for c in range(N_CORES)]
    else:
        cur = xs
        for bi in range(8):
            nc, host = _build_block_program(bi, inputs)
            base = dict(host)
            in_maps = [dict(base, x_in=np.ascontiguousarray(cur[c]))
                       for c in range(N_CORES)]
            res = run_bass_kernel_spmd(nc, in_maps,
                                       core_ids=list(range(N_CORES)))
            cur = [np.asarray(res.results[c]["x_out"])
                   for c in range(N_CORES)]
        outs = cur
    return np.ascontiguousarray(
        np.stack([o.T for o in outs], axis=0)).astype(np.float32)
